# version 06
# baseline (speedup 1.0000x reference)
import numpy as np
import concourse.bass as bass
import concourse.mybir as mybir
from concourse.bass_utils import run_bass_kernel_spmd

F32 = mybir.dt.float32
BF16 = mybir.dt.bfloat16
AF = mybir.ActivationFunctionType
AL = mybir.AluOpType

B, L, D = 4, 4096, 1024
NT = 2048
HALO = 8
NTH = NT + HALO
NCH = 8
H = 4
DK = 256
MAGIC = 12582912.0
LN_EPS = 1e-5
L2_EPS = 1e-6
RMS_EPS = 1e-5
CHUNK = 128
NCK = L // CHUNK
PIECES = [(0, HALO)] + [(HALO + 512 * i, HALO + 512 * (i + 1)) for i in range(4)]
RPIECES = PIECES[1:]
ROWW = 2176
ENGINES = ("sync", "tensor", "vector", "scalar", "gpsimd")

_STAGE = 99
_DCK = 99999
_DBG = False


class Sched:
    def __init__(self, nc):
        self.nc = nc
        self.ops = {e: [] for e in ENGINES}
        self._sems = {}
        self.cnt = {}
        self.ph = self.sem("ph")
        self.nbar = 0
        self.dsem = {e: self.sem("dma_" + e) for e in ENGINES}
        self.dcnt = {e: 0 for e in ENGINES}
        self.dwait = {e: 0 for e in ENGINES}

    def canon(self, name):
        if len(name) > 2 and name[-2] == "_" and name[-1] in "vst":
            return "S_" + name[-1]
        return name

    def sem(self, name):
        name = self.canon(name)
        if name not in self._sems:
            self._sems[name] = self.nc.semaphore(name).__enter__()
            self.cnt[name] = 0
        return self._sems[name]

    def on(self, eng, fn, kind="data"):
        self.ops[eng].append([fn, kind, False])

    def raw(self, eng, method, *a, **kw):
        self.on(eng, lambda p: getattr(p, method)(*a, **kw))

    def carrier(self, eng, p):
        ca = self.carr
        if eng == "vector":
            return p.memset(ca[0:1, 0:1], 0.0)
        if eng == "scalar":
            return p.memzero(ca[0:1, 1:2])
        if eng == "gpsimd":
            return p.memset(ca[0:1, 2:3], 0.0)
        if eng == "tensor":
            return p.ldweights(self.eyeb_carr)
        return p.nop()

    def notify(self, eng, name, by=1):
        sm = self.sem(name)
        name = self.canon(name)
        self.cnt[name] += by
        ent = self.ops[eng][-1] if self.ops[eng] else None
        if ent is not None and ent[1] == "data" and not ent[2]:
            prev = ent[0]
            def fn(p, prev=prev, sm=sm, by=by):
                r = prev(p)
                (r if r is not None else self.carrier(eng, p)).then_inc(sm, by)
            ent[0], ent[2] = fn, True
        else:
            def fn(p, sm=sm, by=by):
                self.carrier(eng, p).then_inc(sm, by)
            self.on(eng, fn, kind="seq")
        return self.cnt[name]

    def sep(self, eng):
        self.on(eng, lambda p: p.drain(), kind="seq")

    def wait(self, eng, name, val=None):
        sm = self.sem(name)
        v = self.cnt[self.canon(name)] if val is None else val
        self.on(eng, lambda p: p.wait_ge(sm, v), kind="seq")
        return v

    def dma(self, eng, out, in_):
        def fn(p, _o=out, _i=in_):
            o = _o() if callable(_o) else _o
            i = _i() if callable(_i) else _i
            return p.dma_start(out=o, in_=i).then_inc(self.dsem[eng], 16)
        self.on(eng, fn, kind="seq")
        self.dcnt[eng] += 16
        return self.dcnt[eng]

    def dma_wait(self, eng, upto=None, src=None):
        q = src or eng
        val = self.dcnt[q] if upto is None else upto
        if q == eng:
            if val > self.dwait[q]:
                self.dwait[q] = val
                self.on(eng, lambda p, v=val: p.wait_ge(self.dsem[q], v),
                        kind="seq")
        else:
            self.on(eng, lambda p, v=val: p.wait_ge(self.dsem[q], v),
                    kind="seq")
        return val

    def barrier(self):
        self.nbar += 1
        tgt = 5 * self.nbar
        for e in ENGINES:
            self.dma_wait(e)
            self.notify(e, "ph")
            self.wait(e, "ph", tgt)
        assert self.cnt["ph"] == tgt

    def emit(self):
        with self.nc.Block() as block:
            def mk(eng_name):
                def body(proxy):
                    for ent in self.ops[eng_name]:
                        ent[0](proxy)
                body.__name__ = "stream_" + eng_name
                return body
            block.sync(mk("sync"))
            block.tensor(mk("tensor"))
            block.vector(mk("vector"))
            block.scalar(mk("scalar"))
            block.gpsimd(mk("gpsimd"))


class MK:
    def __init__(self, stage=99, dbg=False, sim=False):
        self.sim_mode = sim
        self.nc = bass.Bass("TRN2", target_bir_lowering=False)
        self.snop = ((lambda p: p.drain()) if sim else
                     (lambda p: p.nop(cycle_cnt=128, nofuse=True)))
        self.s = Sched(self.nc)
        self.stage = stage
        self.dbg_on = dbg
        self.di, self.do, self.dint, self.sb, self.ps = {}, {}, {}, {}, {}
        self.sb_off = {}
        self.cursor = {}
        self.dbg_list = []
        self.dt_done = {}

    def inp(self, name, shape, dtype=F32):
        self.di[name] = self.nc.dram_tensor(name, list(shape), dtype,
                                            kind="ExternalInput")
        return self.di[name]

    def outp(self, name, shape, dtype=F32):
        self.do[name] = self.nc.dram_tensor(name, list(shape), dtype,
                                            kind="ExternalOutput")
        return self.do[name]

    def dram(self, name, shape, dtype=F32):
        self.dint[name] = self.nc.dram_tensor(name, list(shape), dtype)
        return self.dint[name]

    def at(self, name, shape, dtype, off):
        nbytes = int(np.prod(shape[1:])) * mybir.dt.size(dtype)
        assert off % 32 == 0, (name, off)
        assert off >= 24576 and off + nbytes <= 224 * 1024, (name, off, nbytes)
        self.sb[name] = self.nc.alloc_sbuf_tensor_at(name, list(shape), dtype,
                                                     offset=off)
        self.sb_off[name] = (off, off + nbytes)
        return self.sb[name]

    def fix(self, name, shape, dtype):
        off = self.cursor.setdefault("fix", 24576)
        nbytes = int(np.prod(shape[1:])) * mybir.dt.size(dtype)
        nbytes = (nbytes + 31) // 32 * 32
        self.cursor["fix"] = off + nbytes
        return self.at(name, shape, dtype, off)

    def psum(self, name, shape, dtype=F32):
        self.ps[name] = self.nc.alloc_psum_tensor(name, list(shape), dtype)
        return self.ps[name]

    def dbg(self, name, ap, shape, dtype=F32):
        if not self.dbg_on:
            return
        t = self.outp("dbg_" + name, shape, dtype)
        self.s.barrier()
        self.s.dma("gpsimd", t[:, :] if len(shape) == 2 else t[:], ap)
        self.s.barrier()

    ERA = None

    def build(self):
        s = self.s
        self.declare_common()
        self.load_constants()
        self.stage_ln1()
        if self.stage >= 2:
            self.stage_proj_qkv()
        if self.stage >= 3:
            self.stage_delta_prep()
        if self.stage >= 4:
            self.stage_delta()
        if self.stage >= 4:
            self.stage_delta_finish()
        if self.stage >= 5:
            self.stage_fir()
        if self.stage >= 8:
            self.ffn_ternarize()
        if self.stage >= 5:
            self.stage_fir_finish()
        if self.stage >= 6:
            self.stage_gates()
        if self.stage >= 7:
            self.stage_mix_out()
        if self.stage >= 8:
            self.stage_ffn1()
        if self.stage >= 9:
            self.stage_attn_b()
        if self.stage >= 10:
            self.stage_ffn2()
        if self.stage >= 11:
            self.stage_store()
        s.barrier()
        s.emit()
        return self

    def declare_common(self):
        self.inp("x0", [D, NTH])
        self.inp("eye128", [128, 128])
        self.inp("ejpat", [128, 256], BF16)
        self.inp("masks", [128, 512], BF16)
        self.inp("wq", [D, D])
        self.inp("wk", [D, D])
        self.inp("wv", [D, D])
        self.inp("wsmallb", [128, NCH, 16], BF16)
        self.inp("convw", [128, NCH, 16])
        self.inp("firt", [128, 2, 128, 256], BF16)
        self.inp("wo", [D, D])
        self.inp("ffnw", [D, D])
        self.inp("gamc", [128, NCH])
        self.inp("inwt", [D, 64])
        self.inp("bconvw", [128, NCH, 4])
        self.inp("dtwt", [D, 16])
        self.inp("dtb", [16, 1])
        self.inp("trwt", [16, 64])
        self.inp("outwt", [64, D])
        self.inp("outbc", [128, NCH])
        self.inp("inbc", [64, 1])
        self.inp("hmask", [1, 1])
        self.inp("betc", [128, NCH])
        self.inp("wg1", [1040, D])
        self.inp("wg2b", [128, NCH, 8], BF16)
        self.inp("wog1", [1032, 512])
        self.inp("wog2b", [128, 4, 8], BF16)
        self.inp("bg1c", [128, NCH])
        self.inp("bog1c", [128, 4])
        self.inp("normwc", [128, NCH])
        self.inp("vec16", [1, 16])
        self.inp("bg2row", [1, 64])
        self.outp("y", [D, NT])

    def load_constants(self):
        s = self.s
        s.carr = self.fix("CARR", [128, 8], F32)
        s.eyeb_carr = None
        self.fix("EYE", [128, 128], F32)
        self.fix("MASKS", [128, 512], BF16)
        self.fix("EJ", [128, 256], BF16)
        self.fix("ROWT", [16, ROWW], F32)
        self.fix("ROWP", [8, ROWW], F32)
        self.fix("RP", [128, 512], F32)
        self.fix("BCA", [128, NTH], F32)
        self.fix("BCB", [128, NTH], F32)
        self.fix("SC", [128, 1056], F32)
        self.fix("PRSV", [128, 32], F32)
        self.ERA = (self.cursor["fix"] + 127) // 128 * 128
        self.psum("PSW", [128, 2048])
        self.psum("PSR", [128, 512])
        self.psum("PST", [128, 1024], BF16)
        self.psum("PSS", [128, 1024])
        s.dma("sync", self.sb["EYE"][:, :], self.di["eye128"][:, :])
        s.dma("sync", self.sb["MASKS"][:, :], self.di["masks"][:, :])
        s.dma("sync", self.sb["EJ"][:, :], self.di["ejpat"][:, :])
        def v_init0(p):
            p.memset(self.sb["ROWT"][0:16, 0:ROWW], 0.0)
            return p.memset(self.sb["RP"][:, :], 0.0)
        s.on("vector", v_init0)
        s.eyeb_carr = self.sb["MASKS"][:, 256:384]
        self.dyx = {}
        def g_base(p):
            cid = p.compute_val(self.nc.partition_id())
            base = p.snap((cid % 2) * 512)
            for i in range(2):
                for kc in range(2):
                    self.dyx[(i, kc)] = base + (256 * i + 128 * kc)
            self.dyx["tok"] = base * 4
            self.dyx["rank"] = p.snap(cid % 8)
            self.dyx["oth"] = p.snap((cid + 1) % 2)
            self.dyx["o512"] = p.snap(((cid + 1) % 2) * 512)
            self.dyx["otok"] = p.snap(((cid + 1) % 2) * 2048)
            return None
        s.on("gpsimd", g_base, kind="seq")
        self.dyv = {}
        def v_base(p):
            cid = p.compute_val(self.nc.partition_id())
            self.dyv["rank"] = p.snap(cid % 8)
            return None
        s.on("scalar", v_base, kind="seq")
        self.dys = {}
        def s_base(p):
            cid = p.compute_val(self.nc.partition_id())
            base = p.snap((cid % 2) * 512)
            self.dys[(0, 0)] = base
            self.dys["tok"] = base * 4
            self.dys["oth"] = p.snap((cid + 1) % 2)
            self.dys["o512"] = p.snap(((cid + 1) % 2) * 512)
            self.dys["otok"] = p.snap(((cid + 1) % 2) * 2048)
            return None
        s.on("sync", s_base, kind="seq")
        s.barrier()

    def row_reduce(self, jobs, prep=None):
        s = self.s
        PSR = self.ps["PSR"]
        n = 1 + max(j[0] for j in jobs)
        assert n <= 16
        last_t = None
        for (t0, t1) in PIECES:
            w = t1 - t0
            if prep is not None:
                pres = prep(t0, w)
                vals = {}
                for eng, fn in pres:
                    if last_t is not None:
                        s.wait(eng, "rr_t", last_t)
                    s.on(eng, fn)
                    vals[eng] = s.notify(eng, "rr_" + eng[0])
                for eng, v in vals.items():
                    s.wait("tensor", "rr_" + eng[0], v)
            def t_fn(p, t0=t0, w=w):
                pairs = []
                for (pr, mf) in jobs:
                    for c in range(NCH):
                        m = mf(c, t0, w)
                        if m is not None:
                            pairs.append((pr, m))
                r = None
                for i, (pr, m) in enumerate(pairs):
                    r = p.matmul(PSR[0:n, 0:w],
                                 self.sb["EJ"][:, 16 * pr:16 * pr + n],
                                 m, start=(i == 0), stop=(i == len(pairs) - 1))
                return r
            s.on("tensor", t_fn)
            tv = s.notify("tensor", "rr_t")
            last_t = tv
            s.wait("vector", "rr_t", tv)
            def v_fn(p, t0=t0, w=w, n=n):
                return p.tensor_copy(self.sb["ROWT"][0:n, t0:t0 + w],
                                     PSR[0:n, 0:w])
            s.on("vector", v_fn)
            vt = s.notify("vector", "rr_v")
            s.wait("tensor", "rr_v", vt)
        def vz(p, n=n):
            p.memset(self.sb["ROWT"][0:n, NTH:ROWW], 0.0)
        s.on("vector", vz)

    def rows_to_rp(self, rows, rp_slot0=0):
        s = self.s
        for i, r in enumerate(rows):
            sl = rp_slot0 + i
            s.dma("gpsimd", self.sb["RP"][:, 17 * sl:17 * sl + 17],
                  self.sb["ROWT"][r:r + 1, 0:ROWW])

    def rp_to_rows(self, slots, rows):
        s = self.s
        for sl, r in zip(slots, rows):
            s.dma("gpsimd", self.sb["ROWT"][r:r + 1, 0:ROWW],
                  self.sb["RP"][:, 17 * sl:17 * sl + 17])

    def bcast_row(self, row, dst, width=NTH, tile=None):
        s = self.s
        tile = self.sb["ROWT"] if tile is None else tile
        rb = self.dint.get("rowbounce")
        if rb is None:
            rb = self.dram("rowbounce", [16, ROWW], F32)
        s.dma("gpsimd", rb[row:row + 1, 0:width],
              tile[row:row + 1, 0:width])
        s.dma_wait("gpsimd")
        def dst_ap():
            d = dst() if callable(dst) else dst
            return d[:, 0:width]
        s.dma("gpsimd", dst_ap,
              lambda: rb[row:row + 1, 0:width].partition_broadcast(128))

    def stage_ln1(self):
        s = self.s
        X = self.at("X", [128, NCH, NTH], F32, self.ERA)
        XN = self.at("XN", [128, NCH, NTH], BF16, self.ERA + 65792)
        self.at("SCB1", [128, NCH, 512], BF16, self.ERA + 98688)
        self.at("SCB2", [128, NCH, 512], BF16, self.ERA + 106880)
        s.dma("sync", X[:, :, :],
              self.di["x0"].ap().rearrange("(c p) t -> p c t", p=128))
        s.barrier()
        SCB1, SCB2 = self.sb["SCB1"], self.sb["SCB2"]
        def prep(t0, w):
            def v_fn(p, t0=t0, w=w):
                return p.tensor_copy(SCB1[:, :, 0:w], X[:, :, t0:t0 + w])
            def s_fn(p, t0=t0, w=w):
                return p.activation(SCB2[:, :, 0:w], X[:, :, t0:t0 + w],
                                    AF.Square)
            return [("vector", v_fn), ("scalar", s_fn)]
        self.row_reduce([(0, lambda c, t0, w: SCB1[:, c, 0:w]),
                         (1, lambda c, t0, w: SCB2[:, c, 0:w])], prep=prep)
        s.barrier()
        self.rows_to_rp([0, 1])
        s.barrier()
        RP = self.sb["RP"]
        mu, msq, var = RP[:, 0:17], RP[:, 17:34], RP[:, 34:51]
        t2 = RP[:, 51:68]
        def v_math(p):
            p.tensor_scalar_mul(mu, mu, 1.0 / D)
            p.tensor_scalar_mul(msq, msq, 1.0 / D)
            p.drain()
            p.tensor_tensor(t2, mu, mu, AL.mult)
            p.drain()
            p.tensor_tensor(var, msq, t2, AL.subtract)
            p.drain()
            p.tensor_scalar_add(var, var, LN_EPS)
            p.drain()
            return p.reciprocal(var, var)
        s.on("vector", v_math)
        sv = s.notify("vector", "ln_v")
        s.wait("scalar", "ln_v", sv)
        def sc_math(p):
            return p.activation(var, var, AF.Sqrt)
        s.on("scalar", sc_math)
        s.notify("scalar", "ln_s")
        s.barrier()
        self.rp_to_rows([0, 2], [0, 1])
        s.barrier()
        self.bcast_row(0, self.sb["BCA"])
        self.bcast_row(1, self.sb["BCB"])
        s.barrier()
        SC = self.sb["SC"]
        def v_apply(p):
            r = None
            for c in range(NCH):
                for (t0, t1) in ((0, 1024), (1024, NTH)):
                    w = t1 - t0
                    p.tensor_tensor(SC[:, 0:w], X[:, c, t0:t1],
                                    self.sb["BCA"][:, t0:t1], AL.subtract)
                    p.drain()
                    r = p.tensor_tensor(XN[:, c, t0:t1], SC[:, 0:w],
                                        self.sb["BCB"][:, t0:t1], AL.mult)
                    p.drain()
            return r
        s.on("vector", v_apply)
        s.barrier()
        self.dbg("xn", XN[:, :, :], [128, NCH, NTH], BF16)

    def proj_big(self, w_dram, out_fn, oc_range=range(NCH), halo=True,
                 extra=None, xn=None):
        s = self.s
        WB, WF = self.sb["WB"], self.sb["WF32"]
        XN = self.sb["XN"] if xn is None else xn
        PSW, PSR = self.ps["PSW"], self.ps["PSR"]
        for kc in range(NCH):
            s.dma("sync", WF[:, 0:1024],
                  w_dram[128 * kc:128 * (kc + 1), :])
            s.dma_wait("vector", src="sync")
            def vc(p, kc=kc):
                return p.tensor_copy(WB[:, kc, :], WF[:, 0:1024])
            s.on("vector", vc)
            vv = s.notify("vector", "pw_v")
            if kc < NCH - 1:
                s.wait("sync", "pw_v", vv)
        s.wait("tensor", "pw_v")
        pieces = PIECES if halo else RPIECES
        for oc in oc_range:
            def t_fn(p, oc=oc):
                r = None
                for kc in range(NCH):
                    lhs = WB[:, kc, 128 * oc:128 * (oc + 1)]
                    for pi, (t0, t1) in enumerate(pieces):
                        w = t1 - t0
                        if w <= 16:
                            dst = PSR[0:128, 384:384 + w]
                        else:
                            dst = PSW[:, (t0 - HALO):(t0 - HALO) + w]
                        r = p.matmul(dst, lhs, XN[:, kc, t0:t1],
                                     start=(kc == 0), stop=(kc == NCH - 1))
                return r
            s.on("tensor", t_fn)
            tv = s.notify("tensor", "pw_t")
            out_fn(oc, tv)

    def conv4_silu(self, taps_base, src, dst, sign_extra=None):
        s = self.s
        PSW = self.ps["PSW"]
        EYEB = self.sb["EYEB"]
        CW = self.sb["CONVW"]
        DIAG = self.sb["DIAG"]
        for c in range(NCH):
            def v_diag(p, c=c):
                r = None
                for j in range(4):
                    r = p.tensor_scalar(DIAG[:, j, :], EYEB[:, :],
                                        CW[:, c, taps_base + j:taps_base + j + 1],
                                        None, AL.mult)
                return r
            s.on("vector", v_diag)
            dv = s.notify("vector", "cv_v")
            s.wait("tensor", "cv_v", dv)
            def t_fn(p, c=c):
                r = None
                for pi, (t0, t1) in enumerate(RPIECES):
                    w = t1 - t0
                    for j in range(4):
                        sh = t0 - 3 + j
                        r = p.matmul(PSW[:, (t0 - HALO):(t0 - HALO) + w],
                                     DIAG[:, j, :], src[:, c, sh:sh + w],
                                     start=(j == 0), stop=(j == 3))
                return r
            s.on("tensor", t_fn)
            tv = s.notify("tensor", "cv_t")
            s.wait("scalar", "cv_t", tv)
            def sc_fn(p, c=c):
                r = None
                for (t0, t1) in RPIECES:
                    w = t1 - t0
                    r = p.activation(dst[:, c, t0:t1],
                                     PSW[:, (t0 - HALO):(t0 - HALO) + w],
                                     AF.Silu)
                return r
            s.on("scalar", sc_fn)
            sv = s.notify("scalar", "cv_s")
            s.wait("tensor", "cv_s", sv)
            s.wait("vector", "cv_s", sv)

    def stage_proj_qkv(self):
        s = self.s
        E = self.ERA
        QL = self.at("QL", [128, NCH, NTH], BF16, E)
        TIN = self.at("TIN", [128, NCH, NTH], BF16, E + 32896)
        WB = self.at("WB", [128, NCH, 1024], BF16, E + 98688)
        WF32 = self.at("WF32", [128, 1024], F32, E + 115072)
        BB4 = self.at("BB4", [128, 4, NTH], BF16, E + 119168)
        SCL = self.at("SCL", [128, 4, NTH], BF16, E + 135616)
        EYEB = self.at("EYEB", [128, 128], BF16, E + 152064)
        CONVW = self.at("CONVW", [128, NCH, 16], F32, E + 152320)
        WSM = self.at("WSM", [128, NCH, 16], BF16, E + 152832)
        DIAG = self.at("DIAG", [128, 4, 128], BF16, E + 153088)
        self.at("L2SC", [128, NCH, 512], BF16, E + 154112)
        self.dram("xin", [4, D, NT], BF16)
        self.dram("xout", [2, 4, D, NT], BF16)
        self.dram("fir_in", [NCH, 128, NT], BF16)
        self.dram("vdir", [NCH, 128, NT], BF16)
        s.dma("sync", EYEB[:, :], self.di["masks"][:, 256:384])
        s.dma("sync", CONVW[:, :, :], self.di["convw"].ap())
        s.dma("sync", WSM[:, :, :], self.di["wsmallb"].ap())
        s.barrier()
        def v_init(p):
            p.memset(QL[:, :, 0:HALO], 0.0)
            p.memset(self.sb["ROWP"][0:8, 0:ROWW], 0.0)
            return p.memset(TIN[:, :, 0:HALO], 0.0)
        s.on("vector", v_init)
        s.barrier()

        def evac_plain(oc, tv):
            s.wait("scalar", "pw_t", tv)
            def sc(p, oc=oc):
                r = None
                for (t0, t1) in PIECES:
                    w = t1 - t0
                    srcp = (self.ps["PSR"][0:128, 384:384 + w] if w <= 16 else
                            self.ps["PSW"][:, (t0 - HALO):(t0 - HALO) + w])
                    r = p.copy(QL[:, oc, t0:t1], srcp)
                return r
            s.on("scalar", sc)
            sv = s.notify("scalar", "pw_s")
            s.wait("tensor", "pw_s", sv)

        self.proj_big(self.di["wv"].ap(), evac_plain)
        PSR = self.ps["PSR"]
        for (t0, t1) in RPIECES:
            w = t1 - t0
            def t_fn(p, t0=t0, w=w):
                r = None
                for kc in range(NCH):
                    r = p.matmul(PSR[0:16, 0:w], WSM[:, kc, :],
                                 self.sb["XN"][:, kc, t0:t0 + w],
                                 start=(kc == 0), stop=(kc == NCH - 1))
                return r
            s.on("tensor", t_fn)
            tv = s.notify("tensor", "sm_t")
            s.wait("vector", "sm_t", tv)
            def v_fn(p, t0=t0, w=w):
                return p.tensor_copy(self.sb["ROWP"][0:8, t0:t0 + w],
                                     PSR[0:8, 0:w])
            s.on("vector", v_fn)
            vv = s.notify("vector", "sm_v")
            s.wait("tensor", "sm_v", vv)
        s.barrier()
        def sc_beta(p):
            return p.activation(self.sb["ROWP"][0:4, 0:NTH], self.sb["ROWP"][0:4, 0:NTH],
                                AF.Sigmoid)
        s.on("scalar", sc_beta)
        s.notify("scalar", "sm_s")
        s.barrier()
        for h4 in range(4):
            self.bcast_row(h4, self.sb["BCA"], tile=self.sb["ROWP"])
            s.barrier()
            def vc(p, h4=h4):
                return p.tensor_copy(BB4[:, h4, :], self.sb["BCA"][:, :])
            s.on("vector", vc)
            s.barrier()
        self.conv4_silu(8, QL, TIN)
        s.barrier()
        for c in range(NCH):
            s.dma("gpsimd", self.dint["fir_in"][c, :, :], TIN[:, c, HALO:NTH])
            s.dma("gpsimd", self.dint["vdir"][c, :, :], TIN[:, c, HALO:NTH])
        def v_vb(p):
            r = None
            for c in range(NCH):
                r = p.tensor_tensor(QL[:, c, :], TIN[:, c, :],
                                    BB4[:, c // 2, :], AL.mult)
            return r
        s.on("vector", v_vb)
        s.barrier()
        for c in range(NCH):
            s.dma("gpsimd", self.dint["xin"][3, 128 * c:128 * (c + 1), :],
                  QL[:, c, HALO:NTH])
        s.barrier()

        self.proj_big(self.di["wq"].ap(), evac_plain)
        s.barrier()
        self.conv4_silu(0, QL, TIN)
        s.barrier()
        self.l2rows(TIN)
        s.barrier()
        for h4 in range(4):
            self.bcast_row(h4, self.sb["BCA"])
            s.barrier()
            def vc(p, h4=h4):
                return p.tensor_copy(SCL[:, h4, :], self.sb["BCA"][:, :])
            s.on("vector", vc)
            s.barrier()
        def v_qn(p):
            r = None
            for c in range(NCH):
                r = p.tensor_tensor(QL[:, c, :], TIN[:, c, :],
                                    SCL[:, c // 2, :], AL.mult)
            return r
        s.on("vector", v_qn)
        s.barrier()
        for c in range(NCH):
            s.dma("gpsimd", self.dint["xin"][0, 128 * c:128 * (c + 1), :],
                  QL[:, c, HALO:NTH])
        s.barrier()

        self.proj_big(self.di["wk"].ap(), evac_plain)
        s.barrier()
        self.conv4_silu(4, QL, TIN)
        s.barrier()
        self.l2rows(TIN)
        s.barrier()
        for h4 in range(4):
            self.bcast_row(h4, self.sb["BCA"])
            s.barrier()
            def vc(p, h4=h4):
                return p.tensor_copy(SCL[:, h4, :], self.sb["BCA"][:, :])
            s.on("vector", vc)
            s.barrier()
        def v_kn(p):
            r = None
            for c in range(NCH):
                r = p.tensor_tensor(TIN[:, c, :], TIN[:, c, :],
                                    SCL[:, c // 2, :], AL.mult)
            return r
        s.on("vector", v_kn)
        s.barrier()
        for c in range(NCH):
            s.dma("gpsimd", self.dint["xin"][1, 128 * c:128 * (c + 1), :],
                  TIN[:, c, HALO:NTH])
        def v_kb(p):
            r = None
            for c in range(NCH):
                r = p.tensor_tensor(QL[:, c, :], TIN[:, c, :],
                                    BB4[:, c // 2, :], AL.mult)
            return r
        s.on("vector", v_kb)
        s.barrier()
        for c in range(NCH):
            s.dma("gpsimd", self.dint["xin"][2, 128 * c:128 * (c + 1), :],
                  QL[:, c, HALO:NTH])
        s.barrier()
        if self.dbg_on:
            self.dbg("qn", self.dint["xin"][0, :, :], [D, NT], BF16)
            self.dbg("kn", self.dint["xin"][1, :, :], [D, NT], BF16)
            self.dbg("kb", self.dint["xin"][2, :, :], [D, NT], BF16)
            self.dbg("vb", self.dint["xin"][3, :, :], [D, NT], BF16)
            self.dbg("vin", self.dint["fir_in"].ap(), [NCH, 128, NT], BF16)
            self.dbg("beta", self.sb["ROWP"][0:4, 0:NTH], [4, NTH], F32)

    def l2rows(self, TIN):
        s = self.s
        PSR = self.ps["PSR"]
        SCB2 = self.sb["L2SC"]
        last_t = None
        for (t0, t1) in RPIECES:
            w = t1 - t0
            if last_t is not None:
                s.wait("vector", "l2_t", last_t)
            def v_sq(p, t0=t0, w=w):
                return p.tensor_tensor(SCB2[:, :, 0:w], TIN[:, :, t0:t0 + w],
                                       TIN[:, :, t0:t0 + w], AL.mult)
            s.on("vector", v_sq)
            vv = s.notify("vector", "l2_v")
            s.wait("tensor", "l2_v", vv)
            def t_fn(p, t0=t0, w=w):
                r = None
                for c in range(NCH):
                    r = p.matmul(PSR[0:4, 0:w],
                                 self.sb["EJ"][:, 16 * (c // 2):16 * (c // 2) + 4],
                                 SCB2[:, c, 0:w],
                                 start=(c == 0), stop=(c == NCH - 1))
                return r
            s.on("tensor", t_fn)
            tv = s.notify("tensor", "l2_t")
            last_t = tv
            s.wait("vector", "l2_t", tv)
            def v_cp(p, t0=t0, w=w):
                return p.tensor_copy(self.sb["ROWT"][0:4, t0:t0 + w],
                                     PSR[0:4, 0:w])
            s.on("vector", v_cp)
            vv2 = s.notify("vector", "l2_v")
            s.wait("tensor", "l2_v", vv2)
        s.barrier()
        def vz2(p):
            return p.memset(self.sb["ROWT"][0:4, 0:HALO], 0.0)
        s.on("vector", vz2)
        s.barrier()
        self.rows_to_rp([0, 1, 2, 3])
        s.barrier()
        RP = self.sb["RP"]
        def v_m(p):
            p.tensor_scalar_add(RP[:, 0:68], RP[:, 0:68], L2_EPS)
            p.drain()
            return p.reciprocal(RP[:, 0:68], RP[:, 0:68])
        s.on("vector", v_m)
        vv = s.notify("vector", "l2m_v")
        s.wait("scalar", "l2m_v", vv)
        def s_m(p):
            return p.activation(RP[:, 0:68], RP[:, 0:68], AF.Sqrt)
        s.on("scalar", s_m)
        s.notify("scalar", "l2m_s")
        s.barrier()
        self.rp_to_rows([0, 1, 2, 3], [0, 1, 2, 3])
        s.barrier()

    def stage_delta_prep(self):
        s = self.s
        self.dram("xsend", [4, 512, NT], BF16)
        self.dram("xrecv", [2, 4, 512, NT], BF16)
        def g_xs(p):
            src = self.dint["xin"].ap()[:, bass.ds(self.dys["o512"], 512), :]
            return p.dma_start(out=self.dint["xsend"].ap(),
                               in_=src).then_inc(s.dsem["sync"], 16)
        s.on("sync", g_xs, kind="seq")
        s.dcnt["sync"] += 16
        s.dma_wait("gpsimd", src="sync")
        def cc(p):
            return p.collective_compute(
                "AllGather", AL.bypass,
                replica_groups=[[0, 1], [2, 3], [4, 5], [6, 7]],
                ins=[self.dint["xsend"].ap().opt()],
                outs=[self.dint["xrecv"].ap().opt()])
        s.on("gpsimd", cc, kind="data")
        self.ccv_xin = s.notify("gpsimd", "ccd")
        self.dram("fir_x8", [8, NCH, 128, NT], BF16)
        def cc2(p):
            return p.collective_compute(
                "AllGather", AL.bypass,
                replica_groups=[list(range(8))],
                ins=[self.dint["fir_in"].ap().opt()],
                outs=[self.dint["fir_x8"].ap().opt()])
        s.on("gpsimd", cc2, kind="data")
        self.ccv_firx8 = s.notify("gpsimd", "ccd")
        s.wait("sync", "ccd", self.ccv_xin)

    def stage_delta(self):
        s, nc = self.s, self.nc
        E = self.ERA
        MB = self.sb["MASKS"]
        M_SU, M_UI, M_EYE, M_SL = (MB[:, 0:128], MB[:, 128:256],
                                   MB[:, 256:384], MB[:, 384:512])
        DOUT = self.at("DOUT", [128, 4, 4096], BF16, E)
        RING = self.at("RING", [128, 4, 4, 2, 128], BF16, E + 98688)
        W16 = self.at("W16", [128, 16, 128], BF16, E + 106880)
        St = self.at("St", [128, 2, 512], BF16, E + 110976)
        SF = self.at("SF", [128, 2, 512], F32, E + 119232)
        U2T = self.at("U2T", [128, 2, 256], BF16, E + 113024)
        KTM = self.at("KTM", [128, 2, 256], BF16, E + 114048)
        YPP = self.at("YPP", [128, 2, 2, 256], BF16, E + 115072)
        OTM = self.at("OTM", [128, 2, 256], BF16, E + 117120)
        KBSS = self.at("KBSS", [128, 256], F32, E + 118144)
        self.dram("dret_in", [512, 4096], BF16)
        self.dram("dret_out", [2, 512, 4096], BF16)
        self.dram("xmy", [4, 512, 4096], BF16)
        PSW, PSR, PSS = self.ps["PSW"], self.ps["PSR"], self.ps["PSS"]
        psSQ = PSW[:, 0:128]
        psAP = PSW[:, 512:768]
        psKBS = PSW[:, 1024:1280]
        psT = self.ps["PST"][:, 0:256]
        psKT = self.ps["PST"][:, 512:768]
        psGA = PSR[:, 0:384]

        dyx = self.dyx

        xmy = self.dint["xmy"]
        def g_st1(p):
            dd = self.dys
            src = self.dint["xin"].ap()[:, bass.ds(dd[(0, 0)], 512), :]
            return p.dma_start(out=xmy.ap()[:, :, bass.ds(dd["tok"], 2048)],
                               in_=src).then_inc(s.dsem["sync"], 16)
        s.on("sync", g_st1, kind="seq")
        s.dcnt["sync"] += 16
        def g_st2(p):
            dd = self.dys
            src = self.dint["xrecv"].ap()[bass.ds(dd["oth"], 1), :, :, :]
            src = src.rearrange("s a c t -> (s a) c t")
            return p.dma_start(out=xmy.ap()[:, :, bass.ds(dd["otok"], 2048)],
                               in_=src).then_inc(s.dsem["sync"], 16)
        s.on("sync", g_st2, kind="seq")
        s.dcnt["sync"] += 16
        s.barrier()

        def load_chunk2(i, ck, slot):
            sm = s.sem("rs%d" % slot)
            for kc in range(2):
                def fn(p, i=i, ck=ck, kc=kc, slot=slot, sm=sm):
                    src = xmy.ap()[:, 256 * i + 128 * kc:256 * i + 128 * kc + 128,
                                   128 * ck:128 * ck + 128]
                    src = src.transpose([1, 0, 2])
                    return p.dma_start(out=RING[:, slot, :, kc, :],
                                       in_=src).then_inc(sm, 16)
                s.on("sync", fn, kind="seq")
                s.cnt["rs%d" % slot] += 16
            return s.cnt["rs%d" % slot]

        NH = 2
        loaded = {}
        ring_of = {}
        nslot = [0]
        def ensure_loaded(i, ck):
            key = (i, ck)
            if key not in loaded:
                slot = nslot[0] % 4
                nslot[0] += 1
                ring_of[key] = slot
                loaded[key] = load_chunk2(i, ck, slot)
            return loaded[key], ring_of[key]

        order = [(ck, i) for ck in range(NCK) for i in range(NH)]
        order = order[:_DCK]
        for (ck, i) in order[:3]:
            ensure_loaded(i, ck)

        for oi, (ck, i) in enumerate(order):
            dv, slot = ensure_loaded(i, ck)
            if oi + 3 < len(order):
                cka, ia = order[oi + 3]
                if oi >= 1:
                    s.wait("sync", "dt", self.dt_done.get(order[oi - 1], 0))
                ensure_loaded(ia, cka)
            Q = lambda kc, slot=slot: RING[:, slot, 0, kc, :]
            K = lambda kc, slot=slot: RING[:, slot, 1, kc, :]
            KB = lambda kc, slot=slot: RING[:, slot, 2, kc, :]
            VB = lambda kc, slot=slot: RING[:, slot, 3, kc, :]
            first = (ck == 0)

            s.wait("tensor", "rs%d" % slot, dv)
            def t1(p, i=i, ck=ck, slot=slot, first=first, Q=Q, K=K, KB=KB, VB=VB):
                r = None
                for kc in range(2):
                    p.matmul(psGA[0:128, 0:128], K(kc), Q(kc), start=(kc == 0),
                             stop=(kc == 1))
                for kc in range(2):
                    p.matmul(psGA[0:128, 128:256], K(kc), KB(kc),
                             start=(kc == 0), stop=(kc == 1))
                for kc in range(2):
                    p.matmul(psGA[0:128, 256:384], KB(kc), K(kc),
                             start=(kc == 0), stop=(kc == 1))
                for kc in range(2):
                    p.transpose(psT[:, 128 * kc:128 * kc + 128], VB(kc),
                                self.sb["EYEB"][:, :])
                for kc in range(2):
                    r = p.transpose(psKT[:, 128 * kc:128 * kc + 128],
                                    K(kc), self.sb["EYEB"][:, :])
                if not first:
                    for kc in range(2):
                        r = p.matmul(psKBS, KB(kc),
                                     St[:, i, :][:, 256 * kc:256 * kc + 256],
                                     start=(kc == 0), stop=(kc == 1))
                return r
            s.on("tensor", t1)
            tv = s.notify("tensor", "dt")
            s.wait("vector", "dt", tv)
            def v1(p, i=i, first=first):
                p.tensor_tensor(W16[:, 0, :], psGA[0:128, 0:128], M_UI, AL.mult)
                p.tensor_tensor(W16[:, 1, :], psGA[0:128, 128:256], M_SU, AL.mult)
                p.tensor_tensor(W16[:, 2, :], psGA[0:128, 256:384], M_SL, AL.mult)
                p.tensor_copy(KTM[:, i, :], psKT)
                if first:
                    return p.tensor_copy(YPP[:, i, 0, :], psT[0:128, 0:256])
                p.tensor_copy(KBSS[:, :], psKBS)
                p.drain()
                return p.tensor_tensor(YPP[:, i, 0, :], psT[0:128, 0:256],
                                       KBSS[:, :], AL.subtract)
            s.on("vector", v1)
            vv = s.notify("vector", "dv")
            s.wait("tensor", "dv", vv)
            xs = {1: 1}
            xts = {1: 2}
            lv = [2, 4, 8, 16, 32, 64]
            for li, lev in enumerate(lv):
                xs[lev] = 3 + li
                if lev != 64:
                    xts[lev] = 9 + li
            for li, lev in enumerate(lv):
                prev = lev // 2
                def t_sq(p, lev=lev, prev=prev):
                    r = p.matmul(psSQ, W16[:, xts[prev], :], W16[:, xs[prev], :],
                                 start=True, stop=True)
                    if lev != 64:
                        r = p.matmul(PSW[:, 128:256], W16[:, xs[prev], :],
                                     W16[:, xts[prev], :], start=True,
                                     stop=True)
                    return r
                s.on("tensor", t_sq)
                tv = s.notify("tensor", "dt")
                s.wait("vector", "dt", tv)
                def v_sq(p, lev=lev):
                    p.tensor_copy(W16[:, xs[lev], :], psSQ)
                    if lev != 64:
                        return p.tensor_copy(W16[:, xts[lev], :], PSW[:, 128:256])
                    return None
                s.on("vector", v_sq)
                vv = s.notify("vector", "dv")
                s.wait("tensor", "dv", vv)
            pp = 0
            for li, lev in enumerate([64, 32, 16, 8, 4, 2]):
                def t_ap(p, lev=lev, pp=pp, i=i):
                    return p.matmul(psAP, W16[:, xs[lev], :], YPP[:, i, pp, :],
                                    start=True, stop=True)
                s.on("tensor", t_ap)
                tv = s.notify("tensor", "dt")
                s.wait("vector", "dt", tv)
                def v_ap(p, pp=pp, i=i):
                    return p.tensor_tensor(YPP[:, i, 1 - pp, :], YPP[:, i, pp, :],
                                           psAP, AL.add)
                s.on("vector", v_ap)
                vv = s.notify("vector", "dv")
                s.wait("tensor", "dv", vv)
                pp = 1 - pp
            def t_lf(p, pp=pp, i=i):
                return p.matmul(psAP, W16[:, xs[1], :], YPP[:, i, pp, :],
                                start=True, stop=True)
            s.on("tensor", t_lf)
            tv = s.notify("tensor", "dt")
            s.wait("vector", "dt", tv)
            def v_lf(p, pp=pp, i=i):
                return p.tensor_tensor(U2T[:, i, :], YPP[:, i, pp, :], psAP,
                                       AL.subtract)
            s.on("vector", v_lf)
            vv = s.notify("vector", "dv")
            s.wait("tensor", "dv", vv)
            def t_o(p, i=i, ck=ck, slot=slot, first=first, Q=Q):
                st = (True, False) if not first else None
                if not first:
                    for kc in range(2):
                        p.matmul(psKBS, Q(kc), St[:, i, :][:, 256 * kc:256 * kc + 256],
                                 start=(kc == 0), stop=False)
                    p.matmul(psKBS, W16[:, 0, :], U2T[:, i, :], start=False,
                             stop=True)
                else:
                    p.matmul(psKBS, W16[:, 0, :], U2T[:, i, :], start=True,
                             stop=True)
                for kc in range(2):
                    r = p.matmul(PSS[:, 512 * i + 256 * kc:512 * i + 256 * kc + 256],
                                 KTM[:, i, :][:, 128 * kc:128 * kc + 128],
                                 U2T[:, i, :],
                                 start=(kc == 0), stop=(kc == 1))
                return r
            s.on("tensor", t_o)
            tv = s.notify("tensor", "dt")
            s.wait("vector", "dt", tv)
            def v_o(p, i=i, first=first):
                p.tensor_copy(OTM[:, i, :], psKBS)
                if first:
                    p.tensor_copy(SF[:, i, :], PSS[:, 512 * i:512 * i + 512])
                else:
                    p.tensor_tensor(SF[:, i, :], SF[:, i, :],
                                    PSS[:, 512 * i:512 * i + 512], AL.add)
                p.drain()
                return p.tensor_copy(St[:, i, :], SF[:, i, :])
            s.on("vector", v_o)
            vv = s.notify("vector", "dv")
            s.wait("tensor", "dv", vv)
            def t_ot(p, i=i):
                r = None
                for kc in range(2):
                    r = p.transpose(psT[:, 128 * kc:128 * kc + 128],
                                    OTM[:, i, :][:, 128 * kc:128 * kc + 128],
                                    self.sb["EYEB"][:, :])
                return r
            s.on("tensor", t_ot)
            tv = s.notify("tensor", "dt")
            s.wait("vector", "dt", tv)
            def v_ot(p, i=i, ck=ck):
                r = None
                for kc in range(2):
                    r = p.tensor_copy(DOUT[:, 2 * i + kc, 128 * ck:128 * ck + 128],
                                      psT[:, 128 * kc:128 * kc + 128])
                return r
            s.on("vector", v_ot)
            vv = s.notify("vector", "dv")
            s.wait("tensor", "dv", vv)
            self.dt_done[(ck, i)] = s.cnt["dt"]
        s.barrier()
        for c in range(4):
            s.dma("gpsimd", self.dint["dret_in"][128 * c:128 * (c + 1), :],
                  DOUT[:, c, :])
        s.dma_wait("gpsimd")
        def cc(p):
            return p.collective_compute(
                "AllGather", AL.bypass,
                replica_groups=[[0, 1], [2, 3], [4, 5], [6, 7]],
                ins=[self.dint["dret_in"].ap().opt()],
                outs=[self.dint["dret_out"].ap().opt()])
        s.on("gpsimd", cc, kind="data")
        self.ccv_dret = s.notify("gpsimd", "ccd")

    def stage_delta_finish(self):
        s = self.s
        E = self.ERA
        dyx = self.dyx
        s.wait("gpsimd", "ccd", self.ccv_dret)
        DOUTM = self.at("DOUTM", [128, NCH, NT], BF16, E + 32768)
        for sl in range(2):
            def g_read(p, sl=sl):
                src = self.dint["dret_out"].ap()[sl, :, bass.ds(dyx["tok"], NT)]
                src = src.rearrange("(c p) t -> p c t", p=128)
                return p.dma_start(out=DOUTM[:, 4 * sl:4 * sl + 4, :],
                                   in_=src).then_inc(s.dsem["gpsimd"], 16)
            s.on("gpsimd", g_read, kind="seq")
            s.dcnt["gpsimd"] += 16
        s.barrier()
        if self.dbg_on:
            self.dbg("dout", DOUTM[:, :, :], [128, NCH, NT], BF16)

    def stage_fir(self):
        s = self.s
        E = self.ERA
        self.dram("fir_rin", [NCH, 2, 128, NT], BF16)
        self.dram("fir_r8", [8, NCH, 2, 128, NT], BF16)
        s.wait("scalar", "ccd", self.ccv_firx8)
        VT = self.at("VT", [128, 128, 128], BF16, E)
        VCH = self.at("VCH", [128, NCH, NT], BF16, E + 98688)
        YT = self.at("YT", [128, 128, 128], BF16, E + 98688)
        FIRW = self.at("FIRW", [128, 2, 256], BF16, E + 131456)
        STG = self.at("STG", [128, 16, 128], BF16, E + 132480)
        PST, PSW = self.ps["PST"], self.ps["PSW"]
        def g_vch(p):
            src_ = self.dint["fir_x8"].ap()[:, bass.ds(self.dyv["rank"], 1), :, :]
            src_ = src_.rearrange("s o p t -> p (s o) t")
            return p.dma_start(out=VCH[:, :, :], in_=src_).then_inc(
                s.dsem["scalar"], 16)
        s.on("scalar", g_vch, kind="seq")
        s.dcnt["scalar"] += 16
        s.barrier()
        for grp in range(32):
            def t_tr(p, grp=grp):
                r = None
                for k4 in range(4):
                    gi = 4 * grp + k4
                    b4, m = gi // 32, gi % 32
                    src = VCH[:, 2 * b4 + m // 16,
                              128 * (m % 16):128 * (m % 16) + 128]
                    r = p.transpose(PST[:, 128 * k4:128 * k4 + 128], src,
                                    self.sb["EYEB"][:, :])
                return r
            s.on("tensor", t_tr)
            tv = s.notify("tensor", "fi_t")
            s.wait("vector", "fi_t", tv)
            def v_tr(p, grp=grp):
                r = None
                for k4 in range(4):
                    gi = 4 * grp + k4
                    r = p.tensor_copy(VT[:, :, gi], PST[:, 128 * k4:128 * k4 + 128])
                return r
            s.on("vector", v_tr)
            vv = s.notify("vector", "fi_v")
            s.wait("tensor", "fi_v", vv)
        s.barrier()
        lastv = {}
        for pss in range(2):
            for c in range(128):
                buf = c % 2
                if (pss * 128 + c) >= 1:
                    s.wait("sync", "fi_t", lastv[(pss * 128 + c) - 1])
                s.dma("sync", FIRW[:, buf, :],
                      self.di["firt"].ap()[c, pss, :, :])
                s.dma_wait("tensor", src="sync")
                def t_f(p, c=c, buf=buf):
                    p.matmul(PSW[:, 0:128], FIRW[:, buf, 0:128], VT[:, c, :],
                             start=True, stop=False)
                    r = None
                    for b4 in range(4):
                        r = p.matmul(PSW[:, 32 * b4 + 1:32 * b4 + 32],
                                     FIRW[:, buf, 128:256],
                                     VT[:, c, 32 * b4:32 * b4 + 31],
                                     start=False, stop=(b4 == 3))
                    return r
                s.on("tensor", t_f)
                tv = s.notify("tensor", "fi_t")
                lastv[pss * 128 + c] = tv
                s.wait("vector", "fi_t", tv)
                def v_f(p, c=c):
                    return p.tensor_copy(YT[:, c, :], PSW[:, 0:128])
                s.on("vector", v_f)
                vv = s.notify("vector", "fi_v")
                s.wait("tensor", "fi_v", vv)
            s.barrier()
            for sl in range(NCH):
                for grp in range(4):
                    def t_tb(p, sl=sl, grp=grp):
                        r = None
                        for k4 in range(4):
                            m16 = 4 * grp + k4
                            b4, m = sl // 2, 16 * (sl % 2) + m16
                            gi = 32 * b4 + m
                            r = p.transpose(PST[:, 128 * k4:128 * k4 + 128],
                                            YT[:, :, gi], self.sb["EYEB"][:, :])
                        return r
                    s.on("tensor", t_tb)
                    tv = s.notify("tensor", "fi_t")
                    s.wait("vector", "fi_t", tv)
                    def v_tb(p, grp=grp):
                        r = None
                        for k4 in range(4):
                            m16 = 4 * grp + k4
                            r = p.tensor_copy(STG[:, m16, :],
                                              PST[:, 128 * k4:128 * k4 + 128])
                        return r
                    s.on("vector", v_tb)
                    vv = s.notify("vector", "fi_v")
                    s.wait("tensor", "fi_v", vv)
                vvq = s.notify("vector", "fi_v")
                s.wait("scalar", "fi_v", vvq)
                s.dma("scalar", self.dint["fir_rin"][sl, pss, :, :],
                      STG[:, :, :])
                s.dma_wait("vector", src="scalar")
        s.barrier()
        def cc2(p):
            return p.collective_compute(
                "AllGather", AL.bypass,
                replica_groups=[list(range(8))],
                ins=[self.dint["fir_rin"].ap().opt()],
                outs=[self.dint["fir_r8"].ap().opt()])
        s.on("gpsimd", cc2, kind="data")
        self.ccv_firr8 = s.notify("gpsimd", "ccd")

    def stage_fir_finish(self):
        s = self.s
        s.wait("gpsimd", "ccd", self.ccv_firr8)
        self.dram("lsll", [2, NCH, 128, NT], BF16)
        for pss in range(2):
            def g_r(p, pss=pss):
                src_ = self.dint["fir_r8"].ap()[:, bass.ds(self.dyx["rank"], 1),
                                                pss, :, :]
                src_ = src_.rearrange("s o p t -> (s o) p t")
                return p.dma_start(out=self.dint["lsll"].ap()[1 - pss, :, :, :],
                                   in_=src_).then_inc(s.dsem["gpsimd"], 16)
            s.on("gpsimd", g_r, kind="seq")
            s.dcnt["gpsimd"] += 16
        s.barrier()
        if self.dbg_on:
            self.dbg("ls", self.dint["lsll"].ap()[0, :, :, :], [NCH, 128, NT], BF16)
            self.dbg("ll", self.dint["lsll"].ap()[1, :, :, :], [NCH, 128, NT], BF16)

    def stage_gates(self):
        s = self.s
        E = self.ERA
        NP4 = [(0, 512), (512, 1024), (1024, 1536), (1536, 2048)]
        self.NP4 = NP4
        DOUTM = self.sb["DOUTM"]
        PSR = self.ps["PSR"]
        STATR = self.at("STATR", [16, ROWW], F32, E + 131456)
        OGR = self.at("OGR", [8, ROWW], F32, E + 140160)
        VEC = self.at("VEC", [1, 16], F32, E + 148864)
        RT1 = self.at("RT1", [128, 8], F32, E + 148928)
        W9 = self.at("W9", [16, 1024], BF16, E + 149728)
        BG1 = self.at("BG1", [128, NCH], F32, E + 151776)
        WG2 = self.at("WG2", [128, NCH, 8], BF16, E + 151808)
        STATB = self.at("STATB", [16, ROWW], BF16, E + 151936)
        STR = self.at("STR", [128, 2, NCH, 512], BF16, E + 98688)
        self.at("HS1", [128, NCH, 512], BF16, E + 115072)
        self.at("HS2", [128, NCH, 512], BF16, E + 123264)
        self.dram("prow", [8, NT], F32)
        s.dma("sync", VEC[:, :], self.di["vec16"][:, :])
        s.barrier()
        def v_zero(p):
            p.memset(STATR[0:16, 0:ROWW], 0.0)
            return p.memset(OGR[0:8, 0:ROWW], 0.0)
        s.on("vector", v_zero)
        s.barrier()

        self.head_stats(DOUTM, 8, STATR, None, "hsdo")
        self.head_stats(None, 0, STATR, ("lsll", 0), "hsls")
        self.head_stats(None, 4, STATR, ("lsll", 1), "hsll")
        self.head_stats(None, 12, STATR, ("vdir", None), "hsvd")
        if self.dbg_on:
            self.dbg("statr", STATR[0:16, 0:ROWW], [16, ROWW], F32)

        GI = self.at("GI", [128, NCH, NT], BF16, E)
        def v_statb(p):
            return p.tensor_copy(STATB[0:16, 0:ROWW], STATR[0:16, 0:ROWW])
        s.on("vector", v_statb)
        s.barrier()
        WB, WF = self.sb["WB"], self.sb["WF32"]
        for kc in range(NCH):
            s.dma("sync", WF[:, 0:1024], self.di["wg1"][128 * kc:128 * (kc + 1), :])
            s.dma_wait("vector", src="sync")
            def vc(p, kc=kc):
                return p.tensor_copy(WB[:, kc, :], WF[:, 0:1024])
            s.on("vector", vc)
            vv = s.notify("vector", "pw_v")
            s.wait("sync", "pw_v", vv)
        s.dma("sync", WF[0:16, 0:1024], self.di["wg1"][1024:1040, :])
        s.dma_wait("vector", src="sync")
        def vc9(p):
            return p.tensor_copy(W9[0:16, :], WF[0:16, 0:1024])
        s.on("vector", vc9)
        vv9 = s.notify("vector", "pw_v")
        s.wait("tensor", "pw_v")
        s.wait("sync", "pw_v", vv9)
        s.dma("sync", BG1[:, :], self.di["bg1c"][:, :])
        s.dma("sync", WG2[:, :, :], self.di["wg2b"].ap())
        s.barrier()
        PSW = self.ps["PSW"]
        for oc in range(NCH):
            def t_g(p, oc=oc):
                r = None
                for (t0, t1) in NP4:
                    for kc in range(NCH):
                        p.matmul(PSW[:, t0:t1], WB[:, kc, 128 * oc:128 * oc + 128],
                                 self.sb["XN"][:, kc, HALO + t0:HALO + t1],
                                 start=(kc == 0), stop=False)
                    r = p.matmul(PSW[:, t0:t1], W9[0:16, 128 * oc:128 * oc + 128],
                                 STATB[0:16, HALO + t0:HALO + t1],
                                 start=False, stop=True)
                return r
            s.on("tensor", t_g)
            tv = s.notify("tensor", "pw_t")
            s.wait("scalar", "pw_t", tv)
            def s_g(p, oc=oc):
                r = None
                for (t0, t1) in NP4:
                    r = p.activation(GI[:, oc, t0:t1], PSW[:, t0:t1], AF.Gelu,
                                     bias=BG1[:, oc:oc + 1], scale=1.0)
                return r
            s.on("scalar", s_g)
            sv = s.notify("scalar", "pw_s")
            s.wait("tensor", "pw_s", sv)
        s.barrier()
        for (t0, t1) in NP4:
            w = t1 - t0
            def t_l(p, t0=t0, w=w):
                r = None
                for kc in range(NCH):
                    r = p.matmul(PSR[0:8, 0:w], WG2[:, kc, :], GI[:, kc, t0:t0 + w],
                                 start=(kc == 0), stop=(kc == NCH - 1))
                return r
            s.on("tensor", t_l)
            tv = s.notify("tensor", "lg_t")
            s.wait("vector", "lg_t", tv)
            def v_l(p, t0=t0, w=w):
                return p.tensor_copy(self.sb["ROWT"][0:8, HALO + t0:HALO + t0 + w],
                                     PSR[0:8, 0:w])
            s.on("vector", v_l)
            vv = s.notify("vector", "lg_v")
            s.wait("tensor", "lg_v", vv)
        s.barrier()
        def s_T2(p):
            p.activation(RT1[0:1, 1:2], VEC[0:1, 5:6], AF.Exp)
            p.drain()
            return p.activation(RT1[0:1, 2:3], RT1[0:1, 1:2], AF.Ln, bias=1.0,
                                scale=1.0)
        s.on("scalar", s_T2)
        sv = s.notify("scalar", "lg_s")
        s.wait("vector", "lg_s", sv)
        def v_T(p):
            return p.reciprocal(RT1[0:1, 3:4], RT1[0:1, 2:3])
        s.on("vector", v_T)
        s.notify("vector", "lg_v")
        s.barrier()
        self.dram("scal1", [1, 8], F32)
        s.dma("gpsimd", self.dint["scal1"][0:1, 0:1], RT1[0:1, 3:4])
        s.dma_wait("gpsimd")
        s.dma("gpsimd", RT1[:, 0:1],
              self.dint["scal1"].ap()[0:1, 0:1].partition_broadcast(128))
        for h in range(4):
            s.dma("gpsimd", RT1[:, 4 + h:5 + h],
                  self.di["vec16"][0:1, h:h + 1].partition_broadcast(128))
        self.rows_to_rp([0, 1, 2, 3])
        s.barrier()
        RP = self.sb["RP"]
        SL = [RP[:, 17 * i:17 * i + 17] for i in range(4)]
        MXs = RP[:, 68:85]
        def v_sm1(p):
            for h in range(4):
                p.tensor_scalar_add(SL[h], SL[h], RT1[:, 4 + h:5 + h])
            p.drain()
            p.tensor_scalar_mul(RP[:, 0:68], RP[:, 0:68], RT1[:, 0:1])
            p.drain()
            p.tensor_tensor(MXs, SL[0], SL[1], AL.max)
            p.drain()
            p.tensor_tensor(MXs, MXs, SL[2], AL.max)
            p.drain()
            p.tensor_tensor(MXs, MXs, SL[3], AL.max)
            p.drain()
            r = None
            for h in range(4):
                r = p.tensor_tensor(SL[h], SL[h], MXs, AL.subtract)
            return r
        s.on("vector", v_sm1)
        vv = s.notify("vector", "lg_v")
        s.wait("scalar", "lg_v", vv)
        def s_exp(p):
            return p.activation(RP[:, 0:68], RP[:, 0:68], AF.Exp)
        s.on("scalar", s_exp)
        sv = s.notify("scalar", "lg_s")
        s.wait("vector", "lg_s", sv)
        def v_sm2(p):
            r = None
            for pas in range(2):
                p.tensor_tensor(MXs, SL[0], SL[1], AL.add)
                p.drain()
                p.tensor_tensor(MXs, MXs, SL[2], AL.add)
                p.drain()
                p.tensor_tensor(MXs, MXs, SL[3], AL.add)
                p.drain()
                p.reciprocal(MXs, MXs)
                p.drain()
                for h in range(4):
                    r = p.tensor_tensor(SL[h], SL[h], MXs, AL.mult)
                p.drain()
                if pas == 0:
                    p.tensor_scalar_max(RP[:, 0:68], RP[:, 0:68], 0.02)
                    p.drain()
            return r
        s.on("vector", v_sm2)
        s.notify("vector", "lg_v")
        s.barrier()
        self.rp_to_rows([0, 1, 2, 3], [0, 1, 2, 3])
        s.barrier()
        s.dma("gpsimd", self.dint["prow"].ap()[0:4, :],
              self.sb["ROWT"][0:4, HALO:HALO + NT])
        s.barrier()
        for h in range(4):
            s.dma("gpsimd", self.sb["RP"][:, 17 * h:17 * h + 17],
                  self.sb["ROWP"][4 + h:5 + h, 0:ROWW])
            s.dma("gpsimd", RT1[:, 4 + h:5 + h],
                  self.di["vec16"][0:1, 6 + h:7 + h].partition_broadcast(128))
        s.barrier()
        def v_res(p):
            r = None
            for h in range(4):
                r = p.tensor_scalar_add(self.sb["RP"][:, 17 * h:17 * h + 17],
                                        self.sb["RP"][:, 17 * h:17 * h + 17],
                                        RT1[:, 4 + h:5 + h])
            return r
        s.on("vector", v_res)
        vv = s.notify("vector", "rg_v")
        s.wait("scalar", "rg_v", vv)
        def s_res(p):
            return p.activation(self.sb["RP"][:, 0:68], self.sb["RP"][:, 0:68],
                                AF.Sigmoid)
        s.on("scalar", s_res)
        s.notify("scalar", "rg_s")
        s.barrier()
        for h in range(4):
            s.dma("gpsimd", RT1[:, 4 + h:5 + h],
                  self.di["vec16"][0:1, 10 + h:11 + h].partition_broadcast(128))
        s.barrier()
        def s_crl(p):
            return p.activation(RT1[:, 4:8], RT1[:, 4:8], AF.Sigmoid)
        s.on("scalar", s_crl)
        sv = s.notify("scalar", "rg_s")
        s.wait("vector", "rg_s", sv)
        def v_t5(p):
            r = None
            for h in range(4):
                sl_ = self.sb["RP"][:, 17 * h:17 * h + 17]
                r = p.tensor_scalar_mul(sl_, sl_, RT1[:, 4 + h:5 + h])
            return r
        s.on("vector", v_t5)
        s.notify("vector", "rg_v")
        s.barrier()
        for h in range(4):
            s.dma("gpsimd", self.sb["ROWT"][4 + h:5 + h, 0:ROWW],
                  self.sb["RP"][:, 17 * h:17 * h + 17])
        s.barrier()
        if self.dbg_on:
            self.dbg("prow", self.dint["prow"].ap()[0:4, :], [4, NT], F32)
            self.dbg("t5", self.sb["ROWT"][0:8, 0:ROWW], [8, ROWW], F32)

    def head_stats(self, tile, dst_rows, dst_tile, stream, tag):
        s = self.s
        NP4 = self.NP4
        SCB1, SCB2 = self.sb["HS1"], self.sb["HS2"]
        STR = self.sb["STR"]
        PSR = self.ps["PSR"]
        last_t = None
        tvals = []
        for pi, (t0, t1) in enumerate(NP4):
            w = t1 - t0
            if stream is not None:
                nm, idx = stream
                ap = (self.dint[nm].ap()[idx, :, :, t0:t1] if idx is not None
                      else self.dint[nm].ap()[:, :, t0:t1])
                qq = "gpsimd" if pi % 2 == 0 else "sync"
                if pi >= 2:
                    s.wait(qq, tag + "_t", tvals[pi - 2])
                s.dma(qq, STR[:, pi % 2, :, 0:w],
                      ap.rearrange("s p t -> p s t"))
                dv = s.dcnt[qq]
                for e in ("vector", "scalar", "tensor"):
                    s.dma_wait(e, upto=dv, src=qq)
                srcf = lambda c, pi=pi, w=w: STR[:, pi % 2, c, 0:w]
            else:
                srcf = lambda c, t0=t0, w=w: tile[:, c, t0:t0 + w]
            if last_t is not None:
                s.wait("vector", tag + "_t", last_t)
                s.wait("scalar", tag + "_t", last_t)
            def v_sq(p, srcf=srcf, w=w):
                r = None
                for c in range(NCH):
                    r = p.tensor_tensor(SCB1[:, c, 0:w], srcf(c), srcf(c),
                                        AL.mult)
                return r
            s.on("vector", v_sq)
            vv = s.notify("vector", tag + "_v")
            def s_ab(p, srcf=srcf, w=w):
                r = None
                for c in range(NCH):
                    r = p.activation(SCB2[:, c, 0:w], srcf(c), AF.Abs)
                return r
            s.on("scalar", s_ab)
            sv = s.notify("scalar", tag + "_s")
            s.wait("tensor", tag + "_v", vv)
            s.wait("tensor", tag + "_s", sv)
            def t_fn(p, srcf=srcf, w=w):
                r = None
                for c in range(NCH):
                    h = c // 2
                    r = p.matmul(PSR[0:12, 0:w], self.sb["EJ"][:, 16 * h:16 * h + 12],
                                 srcf(c), start=(c == 0), stop=False)
                for c in range(NCH):
                    h = c // 2
                    r = p.matmul(PSR[0:12, 0:w],
                                 self.sb["EJ"][:, 16 * (4 + h):16 * (4 + h) + 12],
                                 SCB1[:, c, 0:w], start=False, stop=False)
                for c in range(NCH):
                    h = c // 2
                    r = p.matmul(PSR[0:12, 0:w],
                                 self.sb["EJ"][:, 16 * (8 + h):16 * (8 + h) + 12],
                                 SCB2[:, c, 0:w], start=False,
                                 stop=(c == NCH - 1))
                return r
            s.on("tensor", t_fn)
            tv = s.notify("tensor", tag + "_t")
            last_t = tv
            tvals.append(tv)
            s.wait("vector", tag + "_t", tv)
            def v_cp(p, t0=t0, w=w):
                return p.tensor_copy(
                    self.sb["ROWT"][0:12, HALO + t0:HALO + t0 + w],
                    PSR[0:12, 0:w])
            s.on("vector", v_cp)
            vv2 = s.notify("vector", tag + "_v")
            s.wait("tensor", tag + "_v", vv2)
        def vz(p):
            p.memset(self.sb["ROWT"][0:12, 0:HALO], 0.0)
            return p.memset(self.sb["ROWT"][0:12, HALO + NT:ROWW], 0.0)
        s.on("vector", vz)
        s.barrier()
        self.rows_to_rp(list(range(12)))
        s.barrier()
        RP = self.sb["RP"]
        def sl(i):
            return RP[:, 17 * i:17 * i + 17]
        def v_m1(p):
            for h in range(4):
                p.tensor_copy(RP[:, 204 + 17 * h:204 + 17 * h + 17], sl(4 + h))
            p.drain()
            for i in range(12):
                p.tensor_scalar_mul(sl(i), sl(i), 1.0 / 256.0)
            p.drain()
            for h in range(4):
                p.tensor_tensor(RP[:, 272 + 17 * h:272 + 17 * h + 17],
                                sl(h), sl(h), AL.mult)
            p.drain()
            r = None
            for h in range(4):
                r = p.tensor_tensor(sl(4 + h), sl(4 + h),
                                    RP[:, 272 + 17 * h:272 + 17 * h + 17],
                                    AL.subtract)
            return r
        s.on("vector", v_m1)
        vv = s.notify("vector", tag + "m_v")
        s.wait("scalar", tag + "m_v", vv)
        def s_m(p):
            return p.activation(RP[:, 204:272], RP[:, 204:272], AF.Sqrt)
        s.on("scalar", s_m)
        sv = s.notify("scalar", tag + "m_s")
        s.wait("vector", tag + "m_s", sv)
        def v_m2(p):
            for kind in range(4):
                if kind < 3:
                    bases = [sl(4 * kind + h) for h in range(4)]
                else:
                    bases = [RP[:, 204 + 17 * h:204 + 17 * h + 17] for h in range(4)]
                d = RP[:, 340 + 17 * kind:340 + 17 * kind + 17]
                p.tensor_tensor(d, bases[0], bases[1], AL.add)
                p.drain()
                p.tensor_tensor(d, d, bases[2], AL.add)
                p.drain()
                p.tensor_tensor(d, d, bases[3], AL.add)
                p.drain()
                r = p.tensor_scalar_mul(d, d, 0.25)
                p.drain()
            return r
        s.on("vector", v_m2)
        s.notify("vector", tag + "m_v")
        s.barrier()
        for kind in range(4):
            s.dma("gpsimd", dst_tile[dst_rows + kind:dst_rows + kind + 1, 0:ROWW],
                  self.sb["RP"][:, 340 + 17 * kind:340 + 17 * kind + 17])
        s.barrier()

    def stage_mix_out(self):
        s = self.s
        E = self.ERA
        NP4 = self.NP4
        DOUTM = self.sb["DOUTM"]
        STR = self.sb["STR"]
        MIXC = self.at("MIXC", [128, 8, NT], BF16, E)
        SCB1 = self.sb["HS1"]
        for i4 in range(4):
            self.bcast_row_dram("prow", i4, self.sb["BCA"])
            s.barrier()
            def vc(p, i4=i4):
                return p.tensor_copy(MIXC[:, i4, :], self.sb["BCA"][:, HALO:HALO + NT])
            s.on("vector", vc)
            s.barrier()
        for h in range(4):
            self.bcast_row(4 + h, self.sb["BCA"])
            s.barrier()
            def vc(p, h=h):
                return p.tensor_copy(MIXC[:, 4 + h, :], self.sb["BCA"][:, HALO:HALO + NT])
            s.on("vector", vc)
            s.barrier()
        for pi, (t0, t1) in enumerate(NP4):
            w = t1 - t0
            s.dma("gpsimd", STR[:, 0, :, 0:w],
                  self.dint["lsll"].ap()[0, :, :, t0:t1].rearrange("s p t -> p s t"))
            s.dma("gpsimd", STR[:, 1, :, 0:w],
                  self.dint["lsll"].ap()[1, :, :, t0:t1].rearrange("s p t -> p s t"))
            dv = s.dcnt["gpsimd"]
            s.dma_wait("vector", upto=dv, src="gpsimd")
            def v_mix1(p, t0=t0, w=w):
                r = None
                for c in range(NCH):
                    p.tensor_tensor(DOUTM[:, c, t0:t0 + w], DOUTM[:, c, t0:t0 + w],
                                    MIXC[:, 2, t0:t0 + w], AL.mult)
                for c in range(NCH):
                    p.tensor_tensor(SCB1[:, c, 0:w], STR[:, 0, c, 0:w],
                                    MIXC[:, 0, t0:t0 + w], AL.mult)
                p.drain()
                p.drain()
                for c in range(NCH):
                    p.tensor_tensor(DOUTM[:, c, t0:t0 + w], DOUTM[:, c, t0:t0 + w],
                                    SCB1[:, c, 0:w], AL.add)
                p.drain()
                for c in range(NCH):
                    p.tensor_tensor(SCB1[:, c, 0:w], STR[:, 0, c, 0:w],
                                    MIXC[:, 4 + c // 2, t0:t0 + w], AL.mult)
                p.drain()
                for c in range(NCH):
                    p.tensor_tensor(DOUTM[:, c, t0:t0 + w], DOUTM[:, c, t0:t0 + w],
                                    SCB1[:, c, 0:w], AL.add)
                p.drain()
                for c in range(NCH):
                    p.tensor_tensor(SCB1[:, c, 0:w], STR[:, 1, c, 0:w],
                                    MIXC[:, 1, t0:t0 + w], AL.mult)
                p.drain()
                for c in range(NCH):
                    r = p.tensor_tensor(DOUTM[:, c, t0:t0 + w],
                                        DOUTM[:, c, t0:t0 + w],
                                        SCB1[:, c, 0:w], AL.add)
                return r
            s.on("vector", v_mix1)
            vv = s.notify("vector", "mx_v")
            s.wait("gpsimd", "mx_v", vv)
            s.dma("gpsimd", STR[:, 0, :, 0:w],
                  self.dint["vdir"].ap()[:, :, t0:t1].rearrange("s p t -> p s t"))
            dv = s.dcnt["gpsimd"]
            s.dma_wait("vector", upto=dv, src="gpsimd")
            def v_mix2(p, t0=t0, w=w):
                r = None
                for c in range(NCH):
                    p.tensor_tensor(SCB1[:, c, 0:w], STR[:, 0, c, 0:w],
                                    MIXC[:, 3, t0:t0 + w], AL.mult)
                p.drain()
                for c in range(NCH):
                    r = p.tensor_tensor(DOUTM[:, c, t0:t0 + w],
                                        DOUTM[:, c, t0:t0 + w],
                                        SCB1[:, c, 0:w], AL.add)
                return r
            s.on("vector", v_mix2)
            vv = s.notify("vector", "mx_v")
            s.wait("gpsimd", "mx_v", vv)
        s.barrier()
        if self.dbg_on:
            self.dbg("omix", DOUTM[:, :, :], [128, NCH, NT], BF16)
        self.head_stats(DOUTM, 0, self.sb["OGR"], None, "hso")
        for kk in range(4):
            s.dma("gpsimd", self.sb["OGR"][4 + kk:5 + kk, 0:ROWW],
                  self.sb["STATR"][12 + kk:13 + kk, 0:ROWW])
        s.barrier()
        OG1 = self.at("OG1", [128, 4, NT], BF16, E + 98688)
        OGSB = self.at("OGSB", [8, ROWW], BF16, E + 156288)
        BOG1 = self.at("BOG1", [128, 4], F32, E + 119488)
        WOG2 = self.at("WOG2", [128, 4, 8], BF16, E + 119552)
        WB, WF = self.sb["WB"], self.sb["WF32"]
        WB2 = self.at("WB2", [128, NCH, 512], BF16, E + 119680)
        W9b = self.at("W9b", [8, 512], BF16, E + 127872)
        def v_ogsb(p):
            return p.tensor_copy(OGSB[0:8, 0:ROWW], self.sb["OGR"][0:8, 0:ROWW])
        s.on("vector", v_ogsb)
        for kc in range(NCH):
            s.dma("sync", WF[:, 0:512], self.di["wog1"][128 * kc:128 * (kc + 1), :])
            s.dma_wait("vector", src="sync")
            def vc(p, kc=kc):
                return p.tensor_copy(WB2[:, kc, :], WF[:, 0:512])
            s.on("vector", vc)
            vv = s.notify("vector", "pw_v")
            s.wait("sync", "pw_v", vv)
        s.dma("sync", WF[0:8, 0:512], self.di["wog1"][1024:1032, :])
        s.dma_wait("vector", src="sync")
        def vc9(p):
            return p.tensor_copy(W9b[0:8, :], WF[0:8, 0:512])
        s.on("vector", vc9)
        vv9 = s.notify("vector", "pw_v")
        s.wait("tensor", "pw_v")
        s.wait("sync", "pw_v", vv9)
        s.dma("sync", BOG1[:, :], self.di["bog1c"][:, :])
        s.dma("sync", WOG2[:, :, :], self.di["wog2b"].ap())
        s.barrier()
        PSW, PSR = self.ps["PSW"], self.ps["PSR"]
        for oc in range(4):
            def t_og(p, oc=oc):
                r = None
                for (t0, t1) in NP4:
                    for kc in range(NCH):
                        p.matmul(PSW[:, t0:t1], WB2[:, kc, 128 * oc:128 * oc + 128],
                                 self.sb["XN"][:, kc, HALO + t0:HALO + t1],
                                 start=(kc == 0), stop=False)
                    r = p.matmul(PSW[:, t0:t1], W9b[0:8, 128 * oc:128 * oc + 128],
                                 OGSB[0:8, HALO + t0:HALO + t1],
                                 start=False, stop=True)
                return r
            s.on("tensor", t_og)
            tv = s.notify("tensor", "pw_t")
            s.wait("scalar", "pw_t", tv)
            def s_og(p, oc=oc):
                r = None
                for (t0, t1) in NP4:
                    r = p.activation(OG1[:, oc, t0:t1], PSW[:, t0:t1], AF.Gelu,
                                     bias=BOG1[:, oc:oc + 1], scale=1.0)
                return r
            s.on("scalar", s_og)
            sv = s.notify("scalar", "pw_s")
            s.wait("tensor", "pw_s", sv)
        s.barrier()
        for (t0, t1) in NP4:
            w = t1 - t0
            def t_o2(p, t0=t0, w=w):
                r = None
                for kc in range(4):
                    r = p.matmul(PSR[0:8, 0:w], WOG2[:, kc, :],
                                 OG1[:, kc, t0:t0 + w],
                                 start=(kc == 0), stop=(kc == 3))
                return r
            s.on("tensor", t_o2)
            tv = s.notify("tensor", "og_t")
            s.wait("scalar", "og_t", tv)
            def s_o2(p, t0=t0, w=w):
                return p.activation(self.sb["ROWT"][0:1, HALO + t0:HALO + t0 + w],
                                    PSR[0:1, 0:w], AF.Sigmoid,
                                    bias=self.sb["VEC"][0:1, 4:5], scale=1.0)
            s.on("scalar", s_o2)
            sv = s.notify("scalar", "og_s")
            s.wait("tensor", "og_s", sv)
        s.barrier()
        self.bcast_row(0, self.sb["BCA"])
        s.barrier()
        def v_og(p):
            r = None
            for c in range(NCH):
                r = p.tensor_tensor(DOUTM[:, c, :], DOUTM[:, c, :],
                                    self.sb["BCA"][:, HALO:HALO + NT], AL.mult)
            return r
        s.on("vector", v_og)
        s.barrier()
        SCB1 = self.sb["HS1"]
        last_t = None
        for (t0, t1) in NP4:
            w = t1 - t0
            if last_t is not None:
                s.wait("vector", "rn_t", last_t)
            def v_sq(p, t0=t0, w=w):
                r = None
                for c in range(NCH):
                    r = p.tensor_tensor(SCB1[:, c, 0:w], DOUTM[:, c, t0:t0 + w],
                                        DOUTM[:, c, t0:t0 + w], AL.mult)
                return r
            s.on("vector", v_sq)
            vv = s.notify("vector", "rn_v")
            s.wait("tensor", "rn_v", vv)
            def t_rn(p, t0=t0, w=w):
                r = None
                for c in range(NCH):
                    h = c // 2
                    r = p.matmul(PSR[0:4, 0:w], self.sb["EJ"][:, 16 * h:16 * h + 4],
                                 SCB1[:, c, 0:w], start=(c == 0),
                                 stop=(c == NCH - 1))
                return r
            s.on("tensor", t_rn)
            tv = s.notify("tensor", "rn_t")
            last_t = tv
            s.wait("vector", "rn_t", tv)
            def v_cp(p, t0=t0, w=w):
                return p.tensor_copy(self.sb["ROWT"][0:4, HALO + t0:HALO + t0 + w],
                                     PSR[0:4, 0:w])
            s.on("vector", v_cp)
            vv2 = s.notify("vector", "rn_v")
            s.wait("tensor", "rn_v", vv2)
        def vz(p):
            p.memset(self.sb["ROWT"][0:4, 0:HALO], 0.0)
            return p.memset(self.sb["ROWT"][0:4, HALO + NT:ROWW], 0.0)
        s.on("vector", vz)
        s.barrier()
        self.rows_to_rp([0, 1, 2, 3])
        s.barrier()
        RP = self.sb["RP"]
        def v_rn(p):
            p.tensor_scalar_mul(RP[:, 0:68], RP[:, 0:68], 1.0 / 256.0)
            p.drain()
            p.tensor_scalar_add(RP[:, 0:68], RP[:, 0:68], RMS_EPS)
            p.drain()
            return p.reciprocal(RP[:, 0:68], RP[:, 0:68])
        s.on("vector", v_rn)
        vv = s.notify("vector", "rn_v")
        s.wait("scalar", "rn_v", vv)
        def s_rn(p):
            return p.activation(RP[:, 0:68], RP[:, 0:68], AF.Sqrt)
        s.on("scalar", s_rn)
        s.notify("scalar", "rn_s")
        s.barrier()
        self.rp_to_rows([0, 1, 2, 3], [0, 1, 2, 3])
        s.barrier()
        RSB = self.at("RSB", [128, 4, NT], BF16, E + 128896)
        for h in range(4):
            self.bcast_row(h, self.sb["BCA"])
            s.barrier()
            def vc(p, h=h):
                return p.tensor_copy(RSB[:, h, :], self.sb["BCA"][:, HALO:HALO + NT])
            s.on("vector", vc)
            s.barrier()
        NRMW = self.at("NRMW", [128, NCH], F32, E + 145280)
        s.dma("sync", NRMW[:, :], self.di["normwc"][:, :])
        s.barrier()
        def v_rnap(p):
            r = None
            for c in range(NCH):
                r = p.scalar_tensor_tensor(DOUTM[:, c, :], DOUTM[:, c, :],
                                           NRMW[:, c:c + 1], RSB[:, c // 2, :],
                                           AL.mult, AL.mult)
            return r
        s.on("vector", v_rnap)
        s.barrier()
        if self.dbg_on:
            self.dbg("onrm", DOUTM[:, :, :], [128, NCH, NT], BF16)
        X1a = self.at("X1a", [128, 4, NT], F32, E)
        X1b = self.at("X1b", [128, 4, NT], F32, E + 65792)
        def x1t(oc):
            return (X1a[:, oc, :] if oc < 4 else X1b[:, oc - 4, :])
        WB, WF = self.sb["WB"], self.sb["WF32"]
        for kc in range(NCH):
            s.dma("sync", WF[:, 0:1024], self.di["wo"][128 * kc:128 * (kc + 1), :])
            s.dma_wait("vector", src="sync")
            def vc(p, kc=kc):
                return p.tensor_copy(WB[:, kc, :], WF[:, 0:1024])
            s.on("vector", vc)
            vv = s.notify("vector", "pw_v")
            s.wait("sync", "pw_v", vv)
        s.wait("tensor", "pw_v")
        for oc in range(NCH):
            s.dma("sync", x1t(oc), self.di["x0"].ap().rearrange(
                "(c p) t -> p c t", p=128)[:, oc, HALO:HALO + NT])
        s.barrier()
        for oc in range(NCH):
            def t_wo(p, oc=oc):
                r = None
                for (t0, t1) in NP4:
                    for kc in range(NCH):
                        r = p.matmul(PSW[:, t0:t1],
                                     WB[:, kc, 128 * oc:128 * oc + 128],
                                     DOUTM[:, kc, t0:t1], start=(kc == 0),
                                     stop=(kc == NCH - 1))
                return r
            s.on("tensor", t_wo)
            tv = s.notify("tensor", "pw_t")
            s.wait("vector", "pw_t", tv)
            def v_wo(p, oc=oc):
                r = None
                for (t0, t1) in NP4:
                    r = p.tensor_tensor(x1t(oc)[:, t0:t1], x1t(oc)[:, t0:t1],
                                        PSW[:, t0:t1], AL.add)
                return r
            s.on("vector", v_wo)
            vv = s.notify("vector", "mx_v")
            s.wait("tensor", "mx_v", vv)
        s.barrier()
        if self.dbg_on:
            self.dbg("x1a", X1a[:, :, :], [128, 4, NT], F32)
            self.dbg("x1b", X1b[:, :, :], [128, 4, NT], F32)

    def bcast_row_dram(self, name, row, dst, width=NT):
        s = self.s
        def dst_ap():
            d = dst() if callable(dst) else dst
            return d[:, HALO:HALO + width]
        s.dma("gpsimd", dst_ap,
              lambda: self.dint[name].ap()[row:row + 1, 0:width]
              .partition_broadcast(128))

    def stage_ffn1(self):
        self.ffn_apply("f1")
        if self.dbg_on:
            X1a, X1b = self.sb["X1a"], self.sb["X1b"]
            self.dbg("x2a", X1a[:, :, :], [128, 4, NT], F32)
            self.dbg("x2b", X1b[:, :, :], [128, 4, NT], F32)

    def ffn_ternarize(self):
        s = self.s
        E = self.ERA
        WE = self.at("WE", [128, NCH, 1024], BF16, E + 98688)
        WF = self.sb["WF32"]
        TN1 = self.at("TN1", [128, 1024], F32, E + 119680)
        TN2 = self.at("TN2", [128, 1024], F32, E + 123776)
        TN3 = self.at("TN3", [128, 1024], F32, E + 127872)
        TB1 = self.at("TB1", [128, 1024], BF16, E + 131968)
        TB2 = self.at("TB2", [128, 1024], BF16, E + 134016)
        PRS = self.sb_off["PRSV"][0]
        SCL2 = self.at("SCL2", [128, 8], F32, E + 152448)
        self.sb["SCL2z"] = SCL2
        GAM = self.at("GAM", [128, NCH], F32, PRS)
        BET = self.at("BET", [128, NCH], F32, PRS + 32)
        AL1 = self.at("AL1", [1, 8], F32, PRS + 64)
        self.at("ROWB", [1, 128], F32, E + 152512)
        self.at("MK8", [128, 4], F32, E + 152480)
        s.dma("sync", GAM[:, :], self.di["gamc"][:, :])
        s.dma("sync", BET[:, :], self.di["betc"][:, :])
        s.barrier()
        PSW = self.ps["PSW"]
        for kc in range(NCH):
            s.dma("sync", WF[:, 0:1024],
                  self.di["ffnw"][128 * kc:128 * (kc + 1), :])
            s.dma_wait("scalar", src="sync")
            def s_a(p):
                return p.activation(TB1[:, :], WF[:, 0:1024], AF.Abs)
            s.on("scalar", s_a)
            sv = s.notify("scalar", "tz_s")
            s.wait("tensor", "tz_s", sv)
            def t_a(p, kc=kc):
                r = None
                for sg in range(2):
                    r = p.matmul(PSW[0:1, 512 * sg:512 * sg + 512],
                                 self.sb["EJ"][:, 0:1],
                                 TB1[:, 512 * sg:512 * sg + 512],
                                 start=(kc == 0), stop=(kc == NCH - 1))
                return r
            s.on("tensor", t_a)
            tv = s.notify("tensor", "tz_t")
            s.wait("sync", "tz_t", tv)
        s.barrier()
        def v_d(p):
            p.tensor_reduce(SCL2[0:1, 4:5], PSW[0:1, 0:1024],
                            mybir.AxisListType.X, AL.add)
            p.drain()
            return p.tensor_scalar_mul(SCL2[0:1, 4:5], SCL2[0:1, 4:5],
                                       0.7 / (1024.0 * 1024.0))
        s.on("vector", v_d)
        s.barrier()
        self.dram("scal2", [1, 8], F32)
        s.dma("gpsimd", self.dint["scal2"][0:1, 0:1], SCL2[0:1, 4:5])
        s.dma_wait("gpsimd")
        s.dma("gpsimd", SCL2[:, 5:6],
              self.dint["scal2"].ap()[0:1, 0:1].partition_broadcast(128))
        s.barrier()
        for kc in range(NCH):
            s.dma("sync", WF[:, 0:1024],
                  self.di["ffnw"][128 * kc:128 * (kc + 1), :])
            s.dma_wait("scalar", src="sync")
            def s_b(p):
                p.activation(TN1[:, :], WF[:, 0:1024], AF.Abs)
                return p.activation(TN2[:, :], WF[:, 0:1024], AF.Sign)
            s.on("scalar", s_b)
            sv = s.notify("scalar", "tz_s")
            s.wait("vector", "tz_s", sv)
            s.wait("sync", "tz_s", sv)
            def v_b(p):
                return p.tensor_scalar_sub(TN3[:, :], TN1[:, :], SCL2[:, 5:6])
            s.on("vector", v_b)
            vv = s.notify("vector", "tz_v")
            s.wait("scalar", "tz_v", vv)
            def s_c(p):
                return p.activation(TN3[:, :], TN3[:, :], AF.Sign)
            s.on("scalar", s_c)
            sv = s.notify("scalar", "tz_s")
            s.wait("vector", "tz_s", sv)
            def v_c(p, kc=kc):
                p.tensor_scalar_max(TN3[:, :], TN3[:, :], 0.0)
                p.drain()
                p.tensor_copy(TB1[:, :], TN3[:, :])
                p.drain()
                p.tensor_tensor(WE[:, kc, :], TN2[:, :], TN3[:, :], AL.mult)
                p.drain()
                return p.tensor_tensor(TB2[:, :], TN1[:, :], TN3[:, :],
                                       AL.mult)
            s.on("vector", v_c)
            vv = s.notify("vector", "tz_v")
            s.wait("tensor", "tz_v", vv)
            def t_b(p, kc=kc):
                r = None
                for sg in range(2):
                    sl_ = slice(512 * sg, 512 * sg + 512)
                    p.matmul(PSW[0:2, 1024 + 512 * sg:1536 + 512 * sg],
                             self.sb["EJ"][:, 0:2], TB2[:, sl_],
                             start=(kc == 0), stop=False)
                    r = p.matmul(PSW[0:2, 1024 + 512 * sg:1536 + 512 * sg],
                                 self.sb["EJ"][:, 16:18], TB1[:, sl_],
                                 start=False, stop=(kc == NCH - 1))
                return r
            s.on("tensor", t_b)
            tv = s.notify("tensor", "tz_t")
            s.wait("sync", "tz_t", tv)
        s.barrier()
        def v_al(p):
            p.tensor_reduce(SCL2[0:2, 6:7], PSW[0:2, 1024:2048],
                            mybir.AxisListType.X, AL.add)
            return p.drain()
        s.on("vector", v_al)
        s.barrier()
        s.dma("gpsimd", AL1[0:1, 2:4], SCL2[0:2, 6:7])
        s.barrier()
        def v_al2(p):
            p.tensor_scalar_max(AL1[0:1, 3:4], AL1[0:1, 3:4], 1.0)
            p.drain()
            p.reciprocal(AL1[0:1, 3:4], AL1[0:1, 3:4])
            p.drain()
            return p.tensor_tensor(AL1[0:1, 0:1], AL1[0:1, 2:3],
                                   AL1[0:1, 3:4], AL.mult)
        s.on("vector", v_al2)
        self.dram("wedr", [128, NCH, 1024], BF16)
        s.dma("scalar", self.dint["wedr"].ap(), WE[:, :, :])
        s.barrier()

    def ffn_apply(self, tag):
        s = self.s
        E = self.ERA
        NP4 = self.NP4
        WE = self.at("WEL" + tag, [128, NCH, 1024], BF16, E + 98688)
        s.dma("sync", WE[:, :, :], self.dint["wedr"].ap())
        SCL2 = self.at("SCL2" + tag, [128, 8], F32, E + 152448)
        GAM, BET, AL1 = self.sb["GAM"], self.sb["BET"], self.sb["AL1"]
        MK8, ROWB = self.sb["MK8"], self.sb["ROWB"]
        HS1, HS2 = self.sb["HS1"], self.sb["HS2"]
        INT = self.at("INT" + tag, [128, NCH, NT], BF16, E + 32768)
        S2 = self.at("S2" + tag, [128, NCH, 512], F32, E + 136064)
        X1a, X1b = self.sb["X1a"], self.sb["X1b"]
        def x1t(c):
            return (X1a[:, c, :] if c < 4 else X1b[:, c - 4, :])
        PSR, PSW = self.ps["PSR"], self.ps["PSW"]
        last_t = None
        for (t0, t1) in NP4:
            w = t1 - t0
            if last_t is not None:
                s.wait("vector", tag + "r_t", last_t)
                s.wait("scalar", tag + "r_t", last_t)
            def v_cast(p, t0=t0, w=w):
                r = None
                for c in range(NCH):
                    r = p.tensor_copy(HS1[:, c, 0:w], x1t(c)[:, t0:t0 + w])
                return r
            s.on("vector", v_cast)
            vv = s.notify("vector", tag + "r_v")
            def s_sq(p, t0=t0, w=w):
                r = None
                for c in range(NCH):
                    r = p.activation(HS2[:, c, 0:w], x1t(c)[:, t0:t0 + w],
                                     AF.Square)
                return r
            s.on("scalar", s_sq)
            sv = s.notify("scalar", tag + "r_s")
            s.wait("tensor", tag + "r_v", vv)
            s.wait("tensor", tag + "r_s", sv)
            def t_r(p, w=w):
                r = None
                for c in range(NCH):
                    p.matmul(PSR[0:2, 0:w], self.sb["EJ"][:, 0:2],
                             HS1[:, c, 0:w], start=(c == 0), stop=False)
                for c in range(NCH):
                    r = p.matmul(PSR[0:2, 0:w], self.sb["EJ"][:, 16:18],
                                 HS2[:, c, 0:w], start=False,
                                 stop=(c == NCH - 1))
                return r
            s.on("tensor", t_r)
            tv = s.notify("tensor", tag + "r_t")
            last_t = tv
            s.wait("vector", tag + "r_t", tv)
            def v_cp(p, t0=t0, w=w):
                return p.tensor_copy(self.sb["ROWT"][0:2, HALO + t0:HALO + t0 + w],
                                     PSR[0:2, 0:w])
            s.on("vector", v_cp)
            vv2 = s.notify("vector", tag + "r_v")
            s.wait("tensor", tag + "r_v", vv2)
        def vz(p):
            p.memset(self.sb["ROWT"][0:2, 0:HALO], 0.0)
            return p.memset(self.sb["ROWT"][0:2, HALO + NT:ROWW], 0.0)
        s.on("vector", vz)
        s.barrier()
        self.rows_to_rp([0, 1])
        s.barrier()
        RP = self.sb["RP"]
        mu, msq, var = RP[:, 0:17], RP[:, 17:34], RP[:, 34:51]
        t_, varb = RP[:, 51:68], RP[:, 68:85]
        def v_math(p):
            p.tensor_scalar_mul(mu, mu, 1.0 / D)
            p.tensor_scalar_mul(msq, msq, 1.0 / D)
            p.drain()
            p.tensor_tensor(var, mu, mu, AL.mult)
            p.drain()
            p.tensor_tensor(var, msq, var, AL.subtract)
            p.drain()
            p.tensor_scalar_add(t_, var, LN_EPS)
            p.drain()
            p.reciprocal(varb, t_)
            p.drain()
            p.tensor_tensor(varb, var, varb, AL.mult)
            p.drain()
            p.tensor_scalar_add(varb, varb, RMS_EPS)
            p.drain()
            p.tensor_tensor(t_, t_, varb, AL.mult)
            p.drain()
            return p.reciprocal(t_, t_)
        s.on("vector", v_math)
        vv = s.notify("vector", tag + "m_v")
        s.wait("scalar", tag + "m_v", vv)
        def s_math(p):
            return p.activation(t_, t_, AF.Sqrt)
        s.on("scalar", s_math)
        s.notify("scalar", tag + "m_s")
        s.barrier()
        self.rp_to_rows([0, 3], [0, 1])
        s.barrier()
        self.bcast_row(0, self.sb["BCA"])
        self.bcast_row(1, self.sb["BCB"])
        s.barrier()
        SC = self.sb["SC"]
        for pi, (t0, t1) in enumerate(NP4):
            w = t1 - t0
            def v_a(p, t0=t0, w=w):
                r = None
                for c in range(NCH):
                    p.tensor_tensor(SC[:, 0:w], x1t(c)[:, t0:t0 + w],
                                    self.sb["BCA"][:, HALO + t0:HALO + t0 + w],
                                    AL.subtract)
                    p.drain()
                    r = p.tensor_tensor(S2[:, c, 0:w], SC[:, 0:w],
                                        self.sb["BCB"][:, HALO + t0:HALO + t0 + w],
                                        AL.mult)
                    p.drain()
                return r
            s.on("vector", v_a)
            vv = s.notify("vector", tag + "a_v")
            s.wait("scalar", tag + "a_v", vv)
            def s_a(p, w=w):
                r = None
                for c in range(NCH):
                    r = p.activation(S2[:, c, 0:w], S2[:, c, 0:w], AF.Identity,
                                     bias=BET[:, c:c + 1], scale=GAM[:, c:c + 1])
                return r
            s.on("scalar", s_a)
            sv = s.notify("scalar", tag + "a_s")
            s.wait("vector", tag + "a_s", sv)
            def v_mx(p, pi=pi, w=w):
                return p.tensor_reduce(MK8[:, pi:pi + 1], S2[:, :, 0:w],
                                       mybir.AxisListType.XY, AL.max,
                                       apply_absolute_value=True)
            s.on("vector", v_mx)
            vv = s.notify("vector", tag + "a_v")
            s.wait("scalar", tag + "a_v", vv)
        s.barrier()
        def v_g(p):
            return p.tensor_reduce(SCL2[:, 2:3], MK8[:, 0:4],
                                   mybir.AxisListType.X, AL.max)
        s.on("vector", v_g)
        s.barrier()
        s.dma("gpsimd", ROWB[0:1, 0:128], SCL2[:, 2:3])
        s.barrier()
        def v_g2(p):
            return p.tensor_reduce(SCL2[0:1, 2:3], ROWB[0:1, 0:128],
                                   mybir.AxisListType.X, AL.max)
        s.on("vector", v_g2)
        s.barrier()
        qin = self.dram("qin" + tag, [1, 1], F32)
        qout = self.dram("qout" + tag, [1, 1], F32)
        s.dma("gpsimd", qin[0:1, 0:1], SCL2[0:1, 2:3])
        s.dma_wait("gpsimd")
        def g_ar(p):
            return p.collective_compute(
                "AllReduce", AL.max,
                replica_groups=[[0, 1, 2, 3, 4, 5, 6, 7]],
                ins=[self.dint["qin" + tag].ap().opt()],
                outs=[self.dint["qout" + tag].ap().opt()])
        s.on("gpsimd", g_ar, kind="data")
        s.notify("gpsimd", "ccd")
        s.wait("gpsimd", "ccd")
        s.dma("gpsimd", SCL2[0:1, 2:3], self.dint["qout" + tag].ap()[0:1, 0:1])
        s.barrier()
        def v_q(p):
            p.tensor_scalar_max(SCL2[0:1, 2:3], SCL2[0:1, 2:3], 1e-8)
            p.drain()
            p.tensor_scalar_mul(SCL2[0:1, 3:4], SCL2[0:1, 2:3], 1.0 / 127.0)
            p.drain()
            p.reciprocal(SCL2[0:1, 2:3], SCL2[0:1, 2:3])
            p.drain()
            p.tensor_scalar_mul(SCL2[0:1, 2:3], SCL2[0:1, 2:3], 127.0)
            p.drain()
            return p.tensor_tensor(SCL2[0:1, 3:4], SCL2[0:1, 3:4],
                                   AL1[0:1, 0:1], AL.mult)
        s.on("vector", v_q)
        s.barrier()
        s.dma("gpsimd", self.dint["scal2"][0:1, 2:4], SCL2[0:1, 2:4])
        s.dma_wait("gpsimd")
        s.dma("gpsimd", SCL2[:, 0:1],
              self.dint["scal2"].ap()[0:1, 2:3].partition_broadcast(128))
        s.dma("gpsimd", SCL2[:, 1:2],
              self.dint["scal2"].ap()[0:1, 3:4].partition_broadcast(128))
        s.barrier()
        def v_mg(p):
            return p.memset(SCL2[:, 7:8], MAGIC)
        s.on("vector", v_mg)
        s.barrier()
        for pi, (t0, t1) in enumerate(NP4):
            w = t1 - t0
            def v_a(p, t0=t0, w=w):
                r = None
                for c in range(NCH):
                    p.tensor_tensor(SC[:, 0:w], x1t(c)[:, t0:t0 + w],
                                    self.sb["BCA"][:, HALO + t0:HALO + t0 + w],
                                    AL.subtract)
                    p.drain()
                    r = p.tensor_tensor(S2[:, c, 0:w], SC[:, 0:w],
                                        self.sb["BCB"][:, HALO + t0:HALO + t0 + w],
                                        AL.mult)
                    p.drain()
                return r
            s.on("vector", v_a)
            vv = s.notify("vector", tag + "b_v")
            s.wait("scalar", tag + "b_v", vv)
            def s_a(p, w=w):
                r = None
                for c in range(NCH):
                    p.activation(S2[:, c, 0:w], S2[:, c, 0:w], AF.Identity,
                                 bias=BET[:, c:c + 1], scale=GAM[:, c:c + 1])
                    self.snop(p)
                    r = p.activation(S2[:, c, 0:w], S2[:, c, 0:w], AF.Identity,
                                     bias=SCL2[:, 7:8], scale=SCL2[:, 0:1])
                    self.snop(p)
                return r
            s.on("scalar", s_a)
            sv = s.notify("scalar", tag + "b_s")
            s.wait("vector", tag + "b_s", sv)
            def v_i(p, t0=t0, w=w):
                r = None
                for c in range(NCH):
                    r = p.tensor_scalar_sub(INT[:, c, t0:t0 + w],
                                            S2[:, c, 0:w], MAGIC)
                return r
            s.on("vector", v_i)
            vv = s.notify("vector", tag + "b_v")
            s.wait("scalar", tag + "b_v", vv)
        s.barrier()
        if self.dbg_on:
            self.dbg("ints" + tag, INT[:, :, :], [128, NCH, NT], BF16)
        for oc in range(NCH):
            def t_f(p, oc=oc):
                r = None
                for (t0, t1) in NP4:
                    for kc in range(NCH):
                        r = p.matmul(PSW[:, t0:t1],
                                     WE[:, kc, 128 * oc:128 * oc + 128],
                                     INT[:, kc, t0:t1], start=(kc == 0),
                                     stop=(kc == NCH - 1))
                return r
            s.on("tensor", t_f)
            tv = s.notify("tensor", tag + "o_t")
            s.wait("vector", tag + "o_t", tv)
            def v_f(p, oc=oc):
                r = None
                for (t0, t1) in NP4:
                    r = p.scalar_tensor_tensor(x1t(oc)[:, t0:t1], PSW[:, t0:t1],
                                               SCL2[:, 1:2], x1t(oc)[:, t0:t1],
                                               AL.mult, AL.add)
                return r
            s.on("vector", v_f)
            vv = s.notify("vector", tag + "o_v")
            s.wait("tensor", tag + "o_v", vv)
        s.barrier()

    def stage_attn_b(self):
        s = self.s
        E = self.ERA
        NP4 = self.NP4
        X1a, X1b = self.sb["X1a"], self.sb["X1b"]
        def x1t(c):
            return (X1a[:, c, :] if c < 4 else X1b[:, c - 4, :])
        PSR, PSW = self.ps["PSR"], self.ps["PSW"]
        HS1, HS2 = self.sb["HS1"], self.sb["HS2"]
        XN3 = self.at("XN3", [128, NCH, NT + 4], BF16, E + 32768)
        PB = self.at("PB", [64, NT + 4], F32, E + 119680)
        DD = self.at("DD", [64, NT], F32, E + 127904)
        FE = self.at("FE", [16, NT], BF16, E + 136096)

        BW = self.at("BW", [128, NCH, 4, 64], BF16, E + 156640)
        CW3 = self.at("CW3", [128, NCH, 4], F32, E + 160736)
        DTWB = self.at("DTWB", [128, NCH, 16], BF16, E + 160864)
        TRWB = self.at("TRWB", [16, 64], BF16, E + 161120)
        DTB = self.at("DTB", [16, 1], F32, E + 161248)
        INB = self.at("INB", [64, 1], F32, E + 161280)
        OUTB = self.at("OUTB", [128, NCH], F32, E + 161312)
        HM = self.at("HM", [128, 1], F32, E + 161344)
        ZAG = self.at("ZAG", [64, 1], F32, E + 161376)
        WF = self.sb["WF32"]
        last_t = None
        for (t0, t1) in NP4:
            w = t1 - t0
            if last_t is not None:
                s.wait("vector", "b3r_t", last_t)
                s.wait("scalar", "b3r_t", last_t)
            def v_cast(p, t0=t0, w=w):
                r = None
                for c in range(NCH):
                    r = p.tensor_copy(HS1[:, c, 0:w], x1t(c)[:, t0:t0 + w])
                return r
            s.on("vector", v_cast)
            vv = s.notify("vector", "b3r_v")
            def s_sq(p, t0=t0, w=w):
                r = None
                for c in range(NCH):
                    r = p.activation(HS2[:, c, 0:w], x1t(c)[:, t0:t0 + w],
                                     AF.Square)
                return r
            s.on("scalar", s_sq)
            sv = s.notify("scalar", "b3r_s")
            s.wait("tensor", "b3r_v", vv)
            s.wait("tensor", "b3r_s", sv)
            def t_r(p, w=w):
                r = None
                for c in range(NCH):
                    p.matmul(PSR[0:2, 0:w], self.sb["EJ"][:, 0:2],
                             HS1[:, c, 0:w], start=(c == 0), stop=False)
                for c in range(NCH):
                    r = p.matmul(PSR[0:2, 0:w], self.sb["EJ"][:, 16:18],
                                 HS2[:, c, 0:w], start=False,
                                 stop=(c == NCH - 1))
                return r
            s.on("tensor", t_r)
            tv = s.notify("tensor", "b3r_t")
            last_t = tv
            s.wait("vector", "b3r_t", tv)
            def v_cp(p, t0=t0, w=w):
                return p.tensor_copy(self.sb["ROWT"][0:2, HALO + t0:HALO + t0 + w],
                                     PSR[0:2, 0:w])
            s.on("vector", v_cp)
            vv2 = s.notify("vector", "b3r_v")
            s.wait("tensor", "b3r_v", vv2)
        def vz(p):
            p.memset(self.sb["ROWT"][0:2, 0:HALO], 0.0)
            return p.memset(self.sb["ROWT"][0:2, HALO + NT:ROWW], 0.0)
        s.on("vector", vz)
        s.barrier()
        self.rows_to_rp([0, 1])
        s.barrier()
        RP = self.sb["RP"]
        mu, msq, var = RP[:, 0:17], RP[:, 17:34], RP[:, 34:51]
        def v_math(p):
            p.tensor_scalar_mul(mu, mu, 1.0 / D)
            p.tensor_scalar_mul(msq, msq, 1.0 / D)
            p.drain()
            p.tensor_tensor(var, mu, mu, AL.mult)
            p.drain()
            p.tensor_tensor(var, msq, var, AL.subtract)
            p.drain()
            p.tensor_scalar_add(var, var, LN_EPS)
            p.drain()
            return p.reciprocal(var, var)
        s.on("vector", v_math)
        vv = s.notify("vector", "b3m_v")
        s.wait("scalar", "b3m_v", vv)
        def s_math(p):
            return p.activation(var, var, AF.Sqrt)
        s.on("scalar", s_math)
        s.notify("scalar", "b3m_s")
        s.barrier()
        self.rp_to_rows([0, 2], [0, 1])
        s.barrier()
        self.bcast_row(0, self.sb["BCA"])
        self.bcast_row(1, self.sb["BCB"])
        s.barrier()
        SC = self.sb["SC"]
        for (t0, t1) in NP4:
            w = t1 - t0
            def v_x(p, t0=t0, w=w):
                r = None
                for c in range(NCH):
                    p.tensor_tensor(SC[:, 0:w], x1t(c)[:, t0:t0 + w],
                                    self.sb["BCA"][:, HALO + t0:HALO + t0 + w],
                                    AL.subtract)
                    p.drain()
                    r = p.tensor_tensor(XN3[:, c, 4 + t0:4 + t0 + w], SC[:, 0:w],
                                        self.sb["BCB"][:, HALO + t0:HALO + t0 + w],
                                        AL.mult)
                    p.drain()
                return r
            s.on("vector", v_x)
            s.notify("vector", "b3x_v")
        s.barrier()
        self.dram("bhx", [NCH, 128, 4], BF16)
        self.dram("bhy", [2, NCH, 128, 4], BF16)
        s.dma("gpsimd", self.dint["bhx"].ap().rearrange("s p t -> p s t"),
              XN3[:, :, NT:NT + 4])
        s.dma("sync", HM[:, 0:1], self.di["hmask"].ap()[0:1, 0:1]
              .partition_broadcast(128))
        s.barrier()
        def cc_h(p):
            return p.collective_compute(
                "AllGather", AL.bypass,
                replica_groups=[[0, 1], [2, 3], [4, 5], [6, 7]],
                ins=[self.dint["bhx"].ap().opt()],
                outs=[self.dint["bhy"].ap().opt()])
        s.on("gpsimd", cc_h, kind="data")
        s.notify("gpsimd", "ccd")
        s.wait("gpsimd", "ccd")
        s.dma("gpsimd", XN3[:, :, 0:4],
              self.dint["bhy"].ap()[0, :, :, :].rearrange("s p t -> p s t"))
        s.barrier()
        def v_hm(p):
            return p.tensor_scalar_mul(XN3[:, :, 0:4], XN3[:, :, 0:4],
                                       HM[:, 0:1])
        s.on("vector", v_hm)
        s.barrier()
        if self.dbg_on:
            self.dbg("xn3", XN3[:, :, :], [128, NCH, NT + 4], BF16)
        s.dma("sync", CW3[:, :, :], self.di["bconvw"].ap())
        s.dma("sync", DTB[:, :], self.di["dtb"][:, :])
        s.dma("sync", INB[:, :], self.di["inbc"][:, :])
        s.dma("sync", OUTB[:, :], self.di["outbc"][:, :])
        s.dma("sync", WF[0:16, 0:64], self.di["trwt"][:, :])
        s.dma_wait("vector", src="sync")
        def v_trw(p):
            return p.tensor_copy(TRWB[0:16, :], WF[0:16, 0:64])
        s.on("vector", v_trw)
        vv = s.notify("vector", "bw_v")
        s.wait("sync", "bw_v", vv)
        s.dma("sync", WF[:, 0:128].rearrange("p (c r) -> p c r", r=16),
              self.di["dtwt"].ap().rearrange("(c p) r -> p c r", p=128))
        s.dma_wait("vector", src="sync")
        def v_dtw(p):
            return p.tensor_copy(DTWB[:, :, :],
                                 WF[:, 0:128].rearrange("p (c r) -> p c r", r=16))
        s.on("vector", v_dtw)
        vv = s.notify("vector", "bw_v")
        s.wait("sync", "bw_v", vv)
        for kc in range(NCH):
            s.dma("sync", WF[:, 0:64], self.di["inwt"][128 * kc:128 * (kc + 1), :])
            s.dma_wait("vector", src="sync")
            def v_bw(p, kc=kc):
                for j in range(3):
                    p.tensor_scalar_mul(BW[:, kc, j, :], WF[:, 0:64],
                                        CW3[:, kc, j:j + 1])
                p.tensor_scalar_add(SC[:, 0:1], CW3[:, kc, 3:4], 1.0)
                p.drain()
                return p.tensor_scalar_mul(BW[:, kc, 3, :], WF[:, 0:64],
                                           SC[:, 0:1])
            s.on("vector", v_bw)
            vv = s.notify("vector", "bw_v")
            s.wait("sync", "bw_v", vv)
        s.wait("tensor", "bw_v")
        s.barrier()
        for (t0, t1) in NP4:
            w = t1 - t0
            def t_fe(p, t0=t0, w=w):
                r = None
                for kc in range(NCH):
                    r = p.matmul(PSR[0:16, 0:w], DTWB[:, kc, :],
                                 XN3[:, kc, 4 + t0:4 + t0 + w],
                                 start=(kc == 0), stop=(kc == NCH - 1))
                return r
            s.on("tensor", t_fe)
            tv = s.notify("tensor", "bd_t")
            s.wait("scalar", "bd_t", tv)
            def s_fe(p, t0=t0, w=w):
                return p.activation(FE[0:16, t0:t0 + w], PSR[0:16, 0:w],
                                    AF.Tanh, bias=DTB[0:16, 0:1], scale=1.0)
            s.on("scalar", s_fe)
            sv = s.notify("scalar", "bd_s")
            s.wait("tensor", "bd_s", sv)
        s.barrier()
        for (t0, t1) in NP4:
            w = t1 - t0
            def t_d(p, t0=t0, w=w):
                return p.matmul(PSR[0:64, 0:w], TRWB[0:16, :],
                                FE[0:16, t0:t0 + w], start=True, stop=True)
            s.on("tensor", t_d)
            tv = s.notify("tensor", "bd_t")
            s.wait("vector", "bd_t", tv)
            def v_d(p, t0=t0, w=w):
                return p.tensor_copy(DD[0:64, t0:t0 + w], PSR[0:64, 0:w])
            s.on("vector", v_d)
            vv = s.notify("vector", "bd_v")
            s.wait("tensor", "bd_v", vv)
        s.barrier()
        PWP = [(0, 1)] + [(1 + 512 * i, 1 + 512 * (i + 1)) for i in range(4)]
        for (u0, u1) in PWP:
            w = u1 - u0
            def t_p(p, u0=u0, w=w):
                r = None
                first = True
                for kc in range(NCH):
                    for j in range(4):
                        r = p.matmul(PSW[0:64, 0:w], BW[:, kc, j, :],
                                     XN3[:, kc, j + u0:j + u0 + w],
                                     start=first, stop=(kc == NCH - 1 and j == 3))
                        first = False
                return r
            s.on("tensor", t_p)
            tv = s.notify("tensor", "bp_t")
            s.wait("scalar", "bp_t", tv)
            def s_p(p, u0=u0, w=w):
                return p.activation(PB[0:64, u0:u0 + w], PSW[0:64, 0:w],
                                    AF.Identity, bias=INB[0:64, 0:1], scale=1.0)
            s.on("scalar", s_p)
            sv = s.notify("scalar", "bp_s")
            s.wait("tensor", "bp_s", sv)
        s.barrier()
        def v_e(p):
            p.tensor_scalar_mul(PB[0:64, 0:1], PB[0:64, 0:1], HM[0:64, 0:1])
            p.drain()
            return p.tensor_tensor(DD[0:64, 0:NT], DD[0:64, 0:NT],
                                   PB[0:64, 0:NT], AL.add)
        s.on("vector", v_e)
        s.barrier()
        if self.dbg_on:
            self.dbg("bpb", PB[0:64, 0:NT + 1], [64, NT + 1], F32)
            self.dbg("be", DD[0:64, 0:NT], [64, NT], F32)
        ZP = self.at("ZP", [64, NT], F32, E + 140192)
        def v_z0(p):
            return p.memset(ZAG[0:64, 0:1], 0.0)
        s.on("vector", v_z0)
        vv = s.notify("vector", "bs_v")
        s.wait("scalar", "bs_v", vv)
        def s_scan1(p):
            r = None
            for t in range(NT):
                zin_ = ZAG[0:64, 0:1] if t == 0 else ZP[0:64, t - 1:t]
                r = p.activation(ZP[0:64, t:t + 1], zin_,
                                 AF.Silu, bias=DD[0:64, t:t + 1], scale=1.0)
                self.snop(p)
            return r
        s.on("scalar", s_scan1)
        s.notify("scalar", "bs_s")
        s.barrier()
        self.dram("zin", [1, 64], F32)
        self.dram("zout", [2, 64], F32)
        s.dma("gpsimd", self.dint["zin"][0:1, :], ZP[0:64, NT - 1:NT])
        s.barrier()
        def cc_z(p):
            return p.collective_compute(
                "AllGather", AL.bypass,
                replica_groups=[[0, 1], [2, 3], [4, 5], [6, 7]],
                ins=[self.dint["zin"].ap().opt()],
                outs=[self.dint["zout"].ap().opt()])
        s.on("gpsimd", cc_z, kind="data")
        s.notify("gpsimd", "ccd")
        s.wait("gpsimd", "ccd")
        s.dma("gpsimd", ZAG[0:64, 0:1], self.dint["zout"].ap()[0:1, :])
        s.barrier()
        def v_seed(p):
            return p.tensor_scalar_mul(ZAG[0:64, 0:1], ZAG[0:64, 0:1],
                                       HM[0:64, 0:1])
        s.on("vector", v_seed)
        vv = s.notify("vector", "bs_v")
        s.wait("scalar", "bs_v", vv)
        def s_scan2(p):
            r = None
            for t in range(NT):
                zin_ = ZAG[0:64, 0:1] if t == 0 else ZP[0:64, t - 1:t]
                r = p.activation(ZP[0:64, t:t + 1], zin_,
                                 AF.Silu, bias=DD[0:64, t:t + 1], scale=1.0)
                self.snop(p)
            return r
        s.on("scalar", s_scan2)
        s.notify("scalar", "bs_s")
        s.barrier()
        OWB = self.at("OWB", [64, 1024], BF16, E + 32768)
        SB3 = self.at("SB3", [64, NT], BF16, E + 34816)
        s.dma("sync", WF[0:64, 0:1024], self.di["outwt"][:, :])
        s.dma_wait("vector", src="sync")
        def v_ow(p):
            p.tensor_copy(OWB[0:64, :], WF[0:64, 0:1024])
            p.drain()
            return p.tensor_tensor(SB3[0:64, 0:NT], ZP[0:64, 0:NT],
                                   PB[0:64, 1:NT + 1], AL.add)
        s.on("vector", v_ow)
        s.barrier()
        if self.dbg_on:
            self.dbg("bst", SB3[0:64, 0:NT], [64, NT], BF16)
        for oc in range(NCH):
            def t_o(p, oc=oc):
                r = None
                for (t0, t1) in NP4:
                    r = p.matmul(PSW[:, t0:t1], OWB[0:64, 128 * oc:128 * oc + 128],
                                 SB3[0:64, t0:t1], start=True, stop=True)
                return r
            s.on("tensor", t_o)
            tv = s.notify("tensor", "bo_t")
            s.wait("vector", "bo_t", tv)
            def v_o(p, oc=oc):
                r = None
                for (t0, t1) in NP4:
                    r = p.scalar_tensor_tensor(x1t(oc)[:, t0:t1], PSW[:, t0:t1],
                                               OUTB[:, oc:oc + 1],
                                               x1t(oc)[:, t0:t1],
                                               AL.add, AL.add)
                return r
            s.on("vector", v_o)
            vv = s.notify("vector", "bo_v")
            s.wait("tensor", "bo_v", vv)
        s.barrier()
        if self.dbg_on:
            self.dbg("x3a", X1a[:, :, :], [128, 4, NT], F32)
            self.dbg("x3b", X1b[:, :, :], [128, 4, NT], F32)

    def stage_ffn2(self):
        self.ffn_apply("f2")


    def stage_store(self):
        s = self.s
        X1a, X1b = self.sb["X1a"], self.sb["X1b"]
        for c in range(NCH):
            t = X1a[:, c, :] if c < 4 else X1b[:, c - 4, :]
            s.dma("gpsimd", self.do["y"].ap().rearrange(
                "(c p) t -> p c t", p=128)[:, c, :], t)
        s.barrier()


def _stage_inputs(inputs):
    x = np.asarray(inputs["L0_attn_a_hidden_states"])
    import ml_dtypes
    bf = ml_dtypes.bfloat16
    eye = np.eye(128, dtype=np.float32)
    ej = np.zeros((128, 256), np.float32)
    for j in range(16):
        ej[:, 16 * j + j] = 1.0
    tri = np.triu(np.ones((128, 128), np.float32), 1)
    triincl = np.triu(np.ones((128, 128), np.float32), 0)
    masks = np.concatenate([tri, triincl, np.eye(128, dtype=np.float32),
                            np.tril(np.ones((128, 128), np.float32), -1)], 1)

    maps = []
    for c in range(8):
        b, h = c // 2, c % 2
        t0 = NT * h
        xf = np.zeros((D, NTH), np.float32)
        xf[:, HALO:] = x[b, t0:t0 + NT, :].T
        if h == 1:
            xf[:, 0:HALO] = x[b, t0 - HALO:t0, :].T
        wsm = np.zeros((D, 16), np.float32)
        wsm[:, 0:4] = np.asarray(inputs["L0_attn_a_Wb"])
        wsm[:, 4:8] = np.asarray(inputs["L0_attn_a_Wres"])
        cvw = np.zeros((D, 16), np.float32)
        cvw[:, 0:4] = np.asarray(inputs["L0_attn_a_wq_conv"])
        cvw[:, 4:8] = np.asarray(inputs["L0_attn_a_wk_conv"])
        cvw[:, 8:12] = np.asarray(inputs["L0_attn_a_wv_conv"])
        wl = np.asarray(inputs["L0_attn_a_fir_long"]).reshape(D, 64)
        ws = np.asarray(inputs["L0_attn_a_fir_short"]).reshape(D, 5)
        ii, oo = np.indices((128, 128))
        def toep(w, K, shift):
            j = ii - oo + (K - 1) - shift
            ok = (j >= 0) & (j < K)
            out = np.zeros((w.shape[0], 128, 128), np.float32)
            jj = np.clip(j, 0, K - 1)
            out = np.where(ok[None, :, :], w[:, jj], 0.0)
            return out
        mych = slice(128 * c, 128 * (c + 1))
        lng = np.concatenate([toep(wl[mych], 64, 0),
                              toep(wl[mych], 64, 128)], axis=2)
        sht = np.concatenate([toep(ws[mych], 5, 0),
                              toep(ws[mych], 5, 128)], axis=2)
        firt = np.stack([lng, sht], axis=1)
        wg2 = np.zeros((D, 8), np.float32)
        wg2[:, 0:4] = np.asarray(inputs["L0_attn_a_Wg2"])
        wog2 = np.zeros((512, 8), np.float32)
        wog2[:, 0:1] = np.asarray(inputs["L0_attn_a_Wog2"])
        vec16 = np.zeros((1, 16), np.float32)
        vec16[0, 0:4] = np.asarray(inputs["L0_attn_a_bg2"])
        vec16[0, 4] = np.asarray(inputs["L0_attn_a_bog2"])[0]
        vec16[0, 5] = np.asarray(inputs["L0_attn_a_logit_temp"])[0]
        vec16[0, 6:10] = np.asarray(inputs["L0_attn_a_bres"])
        vec16[0, 10:14] = np.asarray(inputs["L0_attn_a_conv_res_logit"])
        nw = np.asarray(inputs["L0_attn_a_norm_w"])
        nwc = np.zeros((128, NCH), np.float32)
        for cc in range(NCH):
            nwc[:, cc] = nw[(cc % 2) * 128:(cc % 2) * 128 + 128]
        m = {
            "x0": np.ascontiguousarray(xf),
            "firt": np.ascontiguousarray(firt).astype(bf),
            "wq": np.ascontiguousarray(np.asarray(inputs["L0_attn_a_Wq"])),
            "wk": np.ascontiguousarray(np.asarray(inputs["L0_attn_a_Wk"])),
            "wv": np.ascontiguousarray(np.asarray(inputs["L0_attn_a_Wv"])),
            "wo": np.ascontiguousarray(np.asarray(inputs["L0_attn_a_Wo"])),
            "ffnw": np.ascontiguousarray(np.asarray(inputs["L0_ffn_weight"]).T),
            "gamc": np.ascontiguousarray(
                np.asarray(inputs["L0_ffn_gamma"]).reshape(NCH, 128).T),
            "betc": np.ascontiguousarray(
                np.asarray(inputs["L0_ffn_beta"]).reshape(NCH, 128).T),
            "inwt": np.ascontiguousarray(np.asarray(inputs["L0_attn_b_in_w"]).T),
            "bconvw": np.ascontiguousarray(
                np.asarray(inputs["L0_attn_b_conv_w"]).reshape(NCH, 128, 4)
                .transpose(1, 0, 2)),
            "dtwt": np.ascontiguousarray(np.asarray(inputs["L0_attn_b_dt_w"]).T),
            "dtb": np.ascontiguousarray(
                np.asarray(inputs["L0_attn_b_dt_b"]).reshape(16, 1)),
            "trwt": np.ascontiguousarray(np.asarray(inputs["L0_attn_b_tr_w"]).T),
            "outwt": np.ascontiguousarray(np.asarray(inputs["L0_attn_b_out_w"]).T),
            "outbc": np.ascontiguousarray(
                np.asarray(inputs["L0_attn_b_out_b"]).reshape(NCH, 128).T),
            "inbc": np.ascontiguousarray(
                np.asarray(inputs["L0_attn_b_in_b"]).reshape(64, 1)),
            "hmask": np.array([[float(h)]], np.float32),
            "wg1": np.ascontiguousarray(np.asarray(inputs["L0_attn_a_Wg1"])),
            "wog1": np.ascontiguousarray(np.asarray(inputs["L0_attn_a_Wog1"])),
            "wg2b": np.ascontiguousarray(
                wg2.reshape(NCH, 128, 8).transpose(1, 0, 2)).astype(bf),
            "wog2b": np.ascontiguousarray(
                wog2.reshape(4, 128, 8).transpose(1, 0, 2)).astype(bf),
            "bg1c": np.ascontiguousarray(
                np.asarray(inputs["L0_attn_a_bg1"]).reshape(NCH, 128).T),
            "bog1c": np.ascontiguousarray(
                np.asarray(inputs["L0_attn_a_bog1"]).reshape(4, 128).T),
            "normwc": nwc,
            "vec16": vec16,
            "bg2row": np.tile(np.asarray(inputs["L0_attn_a_bg2"]),
                              16).reshape(1, 64).astype(np.float32),
            "wsmallb": np.ascontiguousarray(
                wsm.reshape(NCH, 128, 16).transpose(1, 0, 2)).astype(bf),
            "convw": np.ascontiguousarray(
                cvw.reshape(NCH, 128, 16).transpose(1, 0, 2)),
            "eye128": eye,
            "ejpat": ej.astype(bf),
            "masks": masks.astype(bf),
        }
        maps.append(m)
    return maps


def kernel(**inputs):
    mk = MK(stage=_STAGE, dbg=_DBG).build()
    maps = _stage_inputs(inputs)
    res = run_bass_kernel_spmd(mk.nc, maps, core_ids=list(range(8)))
    if _STAGE < 11:
        return res
    outs = [r["y"] for r in res.results]
    y = np.empty((B, L, D), np.float32)
    for c in range(8):
        b, h = c // 2, c % 2
        y[b, NT * h:NT * (h + 1), :] = outs[c].T
    return y


# version 08
# speedup vs baseline: 1.0060x; 1.0060x over previous
import numpy as np
import concourse.bass as bass
import concourse.mybir as mybir
from concourse.bass_utils import run_bass_kernel_spmd

F32 = mybir.dt.float32
BF16 = mybir.dt.bfloat16
AF = mybir.ActivationFunctionType
AL = mybir.AluOpType

B, L, D = 4, 4096, 1024
NT = 2048
HALO = 8
NTH = NT + HALO
NCH = 8
H = 4
DK = 256
MAGIC = 12582912.0
LN_EPS = 1e-5
L2_EPS = 1e-6
RMS_EPS = 1e-5
CHUNK = 128
NCK = L // CHUNK
PIECES = [(0, HALO)] + [(HALO + 512 * i, HALO + 512 * (i + 1)) for i in range(4)]
RPIECES = PIECES[1:]
ROWW = 2176
ENGINES = ("sync", "tensor", "vector", "scalar", "gpsimd")

_STAGE = 99
_DCK = 99999
_DBG = False


class Sched:
    def __init__(self, nc):
        self.nc = nc
        self.ops = {e: [] for e in ENGINES}
        self._sems = {}
        self.cnt = {}
        self.ph = self.sem("ph")
        self.nbar = 0
        self.dsem = {e: self.sem("dma_" + e) for e in ENGINES}
        self.dcnt = {e: 0 for e in ENGINES}
        self.dwait = {e: 0 for e in ENGINES}

    def canon(self, name):
        if len(name) > 2 and name[-2] == "_" and name[-1] in "vst":
            return "S_" + name[-1]
        return name

    def sem(self, name):
        name = self.canon(name)
        if name not in self._sems:
            self._sems[name] = self.nc.semaphore(name).__enter__()
            self.cnt[name] = 0
        return self._sems[name]

    def on(self, eng, fn, kind="data"):
        self.ops[eng].append([fn, kind, False])

    def raw(self, eng, method, *a, **kw):
        self.on(eng, lambda p: getattr(p, method)(*a, **kw))

    def carrier(self, eng, p):
        ca = self.carr
        if eng == "vector":
            return p.memset(ca[0:1, 0:1], 0.0)
        if eng == "scalar":
            return p.memzero(ca[0:1, 1:2])
        if eng == "gpsimd":
            return p.memset(ca[0:1, 2:3], 0.0)
        if eng == "tensor":
            return p.ldweights(self.eyeb_carr)
        return p.nop()

    def notify(self, eng, name, by=1):
        sm = self.sem(name)
        name = self.canon(name)
        self.cnt[name] += by
        ent = self.ops[eng][-1] if self.ops[eng] else None
        if ent is not None and ent[1] == "data" and not ent[2]:
            prev = ent[0]
            def fn(p, prev=prev, sm=sm, by=by):
                r = prev(p)
                (r if r is not None else self.carrier(eng, p)).then_inc(sm, by)
            ent[0], ent[2] = fn, True
        else:
            def fn(p, sm=sm, by=by):
                self.carrier(eng, p).then_inc(sm, by)
            self.on(eng, fn, kind="seq")
        return self.cnt[name]

    def sep(self, eng):
        self.on(eng, lambda p: p.drain(), kind="seq")

    def wait(self, eng, name, val=None):
        sm = self.sem(name)
        v = self.cnt[self.canon(name)] if val is None else val
        self.on(eng, lambda p: p.wait_ge(sm, v), kind="seq")
        return v

    def dma(self, eng, out, in_):
        def fn(p, _o=out, _i=in_):
            o = _o() if callable(_o) else _o
            i = _i() if callable(_i) else _i
            return p.dma_start(out=o, in_=i).then_inc(self.dsem[eng], 16)
        self.on(eng, fn, kind="seq")
        self.dcnt[eng] += 16
        return self.dcnt[eng]

    def dma_wait(self, eng, upto=None, src=None):
        q = src or eng
        val = self.dcnt[q] if upto is None else upto
        if q == eng:
            if val > self.dwait[q]:
                self.dwait[q] = val
                self.on(eng, lambda p, v=val: p.wait_ge(self.dsem[q], v),
                        kind="seq")
        else:
            self.on(eng, lambda p, v=val: p.wait_ge(self.dsem[q], v),
                    kind="seq")
        return val

    def barrier(self):
        self.nbar += 1
        tgt = 5 * self.nbar
        for e in ENGINES:
            self.dma_wait(e)
            self.notify(e, "ph")
            self.wait(e, "ph", tgt)
        assert self.cnt["ph"] == tgt

    def emit(self):
        with self.nc.Block() as block:
            def mk(eng_name):
                def body(proxy):
                    for ent in self.ops[eng_name]:
                        ent[0](proxy)
                body.__name__ = "stream_" + eng_name
                return body
            block.sync(mk("sync"))
            block.tensor(mk("tensor"))
            block.vector(mk("vector"))
            block.scalar(mk("scalar"))
            block.gpsimd(mk("gpsimd"))


class MK:
    def __init__(self, stage=99, dbg=False, sim=False):
        self.sim_mode = sim
        self.nc = bass.Bass("TRN2", target_bir_lowering=False)
        self.snop = ((lambda p: p.drain()) if sim else
                     (lambda p: p.nop(cycle_cnt=128, nofuse=True)))
        self.s = Sched(self.nc)
        self.stage = stage
        self.dbg_on = dbg
        self.di, self.do, self.dint, self.sb, self.ps = {}, {}, {}, {}, {}
        self.sb_off = {}
        self.cursor = {}
        self.dbg_list = []
        self.dt_done = {}

    def inp(self, name, shape, dtype=F32):
        self.di[name] = self.nc.dram_tensor(name, list(shape), dtype,
                                            kind="ExternalInput")
        return self.di[name]

    def outp(self, name, shape, dtype=F32):
        self.do[name] = self.nc.dram_tensor(name, list(shape), dtype,
                                            kind="ExternalOutput")
        return self.do[name]

    def dram(self, name, shape, dtype=F32):
        self.dint[name] = self.nc.dram_tensor(name, list(shape), dtype)
        return self.dint[name]

    def at(self, name, shape, dtype, off):
        nbytes = int(np.prod(shape[1:])) * mybir.dt.size(dtype)
        assert off % 32 == 0, (name, off)
        assert off >= 24576 and off + nbytes <= 224 * 1024, (name, off, nbytes)
        self.sb[name] = self.nc.alloc_sbuf_tensor_at(name, list(shape), dtype,
                                                     offset=off)
        self.sb_off[name] = (off, off + nbytes)
        return self.sb[name]

    def fix(self, name, shape, dtype):
        off = self.cursor.setdefault("fix", 24576)
        nbytes = int(np.prod(shape[1:])) * mybir.dt.size(dtype)
        nbytes = (nbytes + 31) // 32 * 32
        self.cursor["fix"] = off + nbytes
        return self.at(name, shape, dtype, off)

    def psum(self, name, shape, dtype=F32):
        self.ps[name] = self.nc.alloc_psum_tensor(name, list(shape), dtype)
        return self.ps[name]

    def dbg(self, name, ap, shape, dtype=F32):
        if not self.dbg_on:
            return
        t = self.outp("dbg_" + name, shape, dtype)
        self.s.barrier()
        self.s.dma("gpsimd", t[:, :] if len(shape) == 2 else t[:], ap)
        self.s.barrier()

    ERA = None

    def build(self):
        s = self.s
        self.declare_common()
        self.load_constants()
        self.stage_ln1()
        if self.stage >= 2:
            self.stage_proj_qkv()
        if self.stage >= 3:
            self.stage_delta_prep()
        if self.stage >= 5:
            self.stage_fir()
        if self.stage >= 4:
            self.stage_delta()
        if self.stage >= 4:
            self.stage_delta_finish()
        if self.stage >= 8:
            self.ffn_ternarize()
        if self.stage >= 5:
            self.stage_fir_finish()
        if self.stage >= 6:
            self.stage_gates()
        if self.stage >= 7:
            self.stage_mix_out()
        if self.stage >= 8:
            self.stage_ffn1()
        if self.stage >= 9:
            self.stage_attn_b()
        if self.stage >= 10:
            self.stage_ffn2()
        if self.stage >= 11:
            self.stage_store()
        s.barrier()
        s.emit()
        return self

    def declare_common(self):
        self.inp("x0", [D, NTH])
        self.inp("eye128", [128, 128])
        self.inp("ejpat", [128, 256], BF16)
        self.inp("masks", [128, 512], BF16)
        self.inp("wq", [D, D])
        self.inp("wk", [D, D])
        self.inp("wv", [D, D])
        self.inp("wsmallb", [128, NCH, 16], BF16)
        self.inp("convw", [128, NCH, 16])
        self.inp("firt", [128, 2, 128, 256], BF16)
        self.inp("wo", [D, D])
        self.inp("ffnw", [D, D])
        self.inp("gamc", [128, NCH])
        self.inp("inwt", [D, 64])
        self.inp("bconvw", [128, NCH, 4])
        self.inp("dtwt", [D, 16])
        self.inp("dtb", [16, 1])
        self.inp("trwt", [16, 64])
        self.inp("outwt", [64, D])
        self.inp("outbc", [128, NCH])
        self.inp("inbc", [64, 1])
        self.inp("hmask", [1, 1])
        self.inp("betc", [128, NCH])
        self.inp("wg1", [1040, D])
        self.inp("wg2b", [128, NCH, 8], BF16)
        self.inp("wog1", [1032, 512])
        self.inp("wog2b", [128, 4, 8], BF16)
        self.inp("bg1c", [128, NCH])
        self.inp("bog1c", [128, 4])
        self.inp("normwc", [128, NCH])
        self.inp("vec16", [1, 16])
        self.inp("bg2row", [1, 64])
        self.outp("y", [D, NT])

    def load_constants(self):
        s = self.s
        s.carr = self.fix("CARR", [128, 8], F32)
        s.eyeb_carr = None
        self.fix("EYE", [128, 128], F32)
        self.fix("MASKS", [128, 512], BF16)
        self.fix("EJ", [128, 256], BF16)
        self.fix("ROWT", [16, ROWW], F32)
        self.fix("ROWP", [8, ROWW], F32)
        self.fix("RP", [128, 512], F32)
        self.fix("BCA", [128, NTH], F32)
        self.fix("BCB", [128, NTH], F32)
        self.fix("SC", [128, 1056], F32)
        self.fix("PRSV", [128, 32], F32)
        self.ERA = (self.cursor["fix"] + 127) // 128 * 128
        self.psum("PSW", [128, 2048])
        self.psum("PSR", [128, 512])
        self.psum("PST", [128, 1024], BF16)
        self.psum("PSS", [128, 1024])
        s.dma("sync", self.sb["EYE"][:, :], self.di["eye128"][:, :])
        s.dma("sync", self.sb["MASKS"][:, :], self.di["masks"][:, :])
        s.dma("sync", self.sb["EJ"][:, :], self.di["ejpat"][:, :])
        def v_init0(p):
            p.memset(self.sb["ROWT"][0:16, 0:ROWW], 0.0)
            return p.memset(self.sb["RP"][:, :], 0.0)
        s.on("vector", v_init0)
        s.eyeb_carr = self.sb["MASKS"][:, 256:384]
        self.dyx = {}
        def g_base(p):
            cid = p.compute_val(self.nc.partition_id())
            base = p.snap((cid % 2) * 512)
            for i in range(2):
                for kc in range(2):
                    self.dyx[(i, kc)] = base + (256 * i + 128 * kc)
            self.dyx["tok"] = base * 4
            self.dyx["rank"] = p.snap(cid % 8)
            self.dyx["oth"] = p.snap((cid + 1) % 2)
            self.dyx["o512"] = p.snap(((cid + 1) % 2) * 512)
            self.dyx["otok"] = p.snap(((cid + 1) % 2) * 2048)
            return None
        s.on("gpsimd", g_base, kind="seq")
        self.dys = {}
        def s_base(p):
            cid = p.compute_val(self.nc.partition_id())
            base = p.snap((cid % 2) * 512)
            self.dys[(0, 0)] = base
            self.dys["tok"] = base * 4
            self.dys["oth"] = p.snap((cid + 1) % 2)
            self.dys["o512"] = p.snap(((cid + 1) % 2) * 512)
            self.dys["otok"] = p.snap(((cid + 1) % 2) * 2048)
            return None
        s.on("sync", s_base, kind="seq")
        s.barrier()

    def row_reduce(self, jobs, prep=None):
        s = self.s
        PSR = self.ps["PSR"]
        n = 1 + max(j[0] for j in jobs)
        assert n <= 16
        last_t = None
        for (t0, t1) in PIECES:
            w = t1 - t0
            if prep is not None:
                pres = prep(t0, w)
                vals = {}
                for eng, fn in pres:
                    if last_t is not None:
                        s.wait(eng, "rr_t", last_t)
                    s.on(eng, fn)
                    vals[eng] = s.notify(eng, "rr_" + eng[0])
                for eng, v in vals.items():
                    s.wait("tensor", "rr_" + eng[0], v)
            def t_fn(p, t0=t0, w=w):
                pairs = []
                for (pr, mf) in jobs:
                    for c in range(NCH):
                        m = mf(c, t0, w)
                        if m is not None:
                            pairs.append((pr, m))
                r = None
                for i, (pr, m) in enumerate(pairs):
                    r = p.matmul(PSR[0:n, 0:w],
                                 self.sb["EJ"][:, 16 * pr:16 * pr + n],
                                 m, start=(i == 0), stop=(i == len(pairs) - 1))
                return r
            s.on("tensor", t_fn)
            tv = s.notify("tensor", "rr_t")
            last_t = tv
            s.wait("vector", "rr_t", tv)
            def v_fn(p, t0=t0, w=w, n=n):
                return p.tensor_copy(self.sb["ROWT"][0:n, t0:t0 + w],
                                     PSR[0:n, 0:w])
            s.on("vector", v_fn)
            vt = s.notify("vector", "rr_v")
            s.wait("tensor", "rr_v", vt)
        def vz(p, n=n):
            p.memset(self.sb["ROWT"][0:n, NTH:ROWW], 0.0)
        s.on("vector", vz)

    def rows_to_rp(self, rows, rp_slot0=0):
        s = self.s
        for i, r in enumerate(rows):
            sl = rp_slot0 + i
            s.dma("gpsimd", self.sb["RP"][:, 17 * sl:17 * sl + 17],
                  self.sb["ROWT"][r:r + 1, 0:ROWW])

    def rp_to_rows(self, slots, rows):
        s = self.s
        for sl, r in zip(slots, rows):
            s.dma("gpsimd", self.sb["ROWT"][r:r + 1, 0:ROWW],
                  self.sb["RP"][:, 17 * sl:17 * sl + 17])

    def bcast_row(self, row, dst, width=NTH, tile=None):
        s = self.s
        tile = self.sb["ROWT"] if tile is None else tile
        rb = self.dint.get("rowbounce")
        if rb is None:
            rb = self.dram("rowbounce", [16, ROWW], F32)
        s.dma("gpsimd", rb[row:row + 1, 0:width],
              tile[row:row + 1, 0:width])
        s.dma_wait("gpsimd")
        def dst_ap():
            d = dst() if callable(dst) else dst
            return d[:, 0:width]
        s.dma("gpsimd", dst_ap,
              lambda: rb[row:row + 1, 0:width].partition_broadcast(128))

    def stage_ln1(self):
        s = self.s
        X = self.at("X", [128, NCH, NTH], F32, self.ERA)
        XN = self.at("XN", [128, NCH, NTH], BF16, self.ERA + 65792)
        self.at("SCB1", [128, NCH, 512], BF16, self.ERA + 98688)
        self.at("SCB2", [128, NCH, 512], BF16, self.ERA + 106880)
        s.dma("sync", X[:, :, :],
              self.di["x0"].ap().rearrange("(c p) t -> p c t", p=128))
        s.barrier()
        SCB1, SCB2 = self.sb["SCB1"], self.sb["SCB2"]
        def prep(t0, w):
            def v_fn(p, t0=t0, w=w):
                return p.tensor_copy(SCB1[:, :, 0:w], X[:, :, t0:t0 + w])
            def s_fn(p, t0=t0, w=w):
                return p.activation(SCB2[:, :, 0:w], X[:, :, t0:t0 + w],
                                    AF.Square)
            return [("vector", v_fn), ("scalar", s_fn)]
        self.row_reduce([(0, lambda c, t0, w: SCB1[:, c, 0:w]),
                         (1, lambda c, t0, w: SCB2[:, c, 0:w])], prep=prep)
        s.barrier()
        self.rows_to_rp([0, 1])
        s.barrier()
        RP = self.sb["RP"]
        mu, msq, var = RP[:, 0:17], RP[:, 17:34], RP[:, 34:51]
        t2 = RP[:, 51:68]
        def v_math(p):
            p.tensor_scalar_mul(mu, mu, 1.0 / D)
            p.tensor_scalar_mul(msq, msq, 1.0 / D)
            p.drain()
            p.tensor_tensor(t2, mu, mu, AL.mult)
            p.drain()
            p.tensor_tensor(var, msq, t2, AL.subtract)
            p.drain()
            p.tensor_scalar_add(var, var, LN_EPS)
            p.drain()
            return p.reciprocal(var, var)
        s.on("vector", v_math)
        sv = s.notify("vector", "ln_v")
        s.wait("scalar", "ln_v", sv)
        def sc_math(p):
            return p.activation(var, var, AF.Sqrt)
        s.on("scalar", sc_math)
        s.notify("scalar", "ln_s")
        s.barrier()
        self.rp_to_rows([0, 2], [0, 1])
        s.barrier()
        self.bcast_row(0, self.sb["BCA"])
        self.bcast_row(1, self.sb["BCB"])
        s.barrier()
        SC = self.sb["SC"]
        def v_apply(p):
            r = None
            for c in range(NCH):
                for (t0, t1) in ((0, 1024), (1024, NTH)):
                    w = t1 - t0
                    p.tensor_tensor(SC[:, 0:w], X[:, c, t0:t1],
                                    self.sb["BCA"][:, t0:t1], AL.subtract)
                    p.drain()
                    r = p.tensor_tensor(XN[:, c, t0:t1], SC[:, 0:w],
                                        self.sb["BCB"][:, t0:t1], AL.mult)
                    p.drain()
            return r
        s.on("vector", v_apply)
        s.barrier()
        self.dbg("xn", XN[:, :, :], [128, NCH, NTH], BF16)

    def proj_big(self, w_dram, out_fn, oc_range=range(NCH), halo=True,
                 extra=None, xn=None):
        s = self.s
        WB, WF = self.sb["WB"], self.sb["WF32"]
        XN = self.sb["XN"] if xn is None else xn
        PSW, PSR = self.ps["PSW"], self.ps["PSR"]
        for kc in range(NCH):
            s.dma("sync", WF[:, 0:1024],
                  w_dram[128 * kc:128 * (kc + 1), :])
            s.dma_wait("vector", src="sync")
            def vc(p, kc=kc):
                return p.tensor_copy(WB[:, kc, :], WF[:, 0:1024])
            s.on("vector", vc)
            vv = s.notify("vector", "pw_v")
            if kc < NCH - 1:
                s.wait("sync", "pw_v", vv)
        s.wait("tensor", "pw_v")
        pieces = PIECES if halo else RPIECES
        for oc in oc_range:
            def t_fn(p, oc=oc):
                r = None
                for kc in range(NCH):
                    lhs = WB[:, kc, 128 * oc:128 * (oc + 1)]
                    for pi, (t0, t1) in enumerate(pieces):
                        w = t1 - t0
                        if w <= 16:
                            dst = PSR[0:128, 384:384 + w]
                        else:
                            dst = PSW[:, (t0 - HALO):(t0 - HALO) + w]
                        r = p.matmul(dst, lhs, XN[:, kc, t0:t1],
                                     start=(kc == 0), stop=(kc == NCH - 1))
                return r
            s.on("tensor", t_fn)
            tv = s.notify("tensor", "pw_t")
            out_fn(oc, tv)

    def conv4_silu(self, taps_base, src, dst, sign_extra=None):
        s = self.s
        PSW = self.ps["PSW"]
        EYEB = self.sb["EYEB"]
        CW = self.sb["CONVW"]
        DIAG = self.sb["DIAG"]
        for c in range(NCH):
            def v_diag(p, c=c):
                r = None
                for j in range(4):
                    r = p.tensor_scalar(DIAG[:, j, :], EYEB[:, :],
                                        CW[:, c, taps_base + j:taps_base + j + 1],
                                        None, AL.mult)
                return r
            s.on("vector", v_diag)
            dv = s.notify("vector", "cv_v")
            s.wait("tensor", "cv_v", dv)
            def t_fn(p, c=c):
                r = None
                for pi, (t0, t1) in enumerate(RPIECES):
                    w = t1 - t0
                    for j in range(4):
                        sh = t0 - 3 + j
                        r = p.matmul(PSW[:, (t0 - HALO):(t0 - HALO) + w],
                                     DIAG[:, j, :], src[:, c, sh:sh + w],
                                     start=(j == 0), stop=(j == 3))
                return r
            s.on("tensor", t_fn)
            tv = s.notify("tensor", "cv_t")
            s.wait("scalar", "cv_t", tv)
            def sc_fn(p, c=c):
                r = None
                for (t0, t1) in RPIECES:
                    w = t1 - t0
                    r = p.activation(dst[:, c, t0:t1],
                                     PSW[:, (t0 - HALO):(t0 - HALO) + w],
                                     AF.Silu)
                return r
            s.on("scalar", sc_fn)
            sv = s.notify("scalar", "cv_s")
            s.wait("tensor", "cv_s", sv)
            s.wait("vector", "cv_s", sv)

    def stage_proj_qkv(self):
        s = self.s
        E = self.ERA
        QL = self.at("QL", [128, NCH, NTH], BF16, E)
        TIN = self.at("TIN", [128, NCH, NTH], BF16, E + 32896)
        WB = self.at("WB", [128, NCH, 1024], BF16, E + 98688)
        WF32 = self.at("WF32", [128, 1024], F32, E + 115072)
        BB4 = self.at("BB4", [128, 4, NTH], BF16, E + 119168)
        SCL = self.at("SCL", [128, 4, NTH], BF16, E + 135616)
        EYEB = self.at("EYEB", [128, 128], BF16, E + 152064)
        CONVW = self.at("CONVW", [128, NCH, 16], F32, E + 152320)
        WSM = self.at("WSM", [128, NCH, 16], BF16, E + 152832)
        DIAG = self.at("DIAG", [128, 4, 128], BF16, E + 153088)
        self.at("L2SC", [128, NCH, 512], BF16, E + 154112)
        self.dram("xin", [4, D, NT], BF16)
        self.dram("xout", [2, 4, D, NT], BF16)
        self.dram("fir_in", [NCH, 128, NT], BF16)
        self.dram("vdir", [NCH, 128, NT], BF16)
        s.dma("sync", EYEB[:, :], self.di["masks"][:, 256:384])
        s.dma("sync", CONVW[:, :, :], self.di["convw"].ap())
        s.dma("sync", WSM[:, :, :], self.di["wsmallb"].ap())
        s.barrier()
        def v_init(p):
            p.memset(QL[:, :, 0:HALO], 0.0)
            p.memset(self.sb["ROWP"][0:8, 0:ROWW], 0.0)
            return p.memset(TIN[:, :, 0:HALO], 0.0)
        s.on("vector", v_init)
        s.barrier()

        def evac_plain(oc, tv):
            s.wait("scalar", "pw_t", tv)
            def sc(p, oc=oc):
                r = None
                for (t0, t1) in PIECES:
                    w = t1 - t0
                    srcp = (self.ps["PSR"][0:128, 384:384 + w] if w <= 16 else
                            self.ps["PSW"][:, (t0 - HALO):(t0 - HALO) + w])
                    r = p.copy(QL[:, oc, t0:t1], srcp)
                return r
            s.on("scalar", sc)
            sv = s.notify("scalar", "pw_s")
            s.wait("tensor", "pw_s", sv)

        self.proj_big(self.di["wv"].ap(), evac_plain)
        PSR = self.ps["PSR"]
        for (t0, t1) in RPIECES:
            w = t1 - t0
            def t_fn(p, t0=t0, w=w):
                r = None
                for kc in range(NCH):
                    r = p.matmul(PSR[0:16, 0:w], WSM[:, kc, :],
                                 self.sb["XN"][:, kc, t0:t0 + w],
                                 start=(kc == 0), stop=(kc == NCH - 1))
                return r
            s.on("tensor", t_fn)
            tv = s.notify("tensor", "sm_t")
            s.wait("vector", "sm_t", tv)
            def v_fn(p, t0=t0, w=w):
                return p.tensor_copy(self.sb["ROWP"][0:8, t0:t0 + w],
                                     PSR[0:8, 0:w])
            s.on("vector", v_fn)
            vv = s.notify("vector", "sm_v")
            s.wait("tensor", "sm_v", vv)
        s.barrier()
        def sc_beta(p):
            return p.activation(self.sb["ROWP"][0:4, 0:NTH], self.sb["ROWP"][0:4, 0:NTH],
                                AF.Sigmoid)
        s.on("scalar", sc_beta)
        s.notify("scalar", "sm_s")
        s.barrier()
        for h4 in range(4):
            self.bcast_row(h4, self.sb["BCA"], tile=self.sb["ROWP"])
            s.barrier()
            def vc(p, h4=h4):
                return p.tensor_copy(BB4[:, h4, :], self.sb["BCA"][:, :])
            s.on("vector", vc)
            s.barrier()
        self.conv4_silu(8, QL, TIN)
        s.barrier()
        for c in range(NCH):
            s.dma("gpsimd", self.dint["fir_in"][c, :, :], TIN[:, c, HALO:NTH])
            s.dma("gpsimd", self.dint["vdir"][c, :, :], TIN[:, c, HALO:NTH])
        def v_vb(p):
            r = None
            for c in range(NCH):
                r = p.tensor_tensor(QL[:, c, :], TIN[:, c, :],
                                    BB4[:, c // 2, :], AL.mult)
            return r
        s.on("vector", v_vb)
        s.barrier()
        for c in range(NCH):
            s.dma("gpsimd", self.dint["xin"][3, 128 * c:128 * (c + 1), :],
                  QL[:, c, HALO:NTH])
        s.barrier()

        self.proj_big(self.di["wq"].ap(), evac_plain)
        s.barrier()
        self.conv4_silu(0, QL, TIN)
        s.barrier()
        self.l2rows(TIN)
        s.barrier()
        for h4 in range(4):
            self.bcast_row(h4, self.sb["BCA"])
            s.barrier()
            def vc(p, h4=h4):
                return p.tensor_copy(SCL[:, h4, :], self.sb["BCA"][:, :])
            s.on("vector", vc)
            s.barrier()
        def v_qn(p):
            r = None
            for c in range(NCH):
                r = p.tensor_tensor(QL[:, c, :], TIN[:, c, :],
                                    SCL[:, c // 2, :], AL.mult)
            return r
        s.on("vector", v_qn)
        s.barrier()
        for c in range(NCH):
            s.dma("gpsimd", self.dint["xin"][0, 128 * c:128 * (c + 1), :],
                  QL[:, c, HALO:NTH])
        s.barrier()

        self.proj_big(self.di["wk"].ap(), evac_plain)
        s.barrier()
        self.conv4_silu(4, QL, TIN)
        s.barrier()
        self.l2rows(TIN)
        s.barrier()
        for h4 in range(4):
            self.bcast_row(h4, self.sb["BCA"])
            s.barrier()
            def vc(p, h4=h4):
                return p.tensor_copy(SCL[:, h4, :], self.sb["BCA"][:, :])
            s.on("vector", vc)
            s.barrier()
        def v_kn(p):
            r = None
            for c in range(NCH):
                r = p.tensor_tensor(TIN[:, c, :], TIN[:, c, :],
                                    SCL[:, c // 2, :], AL.mult)
            return r
        s.on("vector", v_kn)
        s.barrier()
        for c in range(NCH):
            s.dma("gpsimd", self.dint["xin"][1, 128 * c:128 * (c + 1), :],
                  TIN[:, c, HALO:NTH])
        def v_kb(p):
            r = None
            for c in range(NCH):
                r = p.tensor_tensor(QL[:, c, :], TIN[:, c, :],
                                    BB4[:, c // 2, :], AL.mult)
            return r
        s.on("vector", v_kb)
        s.barrier()
        for c in range(NCH):
            s.dma("gpsimd", self.dint["xin"][2, 128 * c:128 * (c + 1), :],
                  QL[:, c, HALO:NTH])
        s.barrier()
        if self.dbg_on:
            self.dbg("qn", self.dint["xin"][0, :, :], [D, NT], BF16)
            self.dbg("kn", self.dint["xin"][1, :, :], [D, NT], BF16)
            self.dbg("kb", self.dint["xin"][2, :, :], [D, NT], BF16)
            self.dbg("vb", self.dint["xin"][3, :, :], [D, NT], BF16)
            self.dbg("vin", self.dint["fir_in"].ap(), [NCH, 128, NT], BF16)
            self.dbg("beta", self.sb["ROWP"][0:4, 0:NTH], [4, NTH], F32)

    def l2rows(self, TIN):
        s = self.s
        PSR = self.ps["PSR"]
        SCB2 = self.sb["L2SC"]
        last_t = None
        for (t0, t1) in RPIECES:
            w = t1 - t0
            if last_t is not None:
                s.wait("vector", "l2_t", last_t)
            def v_sq(p, t0=t0, w=w):
                return p.tensor_tensor(SCB2[:, :, 0:w], TIN[:, :, t0:t0 + w],
                                       TIN[:, :, t0:t0 + w], AL.mult)
            s.on("vector", v_sq)
            vv = s.notify("vector", "l2_v")
            s.wait("tensor", "l2_v", vv)
            def t_fn(p, t0=t0, w=w):
                r = None
                for c in range(NCH):
                    r = p.matmul(PSR[0:4, 0:w],
                                 self.sb["EJ"][:, 16 * (c // 2):16 * (c // 2) + 4],
                                 SCB2[:, c, 0:w],
                                 start=(c == 0), stop=(c == NCH - 1))
                return r
            s.on("tensor", t_fn)
            tv = s.notify("tensor", "l2_t")
            last_t = tv
            s.wait("vector", "l2_t", tv)
            def v_cp(p, t0=t0, w=w):
                return p.tensor_copy(self.sb["ROWT"][0:4, t0:t0 + w],
                                     PSR[0:4, 0:w])
            s.on("vector", v_cp)
            vv2 = s.notify("vector", "l2_v")
            s.wait("tensor", "l2_v", vv2)
        s.barrier()
        def vz2(p):
            return p.memset(self.sb["ROWT"][0:4, 0:HALO], 0.0)
        s.on("vector", vz2)
        s.barrier()
        self.rows_to_rp([0, 1, 2, 3])
        s.barrier()
        RP = self.sb["RP"]
        def v_m(p):
            p.tensor_scalar_add(RP[:, 0:68], RP[:, 0:68], L2_EPS)
            p.drain()
            return p.reciprocal(RP[:, 0:68], RP[:, 0:68])
        s.on("vector", v_m)
        vv = s.notify("vector", "l2m_v")
        s.wait("scalar", "l2m_v", vv)
        def s_m(p):
            return p.activation(RP[:, 0:68], RP[:, 0:68], AF.Sqrt)
        s.on("scalar", s_m)
        s.notify("scalar", "l2m_s")
        s.barrier()
        self.rp_to_rows([0, 1, 2, 3], [0, 1, 2, 3])
        s.barrier()

    def stage_delta_prep(self):
        s = self.s
        self.dram("xsend", [4, 512, NT], BF16)
        self.dram("xrecv", [2, 4, 512, NT], BF16)
        def g_xs(p):
            src = self.dint["xin"].ap()[:, bass.ds(self.dys["o512"], 512), :]
            return p.dma_start(out=self.dint["xsend"].ap(),
                               in_=src).then_inc(s.dsem["sync"], 16)
        s.on("sync", g_xs, kind="seq")
        s.dcnt["sync"] += 16
        s.dma_wait("gpsimd", src="sync")
        def cc(p):
            return p.collective_compute(
                "AllGather", AL.bypass,
                replica_groups=[[0, 1], [2, 3], [4, 5], [6, 7]],
                ins=[self.dint["xsend"].ap().opt()],
                outs=[self.dint["xrecv"].ap().opt()])
        s.on("gpsimd", cc, kind="data")
        self.ccv_xin = s.notify("gpsimd", "ccd")
        self.dram("fir_x8", [8, NCH, 128, NT], BF16)
        def cc2(p):
            return p.collective_compute(
                "AllGather", AL.bypass,
                replica_groups=[list(range(8))],
                ins=[self.dint["fir_in"].ap().opt()],
                outs=[self.dint["fir_x8"].ap().opt()])
        s.on("gpsimd", cc2, kind="data")
        self.ccv_firx8 = s.notify("gpsimd", "ccd")
        s.wait("sync", "ccd", self.ccv_xin)

    def stage_delta(self):
        s, nc = self.s, self.nc
        E = self.ERA
        MB = self.sb["MASKS"]
        M_SU, M_UI, M_EYE, M_SL = (MB[:, 0:128], MB[:, 128:256],
                                   MB[:, 256:384], MB[:, 384:512])
        DOUT = self.at("DOUT", [128, 4, 4096], BF16, E)
        RING = self.at("RING", [128, 4, 4, 2, 128], BF16, E + 98688)
        W16 = self.at("W16", [128, 16, 128], BF16, E + 106880)
        St = self.at("St", [128, 2, 512], BF16, E + 110976)
        SF = self.at("SF", [128, 2, 512], F32, E + 119232)
        U2T = self.at("U2T", [128, 2, 256], BF16, E + 113024)
        KTM = self.at("KTM", [128, 2, 256], BF16, E + 114048)
        YPP = self.at("YPP", [128, 2, 2, 256], BF16, E + 115072)
        OTM = self.at("OTM", [128, 2, 256], BF16, E + 117120)
        KBSS = self.at("KBSS", [128, 256], F32, E + 118144)
        self.dram("dret_in", [512, 4096], BF16)
        self.dram("dret_out", [2, 512, 4096], BF16)
        self.dram("xmy", [4, 512, 4096], BF16)
        PSW, PSR, PSS = self.ps["PSW"], self.ps["PSR"], self.ps["PSS"]
        psSQ = PSW[:, 0:128]
        psAP = PSW[:, 512:768]
        psKBS = PSW[:, 1024:1280]
        psT = self.ps["PST"][:, 0:256]
        psKT = self.ps["PST"][:, 512:768]
        psGA = PSR[:, 0:384]

        dyx = self.dyx

        xmy = self.dint["xmy"]
        def g_st1(p):
            dd = self.dys
            src = self.dint["xin"].ap()[:, bass.ds(dd[(0, 0)], 512), :]
            return p.dma_start(out=xmy.ap()[:, :, bass.ds(dd["tok"], 2048)],
                               in_=src).then_inc(s.dsem["sync"], 16)
        s.on("sync", g_st1, kind="seq")
        s.dcnt["sync"] += 16
        def g_st2(p):
            dd = self.dys
            src = self.dint["xrecv"].ap()[bass.ds(dd["oth"], 1), :, :, :]
            src = src.rearrange("s a c t -> (s a) c t")
            return p.dma_start(out=xmy.ap()[:, :, bass.ds(dd["otok"], 2048)],
                               in_=src).then_inc(s.dsem["sync"], 16)
        s.on("sync", g_st2, kind="seq")
        s.dcnt["sync"] += 16
        s.barrier()

        def load_chunk2(i, ck, slot):
            sm = s.sem("rs%d" % slot)
            for kc in range(2):
                def fn(p, i=i, ck=ck, kc=kc, slot=slot, sm=sm):
                    src = xmy.ap()[:, 256 * i + 128 * kc:256 * i + 128 * kc + 128,
                                   128 * ck:128 * ck + 128]
                    src = src.transpose([1, 0, 2])
                    return p.dma_start(out=RING[:, slot, :, kc, :],
                                       in_=src).then_inc(sm, 16)
                s.on("sync", fn, kind="seq")
                s.cnt["rs%d" % slot] += 16
            return s.cnt["rs%d" % slot]

        NH = 2
        loaded = {}
        ring_of = {}
        nslot = [0]
        def ensure_loaded(i, ck):
            key = (i, ck)
            if key not in loaded:
                slot = nslot[0] % 4
                nslot[0] += 1
                ring_of[key] = slot
                loaded[key] = load_chunk2(i, ck, slot)
            return loaded[key], ring_of[key]

        order = [(ck, i) for ck in range(NCK) for i in range(NH)]
        order = order[:_DCK]
        for (ck, i) in order[:3]:
            ensure_loaded(i, ck)

        for oi, (ck, i) in enumerate(order):
            dv, slot = ensure_loaded(i, ck)
            if oi + 3 < len(order):
                cka, ia = order[oi + 3]
                if oi >= 1:
                    s.wait("sync", "dt", self.dt_done.get(order[oi - 1], 0))
                ensure_loaded(ia, cka)
            Q = lambda kc, slot=slot: RING[:, slot, 0, kc, :]
            K = lambda kc, slot=slot: RING[:, slot, 1, kc, :]
            KB = lambda kc, slot=slot: RING[:, slot, 2, kc, :]
            VB = lambda kc, slot=slot: RING[:, slot, 3, kc, :]
            first = (ck == 0)

            s.wait("tensor", "rs%d" % slot, dv)
            def t1(p, i=i, ck=ck, slot=slot, first=first, Q=Q, K=K, KB=KB, VB=VB):
                r = None
                for kc in range(2):
                    p.matmul(psGA[0:128, 0:128], K(kc), Q(kc), start=(kc == 0),
                             stop=(kc == 1))
                for kc in range(2):
                    p.matmul(psGA[0:128, 128:256], K(kc), KB(kc),
                             start=(kc == 0), stop=(kc == 1))
                for kc in range(2):
                    p.matmul(psGA[0:128, 256:384], KB(kc), K(kc),
                             start=(kc == 0), stop=(kc == 1))
                for kc in range(2):
                    p.transpose(psT[:, 128 * kc:128 * kc + 128], VB(kc),
                                self.sb["EYEB"][:, :])
                for kc in range(2):
                    r = p.transpose(psKT[:, 128 * kc:128 * kc + 128],
                                    K(kc), self.sb["EYEB"][:, :])
                if not first:
                    for kc in range(2):
                        r = p.matmul(psKBS, KB(kc),
                                     St[:, i, :][:, 256 * kc:256 * kc + 256],
                                     start=(kc == 0), stop=(kc == 1))
                return r
            s.on("tensor", t1)
            tv = s.notify("tensor", "dt")
            s.wait("vector", "dt", tv)
            def v1(p, i=i, first=first):
                p.tensor_tensor(W16[:, 0, :], psGA[0:128, 0:128], M_UI, AL.mult)
                p.tensor_tensor(W16[:, 1, :], psGA[0:128, 128:256], M_SU, AL.mult)
                p.tensor_tensor(W16[:, 2, :], psGA[0:128, 256:384], M_SL, AL.mult)
                p.tensor_copy(KTM[:, i, :], psKT)
                if first:
                    return p.tensor_copy(YPP[:, i, 0, :], psT[0:128, 0:256])
                p.tensor_copy(KBSS[:, :], psKBS)
                p.drain()
                return p.tensor_tensor(YPP[:, i, 0, :], psT[0:128, 0:256],
                                       KBSS[:, :], AL.subtract)
            s.on("vector", v1)
            vv = s.notify("vector", "dv")
            s.wait("tensor", "dv", vv)
            xs = {1: 1}
            xts = {1: 2}
            lv = [2, 4, 8, 16, 32, 64]
            for li, lev in enumerate(lv):
                xs[lev] = 3 + li
                if lev != 64:
                    xts[lev] = 9 + li
            for li, lev in enumerate(lv):
                prev = lev // 2
                def t_sq(p, lev=lev, prev=prev):
                    r = p.matmul(psSQ, W16[:, xts[prev], :], W16[:, xs[prev], :],
                                 start=True, stop=True)
                    if lev != 64:
                        r = p.matmul(PSW[:, 128:256], W16[:, xs[prev], :],
                                     W16[:, xts[prev], :], start=True,
                                     stop=True)
                    return r
                s.on("tensor", t_sq)
                tv = s.notify("tensor", "dt")
                s.wait("vector", "dt", tv)
                def v_sq(p, lev=lev):
                    p.tensor_copy(W16[:, xs[lev], :], psSQ)
                    if lev != 64:
                        return p.tensor_copy(W16[:, xts[lev], :], PSW[:, 128:256])
                    return None
                s.on("vector", v_sq)
                vv = s.notify("vector", "dv")
                s.wait("tensor", "dv", vv)
            pp = 0
            for li, lev in enumerate([64, 32, 16, 8, 4, 2]):
                def t_ap(p, lev=lev, pp=pp, i=i):
                    return p.matmul(psAP, W16[:, xs[lev], :], YPP[:, i, pp, :],
                                    start=True, stop=True)
                s.on("tensor", t_ap)
                tv = s.notify("tensor", "dt")
                s.wait("vector", "dt", tv)
                def v_ap(p, pp=pp, i=i):
                    return p.tensor_tensor(YPP[:, i, 1 - pp, :], YPP[:, i, pp, :],
                                           psAP, AL.add)
                s.on("vector", v_ap)
                vv = s.notify("vector", "dv")
                s.wait("tensor", "dv", vv)
                pp = 1 - pp
            def t_lf(p, pp=pp, i=i):
                return p.matmul(psAP, W16[:, xs[1], :], YPP[:, i, pp, :],
                                start=True, stop=True)
            s.on("tensor", t_lf)
            tv = s.notify("tensor", "dt")
            s.wait("vector", "dt", tv)
            def v_lf(p, pp=pp, i=i):
                return p.tensor_tensor(U2T[:, i, :], YPP[:, i, pp, :], psAP,
                                       AL.subtract)
            s.on("vector", v_lf)
            vv = s.notify("vector", "dv")
            s.wait("tensor", "dv", vv)
            def t_o(p, i=i, ck=ck, slot=slot, first=first, Q=Q):
                st = (True, False) if not first else None
                if not first:
                    for kc in range(2):
                        p.matmul(psKBS, Q(kc), St[:, i, :][:, 256 * kc:256 * kc + 256],
                                 start=(kc == 0), stop=False)
                    p.matmul(psKBS, W16[:, 0, :], U2T[:, i, :], start=False,
                             stop=True)
                else:
                    p.matmul(psKBS, W16[:, 0, :], U2T[:, i, :], start=True,
                             stop=True)
                for kc in range(2):
                    r = p.matmul(PSS[:, 512 * i + 256 * kc:512 * i + 256 * kc + 256],
                                 KTM[:, i, :][:, 128 * kc:128 * kc + 128],
                                 U2T[:, i, :],
                                 start=(kc == 0), stop=(kc == 1))
                return r
            s.on("tensor", t_o)
            tv = s.notify("tensor", "dt")
            s.wait("vector", "dt", tv)
            def v_o(p, i=i, first=first):
                p.tensor_copy(OTM[:, i, :], psKBS)
                if first:
                    p.tensor_copy(SF[:, i, :], PSS[:, 512 * i:512 * i + 512])
                else:
                    p.tensor_tensor(SF[:, i, :], SF[:, i, :],
                                    PSS[:, 512 * i:512 * i + 512], AL.add)
                p.drain()
                return p.tensor_copy(St[:, i, :], SF[:, i, :])
            s.on("vector", v_o)
            vv = s.notify("vector", "dv")
            s.wait("tensor", "dv", vv)
            def t_ot(p, i=i):
                r = None
                for kc in range(2):
                    r = p.transpose(psT[:, 128 * kc:128 * kc + 128],
                                    OTM[:, i, :][:, 128 * kc:128 * kc + 128],
                                    self.sb["EYEB"][:, :])
                return r
            s.on("tensor", t_ot)
            tv = s.notify("tensor", "dt")
            s.wait("vector", "dt", tv)
            def v_ot(p, i=i, ck=ck):
                r = None
                for kc in range(2):
                    r = p.tensor_copy(DOUT[:, 2 * i + kc, 128 * ck:128 * ck + 128],
                                      psT[:, 128 * kc:128 * kc + 128])
                return r
            s.on("vector", v_ot)
            vv = s.notify("vector", "dv")
            s.wait("tensor", "dv", vv)
            self.dt_done[(ck, i)] = s.cnt["dt"]
        s.barrier()
        for c in range(4):
            s.dma("gpsimd", self.dint["dret_in"][128 * c:128 * (c + 1), :],
                  DOUT[:, c, :])
        s.dma_wait("gpsimd")
        def cc(p):
            return p.collective_compute(
                "AllGather", AL.bypass,
                replica_groups=[[0, 1], [2, 3], [4, 5], [6, 7]],
                ins=[self.dint["dret_in"].ap().opt()],
                outs=[self.dint["dret_out"].ap().opt()])
        s.on("gpsimd", cc, kind="data")
        self.ccv_dret = s.notify("gpsimd", "ccd")

    def stage_delta_finish(self):
        s = self.s
        E = self.ERA
        dyx = self.dyx
        s.wait("gpsimd", "ccd", self.ccv_dret)
        DOUTM = self.at("DOUTM", [128, NCH, NT], BF16, E + 32768)
        for sl in range(2):
            def g_read(p, sl=sl):
                src = self.dint["dret_out"].ap()[sl, :, bass.ds(dyx["tok"], NT)]
                src = src.rearrange("(c p) t -> p c t", p=128)
                return p.dma_start(out=DOUTM[:, 4 * sl:4 * sl + 4, :],
                                   in_=src).then_inc(s.dsem["gpsimd"], 16)
            s.on("gpsimd", g_read, kind="seq")
            s.dcnt["gpsimd"] += 16
        s.barrier()
        if self.dbg_on:
            self.dbg("dout", DOUTM[:, :, :], [128, NCH, NT], BF16)

    def stage_fir(self):
        s = self.s
        E = self.ERA
        self.dram("fir_rin", [NCH, 2, 128, NT], BF16)
        self.dram("fir_r8", [8, NCH, 2, 128, NT], BF16)
        s.wait("gpsimd", "ccd", self.ccv_firx8)
        VT = self.at("VT", [128, 128, 128], BF16, E)
        VCH = self.at("VCH", [128, NCH, NT], BF16, E + 98688)
        YT = self.at("YT", [128, 128, 128], BF16, E + 98688)
        FIRW = self.at("FIRW", [128, 2, 256], BF16, E + 131456)
        STG = self.at("STG", [128, 16, 128], BF16, E + 132480)
        PST, PSW = self.ps["PST"], self.ps["PSW"]
        def g_vch(p):
            src_ = self.dint["fir_x8"].ap()[:, bass.ds(self.dyx["rank"], 1), :, :]
            src_ = src_.rearrange("s o p t -> p (s o) t")
            return p.dma_start(out=VCH[:, :, :], in_=src_).then_inc(
                s.dsem["gpsimd"], 16)
        s.on("gpsimd", g_vch, kind="seq")
        s.dcnt["gpsimd"] += 16
        s.barrier()
        for grp in range(32):
            def t_tr(p, grp=grp):
                r = None
                for k4 in range(4):
                    gi = 4 * grp + k4
                    b4, m = gi // 32, gi % 32
                    src = VCH[:, 2 * b4 + m // 16,
                              128 * (m % 16):128 * (m % 16) + 128]
                    r = p.transpose(PST[:, 128 * k4:128 * k4 + 128], src,
                                    self.sb["EYEB"][:, :])
                return r
            s.on("tensor", t_tr)
            tv = s.notify("tensor", "fi_t")
            s.wait("vector", "fi_t", tv)
            def v_tr(p, grp=grp):
                r = None
                for k4 in range(4):
                    gi = 4 * grp + k4
                    r = p.tensor_copy(VT[:, :, gi], PST[:, 128 * k4:128 * k4 + 128])
                return r
            s.on("vector", v_tr)
            vv = s.notify("vector", "fi_v")
            s.wait("tensor", "fi_v", vv)
        s.barrier()
        lastv = {}
        for pss in range(2):
            for c in range(128):
                buf = c % 2
                if (pss * 128 + c) >= 1:
                    s.wait("sync", "fi_t", lastv[(pss * 128 + c) - 1])
                s.dma("sync", FIRW[:, buf, :],
                      self.di["firt"].ap()[c, pss, :, :])
                s.dma_wait("tensor", src="sync")
                def t_f(p, c=c, buf=buf):
                    p.matmul(PSW[:, 0:128], FIRW[:, buf, 0:128], VT[:, c, :],
                             start=True, stop=False)
                    r = None
                    for b4 in range(4):
                        r = p.matmul(PSW[:, 32 * b4 + 1:32 * b4 + 32],
                                     FIRW[:, buf, 128:256],
                                     VT[:, c, 32 * b4:32 * b4 + 31],
                                     start=False, stop=(b4 == 3))
                    return r
                s.on("tensor", t_f)
                tv = s.notify("tensor", "fi_t")
                lastv[pss * 128 + c] = tv
                s.wait("vector", "fi_t", tv)
                def v_f(p, c=c):
                    return p.tensor_copy(YT[:, c, :], PSW[:, 0:128])
                s.on("vector", v_f)
                vv = s.notify("vector", "fi_v")
                s.wait("tensor", "fi_v", vv)
            s.barrier()
            for sl in range(NCH):
                for grp in range(4):
                    def t_tb(p, sl=sl, grp=grp):
                        r = None
                        for k4 in range(4):
                            m16 = 4 * grp + k4
                            b4, m = sl // 2, 16 * (sl % 2) + m16
                            gi = 32 * b4 + m
                            r = p.transpose(PST[:, 128 * k4:128 * k4 + 128],
                                            YT[:, :, gi], self.sb["EYEB"][:, :])
                        return r
                    s.on("tensor", t_tb)
                    tv = s.notify("tensor", "fi_t")
                    s.wait("vector", "fi_t", tv)
                    def v_tb(p, grp=grp):
                        r = None
                        for k4 in range(4):
                            m16 = 4 * grp + k4
                            r = p.tensor_copy(STG[:, m16, :],
                                              PST[:, 128 * k4:128 * k4 + 128])
                        return r
                    s.on("vector", v_tb)
                    vv = s.notify("vector", "fi_v")
                    s.wait("tensor", "fi_v", vv)
                s.wait("gpsimd", "fi_v", s.cnt[s.canon("fi_v")])
                s.dma("gpsimd", self.dint["fir_rin"][sl, pss, :, :],
                      STG[:, :, :])
                dvv = s.dcnt["gpsimd"]
                s.dma_wait("vector", upto=dvv, src="gpsimd")
        s.barrier()
        def cc2(p):
            return p.collective_compute(
                "AllGather", AL.bypass,
                replica_groups=[list(range(8))],
                ins=[self.dint["fir_rin"].ap().opt()],
                outs=[self.dint["fir_r8"].ap().opt()])
        s.on("gpsimd", cc2, kind="data")
        self.ccv_firr8 = s.notify("gpsimd", "ccd")

    def stage_fir_finish(self):
        s = self.s
        s.wait("gpsimd", "ccd", self.ccv_firr8)
        self.dram("lsll", [2, NCH, 128, NT], BF16)
        for pss in range(2):
            def g_r(p, pss=pss):
                src_ = self.dint["fir_r8"].ap()[:, bass.ds(self.dyx["rank"], 1),
                                                pss, :, :]
                src_ = src_.rearrange("s o p t -> (s o) p t")
                return p.dma_start(out=self.dint["lsll"].ap()[1 - pss, :, :, :],
                                   in_=src_).then_inc(s.dsem["gpsimd"], 16)
            s.on("gpsimd", g_r, kind="seq")
            s.dcnt["gpsimd"] += 16
        s.barrier()
        if self.dbg_on:
            self.dbg("ls", self.dint["lsll"].ap()[0, :, :, :], [NCH, 128, NT], BF16)
            self.dbg("ll", self.dint["lsll"].ap()[1, :, :, :], [NCH, 128, NT], BF16)

    def stage_gates(self):
        s = self.s
        E = self.ERA
        NP4 = [(0, 512), (512, 1024), (1024, 1536), (1536, 2048)]
        self.NP4 = NP4
        DOUTM = self.sb["DOUTM"]
        PSR = self.ps["PSR"]
        STATR = self.at("STATR", [16, ROWW], F32, E + 131456)
        OGR = self.at("OGR", [8, ROWW], F32, E + 140160)
        VEC = self.at("VEC", [1, 16], F32, E + 148864)
        RT1 = self.at("RT1", [128, 8], F32, E + 148928)
        W9 = self.at("W9", [16, 1024], BF16, E + 149728)
        BG1 = self.at("BG1", [128, NCH], F32, E + 151776)
        WG2 = self.at("WG2", [128, NCH, 8], BF16, E + 151808)
        STATB = self.at("STATB", [16, ROWW], BF16, E + 151936)
        STR = self.at("STR", [128, 2, NCH, 512], BF16, E + 98688)
        self.at("HS1", [128, NCH, 512], BF16, E + 115072)
        self.at("HS2", [128, NCH, 512], BF16, E + 123264)
        self.dram("prow", [8, NT], F32)
        s.dma("sync", VEC[:, :], self.di["vec16"][:, :])
        s.barrier()
        def v_zero(p):
            p.memset(STATR[0:16, 0:ROWW], 0.0)
            return p.memset(OGR[0:8, 0:ROWW], 0.0)
        s.on("vector", v_zero)
        s.barrier()

        self.head_stats(DOUTM, 8, STATR, None, "hsdo")
        self.head_stats(None, 0, STATR, ("lsll", 0), "hsls")
        self.head_stats(None, 4, STATR, ("lsll", 1), "hsll")
        self.head_stats(None, 12, STATR, ("vdir", None), "hsvd")
        if self.dbg_on:
            self.dbg("statr", STATR[0:16, 0:ROWW], [16, ROWW], F32)

        GI = self.at("GI", [128, NCH, NT], BF16, E)
        def v_statb(p):
            return p.tensor_copy(STATB[0:16, 0:ROWW], STATR[0:16, 0:ROWW])
        s.on("vector", v_statb)
        s.barrier()
        WB, WF = self.sb["WB"], self.sb["WF32"]
        for kc in range(NCH):
            s.dma("sync", WF[:, 0:1024], self.di["wg1"][128 * kc:128 * (kc + 1), :])
            s.dma_wait("vector", src="sync")
            def vc(p, kc=kc):
                return p.tensor_copy(WB[:, kc, :], WF[:, 0:1024])
            s.on("vector", vc)
            vv = s.notify("vector", "pw_v")
            s.wait("sync", "pw_v", vv)
        s.dma("sync", WF[0:16, 0:1024], self.di["wg1"][1024:1040, :])
        s.dma_wait("vector", src="sync")
        def vc9(p):
            return p.tensor_copy(W9[0:16, :], WF[0:16, 0:1024])
        s.on("vector", vc9)
        vv9 = s.notify("vector", "pw_v")
        s.wait("tensor", "pw_v")
        s.wait("sync", "pw_v", vv9)
        s.dma("sync", BG1[:, :], self.di["bg1c"][:, :])
        s.dma("sync", WG2[:, :, :], self.di["wg2b"].ap())
        s.barrier()
        PSW = self.ps["PSW"]
        for oc in range(NCH):
            def t_g(p, oc=oc):
                r = None
                for (t0, t1) in NP4:
                    for kc in range(NCH):
                        p.matmul(PSW[:, t0:t1], WB[:, kc, 128 * oc:128 * oc + 128],
                                 self.sb["XN"][:, kc, HALO + t0:HALO + t1],
                                 start=(kc == 0), stop=False)
                    r = p.matmul(PSW[:, t0:t1], W9[0:16, 128 * oc:128 * oc + 128],
                                 STATB[0:16, HALO + t0:HALO + t1],
                                 start=False, stop=True)
                return r
            s.on("tensor", t_g)
            tv = s.notify("tensor", "pw_t")
            s.wait("scalar", "pw_t", tv)
            def s_g(p, oc=oc):
                r = None
                for (t0, t1) in NP4:
                    r = p.activation(GI[:, oc, t0:t1], PSW[:, t0:t1], AF.Gelu,
                                     bias=BG1[:, oc:oc + 1], scale=1.0)
                return r
            s.on("scalar", s_g)
            sv = s.notify("scalar", "pw_s")
            s.wait("tensor", "pw_s", sv)
        s.barrier()
        for (t0, t1) in NP4:
            w = t1 - t0
            def t_l(p, t0=t0, w=w):
                r = None
                for kc in range(NCH):
                    r = p.matmul(PSR[0:8, 0:w], WG2[:, kc, :], GI[:, kc, t0:t0 + w],
                                 start=(kc == 0), stop=(kc == NCH - 1))
                return r
            s.on("tensor", t_l)
            tv = s.notify("tensor", "lg_t")
            s.wait("vector", "lg_t", tv)
            def v_l(p, t0=t0, w=w):
                return p.tensor_copy(self.sb["ROWT"][0:8, HALO + t0:HALO + t0 + w],
                                     PSR[0:8, 0:w])
            s.on("vector", v_l)
            vv = s.notify("vector", "lg_v")
            s.wait("tensor", "lg_v", vv)
        s.barrier()
        def s_T2(p):
            p.activation(RT1[0:1, 1:2], VEC[0:1, 5:6], AF.Exp)
            p.drain()
            return p.activation(RT1[0:1, 2:3], RT1[0:1, 1:2], AF.Ln, bias=1.0,
                                scale=1.0)
        s.on("scalar", s_T2)
        sv = s.notify("scalar", "lg_s")
        s.wait("vector", "lg_s", sv)
        def v_T(p):
            return p.reciprocal(RT1[0:1, 3:4], RT1[0:1, 2:3])
        s.on("vector", v_T)
        s.notify("vector", "lg_v")
        s.barrier()
        self.dram("scal1", [1, 8], F32)
        s.dma("gpsimd", self.dint["scal1"][0:1, 0:1], RT1[0:1, 3:4])
        s.dma_wait("gpsimd")
        s.dma("gpsimd", RT1[:, 0:1],
              self.dint["scal1"].ap()[0:1, 0:1].partition_broadcast(128))
        for h in range(4):
            s.dma("gpsimd", RT1[:, 4 + h:5 + h],
                  self.di["vec16"][0:1, h:h + 1].partition_broadcast(128))
        self.rows_to_rp([0, 1, 2, 3])
        s.barrier()
        RP = self.sb["RP"]
        SL = [RP[:, 17 * i:17 * i + 17] for i in range(4)]
        MXs = RP[:, 68:85]
        def v_sm1(p):
            for h in range(4):
                p.tensor_scalar_add(SL[h], SL[h], RT1[:, 4 + h:5 + h])
            p.drain()
            p.tensor_scalar_mul(RP[:, 0:68], RP[:, 0:68], RT1[:, 0:1])
            p.drain()
            p.tensor_tensor(MXs, SL[0], SL[1], AL.max)
            p.drain()
            p.tensor_tensor(MXs, MXs, SL[2], AL.max)
            p.drain()
            p.tensor_tensor(MXs, MXs, SL[3], AL.max)
            p.drain()
            r = None
            for h in range(4):
                r = p.tensor_tensor(SL[h], SL[h], MXs, AL.subtract)
            return r
        s.on("vector", v_sm1)
        vv = s.notify("vector", "lg_v")
        s.wait("scalar", "lg_v", vv)
        def s_exp(p):
            return p.activation(RP[:, 0:68], RP[:, 0:68], AF.Exp)
        s.on("scalar", s_exp)
        sv = s.notify("scalar", "lg_s")
        s.wait("vector", "lg_s", sv)
        def v_sm2(p):
            r = None
            for pas in range(2):
                p.tensor_tensor(MXs, SL[0], SL[1], AL.add)
                p.drain()
                p.tensor_tensor(MXs, MXs, SL[2], AL.add)
                p.drain()
                p.tensor_tensor(MXs, MXs, SL[3], AL.add)
                p.drain()
                p.reciprocal(MXs, MXs)
                p.drain()
                for h in range(4):
                    r = p.tensor_tensor(SL[h], SL[h], MXs, AL.mult)
                p.drain()
                if pas == 0:
                    p.tensor_scalar_max(RP[:, 0:68], RP[:, 0:68], 0.02)
                    p.drain()
            return r
        s.on("vector", v_sm2)
        s.notify("vector", "lg_v")
        s.barrier()
        self.rp_to_rows([0, 1, 2, 3], [0, 1, 2, 3])
        s.barrier()
        s.dma("gpsimd", self.dint["prow"].ap()[0:4, :],
              self.sb["ROWT"][0:4, HALO:HALO + NT])
        s.barrier()
        for h in range(4):
            s.dma("gpsimd", self.sb["RP"][:, 17 * h:17 * h + 17],
                  self.sb["ROWP"][4 + h:5 + h, 0:ROWW])
            s.dma("gpsimd", RT1[:, 4 + h:5 + h],
                  self.di["vec16"][0:1, 6 + h:7 + h].partition_broadcast(128))
        s.barrier()
        def v_res(p):
            r = None
            for h in range(4):
                r = p.tensor_scalar_add(self.sb["RP"][:, 17 * h:17 * h + 17],
                                        self.sb["RP"][:, 17 * h:17 * h + 17],
                                        RT1[:, 4 + h:5 + h])
            return r
        s.on("vector", v_res)
        vv = s.notify("vector", "rg_v")
        s.wait("scalar", "rg_v", vv)
        def s_res(p):
            return p.activation(self.sb["RP"][:, 0:68], self.sb["RP"][:, 0:68],
                                AF.Sigmoid)
        s.on("scalar", s_res)
        s.notify("scalar", "rg_s")
        s.barrier()
        for h in range(4):
            s.dma("gpsimd", RT1[:, 4 + h:5 + h],
                  self.di["vec16"][0:1, 10 + h:11 + h].partition_broadcast(128))
        s.barrier()
        def s_crl(p):
            return p.activation(RT1[:, 4:8], RT1[:, 4:8], AF.Sigmoid)
        s.on("scalar", s_crl)
        sv = s.notify("scalar", "rg_s")
        s.wait("vector", "rg_s", sv)
        def v_t5(p):
            r = None
            for h in range(4):
                sl_ = self.sb["RP"][:, 17 * h:17 * h + 17]
                r = p.tensor_scalar_mul(sl_, sl_, RT1[:, 4 + h:5 + h])
            return r
        s.on("vector", v_t5)
        s.notify("vector", "rg_v")
        s.barrier()
        for h in range(4):
            s.dma("gpsimd", self.sb["ROWT"][4 + h:5 + h, 0:ROWW],
                  self.sb["RP"][:, 17 * h:17 * h + 17])
        s.barrier()
        if self.dbg_on:
            self.dbg("prow", self.dint["prow"].ap()[0:4, :], [4, NT], F32)
            self.dbg("t5", self.sb["ROWT"][0:8, 0:ROWW], [8, ROWW], F32)

    def head_stats(self, tile, dst_rows, dst_tile, stream, tag):
        s = self.s
        NP4 = self.NP4
        SCB1, SCB2 = self.sb["HS1"], self.sb["HS2"]
        STR = self.sb["STR"]
        PSR = self.ps["PSR"]
        last_t = None
        tvals = []
        for pi, (t0, t1) in enumerate(NP4):
            w = t1 - t0
            if stream is not None:
                nm, idx = stream
                ap = (self.dint[nm].ap()[idx, :, :, t0:t1] if idx is not None
                      else self.dint[nm].ap()[:, :, t0:t1])
                qq = "gpsimd" if pi % 2 == 0 else "sync"
                if pi >= 2:
                    s.wait(qq, tag + "_t", tvals[pi - 2])
                s.dma(qq, STR[:, pi % 2, :, 0:w],
                      ap.rearrange("s p t -> p s t"))
                dv = s.dcnt[qq]
                for e in ("vector", "scalar", "tensor"):
                    s.dma_wait(e, upto=dv, src=qq)
                srcf = lambda c, pi=pi, w=w: STR[:, pi % 2, c, 0:w]
            else:
                srcf = lambda c, t0=t0, w=w: tile[:, c, t0:t0 + w]
            if last_t is not None:
                s.wait("vector", tag + "_t", last_t)
                s.wait("scalar", tag + "_t", last_t)
            def v_sq(p, srcf=srcf, w=w):
                r = None
                for c in range(NCH):
                    r = p.tensor_tensor(SCB1[:, c, 0:w], srcf(c), srcf(c),
                                        AL.mult)
                return r
            s.on("vector", v_sq)
            vv = s.notify("vector", tag + "_v")
            def s_ab(p, srcf=srcf, w=w):
                r = None
                for c in range(NCH):
                    r = p.activation(SCB2[:, c, 0:w], srcf(c), AF.Abs)
                return r
            s.on("scalar", s_ab)
            sv = s.notify("scalar", tag + "_s")
            s.wait("tensor", tag + "_v", vv)
            s.wait("tensor", tag + "_s", sv)
            def t_fn(p, srcf=srcf, w=w):
                r = None
                for c in range(NCH):
                    h = c // 2
                    r = p.matmul(PSR[0:12, 0:w], self.sb["EJ"][:, 16 * h:16 * h + 12],
                                 srcf(c), start=(c == 0), stop=False)
                for c in range(NCH):
                    h = c // 2
                    r = p.matmul(PSR[0:12, 0:w],
                                 self.sb["EJ"][:, 16 * (4 + h):16 * (4 + h) + 12],
                                 SCB1[:, c, 0:w], start=False, stop=False)
                for c in range(NCH):
                    h = c // 2
                    r = p.matmul(PSR[0:12, 0:w],
                                 self.sb["EJ"][:, 16 * (8 + h):16 * (8 + h) + 12],
                                 SCB2[:, c, 0:w], start=False,
                                 stop=(c == NCH - 1))
                return r
            s.on("tensor", t_fn)
            tv = s.notify("tensor", tag + "_t")
            last_t = tv
            tvals.append(tv)
            s.wait("vector", tag + "_t", tv)
            def v_cp(p, t0=t0, w=w):
                return p.tensor_copy(
                    self.sb["ROWT"][0:12, HALO + t0:HALO + t0 + w],
                    PSR[0:12, 0:w])
            s.on("vector", v_cp)
            vv2 = s.notify("vector", tag + "_v")
            s.wait("tensor", tag + "_v", vv2)
        def vz(p):
            p.memset(self.sb["ROWT"][0:12, 0:HALO], 0.0)
            return p.memset(self.sb["ROWT"][0:12, HALO + NT:ROWW], 0.0)
        s.on("vector", vz)
        s.barrier()
        self.rows_to_rp(list(range(12)))
        s.barrier()
        RP = self.sb["RP"]
        def sl(i):
            return RP[:, 17 * i:17 * i + 17]
        def v_m1(p):
            for h in range(4):
                p.tensor_copy(RP[:, 204 + 17 * h:204 + 17 * h + 17], sl(4 + h))
            p.drain()
            for i in range(12):
                p.tensor_scalar_mul(sl(i), sl(i), 1.0 / 256.0)
            p.drain()
            for h in range(4):
                p.tensor_tensor(RP[:, 272 + 17 * h:272 + 17 * h + 17],
                                sl(h), sl(h), AL.mult)
            p.drain()
            r = None
            for h in range(4):
                r = p.tensor_tensor(sl(4 + h), sl(4 + h),
                                    RP[:, 272 + 17 * h:272 + 17 * h + 17],
                                    AL.subtract)
            return r
        s.on("vector", v_m1)
        vv = s.notify("vector", tag + "m_v")
        s.wait("scalar", tag + "m_v", vv)
        def s_m(p):
            return p.activation(RP[:, 204:272], RP[:, 204:272], AF.Sqrt)
        s.on("scalar", s_m)
        sv = s.notify("scalar", tag + "m_s")
        s.wait("vector", tag + "m_s", sv)
        def v_m2(p):
            for kind in range(4):
                if kind < 3:
                    bases = [sl(4 * kind + h) for h in range(4)]
                else:
                    bases = [RP[:, 204 + 17 * h:204 + 17 * h + 17] for h in range(4)]
                d = RP[:, 340 + 17 * kind:340 + 17 * kind + 17]
                p.tensor_tensor(d, bases[0], bases[1], AL.add)
                p.drain()
                p.tensor_tensor(d, d, bases[2], AL.add)
                p.drain()
                p.tensor_tensor(d, d, bases[3], AL.add)
                p.drain()
                r = p.tensor_scalar_mul(d, d, 0.25)
                p.drain()
            return r
        s.on("vector", v_m2)
        s.notify("vector", tag + "m_v")
        s.barrier()
        for kind in range(4):
            s.dma("gpsimd", dst_tile[dst_rows + kind:dst_rows + kind + 1, 0:ROWW],
                  self.sb["RP"][:, 340 + 17 * kind:340 + 17 * kind + 17])
        s.barrier()

    def stage_mix_out(self):
        s = self.s
        E = self.ERA
        NP4 = self.NP4
        DOUTM = self.sb["DOUTM"]
        STR = self.sb["STR"]
        MIXC = self.at("MIXC", [128, 8, NT], BF16, E)
        SCB1 = self.sb["HS1"]
        rb = self.dint.get("rowbounce") or self.dram("rowbounce", [16, ROWW], F32)
        s.dma("gpsimd", rb[4:8, 0:ROWW], self.sb["ROWT"][4:8, 0:ROWW])
        s.dma_wait("gpsimd")
        s.dma_wait("sync", src="gpsimd")
        vcs = []
        for k in range(8):
            qq = "gpsimd" if k % 2 == 0 else "sync"
            dst = self.sb["BCA"] if k % 2 == 0 else self.sb["BCB"]
            if k >= 2:
                s.wait(qq, "mx_v", vcs[k - 2])
            if k < 4:
                srcap = self.dint["prow"].ap()[k:k + 1, 0:NT]
            else:
                srcap = rb[k:k + 1, HALO:HALO + NT]
            s.dma(qq, dst[:, HALO:HALO + NT], srcap.partition_broadcast(128))
            dv = s.dcnt[qq]
            s.dma_wait("vector", upto=dv, src=qq)
            def vc(p, k=k, dst=dst):
                return p.tensor_copy(MIXC[:, k, :], dst[:, HALO:HALO + NT])
            s.on("vector", vc)
            vcs.append(s.notify("vector", "mx_v"))
        s.barrier()
        for pi, (t0, t1) in enumerate(NP4):
            w = t1 - t0
            s.dma("gpsimd", STR[:, 0, :, 0:w],
                  self.dint["lsll"].ap()[0, :, :, t0:t1].rearrange("s p t -> p s t"))
            s.dma("gpsimd", STR[:, 1, :, 0:w],
                  self.dint["lsll"].ap()[1, :, :, t0:t1].rearrange("s p t -> p s t"))
            dv = s.dcnt["gpsimd"]
            s.dma_wait("vector", upto=dv, src="gpsimd")
            def v_mix1(p, t0=t0, w=w):
                r = None
                for c in range(NCH):
                    p.tensor_tensor(DOUTM[:, c, t0:t0 + w], DOUTM[:, c, t0:t0 + w],
                                    MIXC[:, 2, t0:t0 + w], AL.mult)
                for c in range(NCH):
                    p.tensor_tensor(SCB1[:, c, 0:w], STR[:, 0, c, 0:w],
                                    MIXC[:, 0, t0:t0 + w], AL.mult)
                p.drain()
                p.drain()
                for c in range(NCH):
                    p.tensor_tensor(DOUTM[:, c, t0:t0 + w], DOUTM[:, c, t0:t0 + w],
                                    SCB1[:, c, 0:w], AL.add)
                p.drain()
                for c in range(NCH):
                    p.tensor_tensor(SCB1[:, c, 0:w], STR[:, 0, c, 0:w],
                                    MIXC[:, 4 + c // 2, t0:t0 + w], AL.mult)
                p.drain()
                for c in range(NCH):
                    p.tensor_tensor(DOUTM[:, c, t0:t0 + w], DOUTM[:, c, t0:t0 + w],
                                    SCB1[:, c, 0:w], AL.add)
                p.drain()
                for c in range(NCH):
                    p.tensor_tensor(SCB1[:, c, 0:w], STR[:, 1, c, 0:w],
                                    MIXC[:, 1, t0:t0 + w], AL.mult)
                p.drain()
                for c in range(NCH):
                    r = p.tensor_tensor(DOUTM[:, c, t0:t0 + w],
                                        DOUTM[:, c, t0:t0 + w],
                                        SCB1[:, c, 0:w], AL.add)
                return r
            s.on("vector", v_mix1)
            vv = s.notify("vector", "mx_v")
            s.wait("gpsimd", "mx_v", vv)
            s.dma("gpsimd", STR[:, 0, :, 0:w],
                  self.dint["vdir"].ap()[:, :, t0:t1].rearrange("s p t -> p s t"))
            dv = s.dcnt["gpsimd"]
            s.dma_wait("vector", upto=dv, src="gpsimd")
            def v_mix2(p, t0=t0, w=w):
                r = None
                for c in range(NCH):
                    p.tensor_tensor(SCB1[:, c, 0:w], STR[:, 0, c, 0:w],
                                    MIXC[:, 3, t0:t0 + w], AL.mult)
                p.drain()
                for c in range(NCH):
                    r = p.tensor_tensor(DOUTM[:, c, t0:t0 + w],
                                        DOUTM[:, c, t0:t0 + w],
                                        SCB1[:, c, 0:w], AL.add)
                return r
            s.on("vector", v_mix2)
            vv = s.notify("vector", "mx_v")
            s.wait("gpsimd", "mx_v", vv)
        s.barrier()
        if self.dbg_on:
            self.dbg("omix", DOUTM[:, :, :], [128, NCH, NT], BF16)
        self.head_stats(DOUTM, 0, self.sb["OGR"], None, "hso")
        for kk in range(4):
            s.dma("gpsimd", self.sb["OGR"][4 + kk:5 + kk, 0:ROWW],
                  self.sb["STATR"][12 + kk:13 + kk, 0:ROWW])
        s.barrier()
        OG1 = self.at("OG1", [128, 4, NT], BF16, E + 98688)
        OGSB = self.at("OGSB", [8, ROWW], BF16, E + 156288)
        BOG1 = self.at("BOG1", [128, 4], F32, E + 119488)
        WOG2 = self.at("WOG2", [128, 4, 8], BF16, E + 119552)
        WB, WF = self.sb["WB"], self.sb["WF32"]
        WB2 = self.at("WB2", [128, NCH, 512], BF16, E + 119680)
        W9b = self.at("W9b", [8, 512], BF16, E + 127872)
        def v_ogsb(p):
            return p.tensor_copy(OGSB[0:8, 0:ROWW], self.sb["OGR"][0:8, 0:ROWW])
        s.on("vector", v_ogsb)
        for kc in range(NCH):
            s.dma("sync", WF[:, 0:512], self.di["wog1"][128 * kc:128 * (kc + 1), :])
            s.dma_wait("vector", src="sync")
            def vc(p, kc=kc):
                return p.tensor_copy(WB2[:, kc, :], WF[:, 0:512])
            s.on("vector", vc)
            vv = s.notify("vector", "pw_v")
            s.wait("sync", "pw_v", vv)
        s.dma("sync", WF[0:8, 0:512], self.di["wog1"][1024:1032, :])
        s.dma_wait("vector", src="sync")
        def vc9(p):
            return p.tensor_copy(W9b[0:8, :], WF[0:8, 0:512])
        s.on("vector", vc9)
        vv9 = s.notify("vector", "pw_v")
        s.wait("tensor", "pw_v")
        s.wait("sync", "pw_v", vv9)
        s.dma("sync", BOG1[:, :], self.di["bog1c"][:, :])
        s.dma("sync", WOG2[:, :, :], self.di["wog2b"].ap())
        s.barrier()
        PSW, PSR = self.ps["PSW"], self.ps["PSR"]
        for oc in range(4):
            def t_og(p, oc=oc):
                r = None
                for (t0, t1) in NP4:
                    for kc in range(NCH):
                        p.matmul(PSW[:, t0:t1], WB2[:, kc, 128 * oc:128 * oc + 128],
                                 self.sb["XN"][:, kc, HALO + t0:HALO + t1],
                                 start=(kc == 0), stop=False)
                    r = p.matmul(PSW[:, t0:t1], W9b[0:8, 128 * oc:128 * oc + 128],
                                 OGSB[0:8, HALO + t0:HALO + t1],
                                 start=False, stop=True)
                return r
            s.on("tensor", t_og)
            tv = s.notify("tensor", "pw_t")
            s.wait("scalar", "pw_t", tv)
            def s_og(p, oc=oc):
                r = None
                for (t0, t1) in NP4:
                    r = p.activation(OG1[:, oc, t0:t1], PSW[:, t0:t1], AF.Gelu,
                                     bias=BOG1[:, oc:oc + 1], scale=1.0)
                return r
            s.on("scalar", s_og)
            sv = s.notify("scalar", "pw_s")
            s.wait("tensor", "pw_s", sv)
        s.barrier()
        for (t0, t1) in NP4:
            w = t1 - t0
            def t_o2(p, t0=t0, w=w):
                r = None
                for kc in range(4):
                    r = p.matmul(PSR[0:8, 0:w], WOG2[:, kc, :],
                                 OG1[:, kc, t0:t0 + w],
                                 start=(kc == 0), stop=(kc == 3))
                return r
            s.on("tensor", t_o2)
            tv = s.notify("tensor", "og_t")
            s.wait("scalar", "og_t", tv)
            def s_o2(p, t0=t0, w=w):
                return p.activation(self.sb["ROWT"][0:1, HALO + t0:HALO + t0 + w],
                                    PSR[0:1, 0:w], AF.Sigmoid,
                                    bias=self.sb["VEC"][0:1, 4:5], scale=1.0)
            s.on("scalar", s_o2)
            sv = s.notify("scalar", "og_s")
            s.wait("tensor", "og_s", sv)
        s.barrier()
        self.bcast_row(0, self.sb["BCA"])
        s.barrier()
        def v_og(p):
            r = None
            for c in range(NCH):
                r = p.tensor_tensor(DOUTM[:, c, :], DOUTM[:, c, :],
                                    self.sb["BCA"][:, HALO:HALO + NT], AL.mult)
            return r
        s.on("vector", v_og)
        s.barrier()
        SCB1 = self.sb["HS1"]
        last_t = None
        for (t0, t1) in NP4:
            w = t1 - t0
            if last_t is not None:
                s.wait("vector", "rn_t", last_t)
            def v_sq(p, t0=t0, w=w):
                r = None
                for c in range(NCH):
                    r = p.tensor_tensor(SCB1[:, c, 0:w], DOUTM[:, c, t0:t0 + w],
                                        DOUTM[:, c, t0:t0 + w], AL.mult)
                return r
            s.on("vector", v_sq)
            vv = s.notify("vector", "rn_v")
            s.wait("tensor", "rn_v", vv)
            def t_rn(p, t0=t0, w=w):
                r = None
                for c in range(NCH):
                    h = c // 2
                    r = p.matmul(PSR[0:4, 0:w], self.sb["EJ"][:, 16 * h:16 * h + 4],
                                 SCB1[:, c, 0:w], start=(c == 0),
                                 stop=(c == NCH - 1))
                return r
            s.on("tensor", t_rn)
            tv = s.notify("tensor", "rn_t")
            last_t = tv
            s.wait("vector", "rn_t", tv)
            def v_cp(p, t0=t0, w=w):
                return p.tensor_copy(self.sb["ROWT"][0:4, HALO + t0:HALO + t0 + w],
                                     PSR[0:4, 0:w])
            s.on("vector", v_cp)
            vv2 = s.notify("vector", "rn_v")
            s.wait("tensor", "rn_v", vv2)
        def vz(p):
            p.memset(self.sb["ROWT"][0:4, 0:HALO], 0.0)
            return p.memset(self.sb["ROWT"][0:4, HALO + NT:ROWW], 0.0)
        s.on("vector", vz)
        s.barrier()
        self.rows_to_rp([0, 1, 2, 3])
        s.barrier()
        RP = self.sb["RP"]
        def v_rn(p):
            p.tensor_scalar_mul(RP[:, 0:68], RP[:, 0:68], 1.0 / 256.0)
            p.drain()
            p.tensor_scalar_add(RP[:, 0:68], RP[:, 0:68], RMS_EPS)
            p.drain()
            return p.reciprocal(RP[:, 0:68], RP[:, 0:68])
        s.on("vector", v_rn)
        vv = s.notify("vector", "rn_v")
        s.wait("scalar", "rn_v", vv)
        def s_rn(p):
            return p.activation(RP[:, 0:68], RP[:, 0:68], AF.Sqrt)
        s.on("scalar", s_rn)
        s.notify("scalar", "rn_s")
        s.barrier()
        self.rp_to_rows([0, 1, 2, 3], [0, 1, 2, 3])
        s.barrier()
        RSB = self.at("RSB", [128, 4, NT], BF16, E + 128896)
        rb2 = self.dint["rowbounce"]
        s.dma("gpsimd", rb2[0:4, 0:ROWW], self.sb["ROWT"][0:4, 0:ROWW])
        s.dma_wait("gpsimd")
        s.dma_wait("sync", src="gpsimd")
        vcs2 = []
        for h in range(4):
            qq = "gpsimd" if h % 2 == 0 else "sync"
            dst = self.sb["BCA"] if h % 2 == 0 else self.sb["BCB"]
            if h >= 2:
                s.wait(qq, "rn_v", vcs2[h - 2])
            s.dma(qq, dst[:, HALO:HALO + NT],
                  rb2[h:h + 1, HALO:HALO + NT].partition_broadcast(128))
            dv = s.dcnt[qq]
            s.dma_wait("vector", upto=dv, src=qq)
            def vc(p, h=h, dst=dst):
                return p.tensor_copy(RSB[:, h, :], dst[:, HALO:HALO + NT])
            s.on("vector", vc)
            vcs2.append(s.notify("vector", "rn_v"))
        s.barrier()
        NRMW = self.at("NRMW", [128, NCH], F32, E + 145280)
        s.dma("sync", NRMW[:, :], self.di["normwc"][:, :])
        s.barrier()
        def v_rnap(p):
            r = None
            for c in range(NCH):
                r = p.scalar_tensor_tensor(DOUTM[:, c, :], DOUTM[:, c, :],
                                           NRMW[:, c:c + 1], RSB[:, c // 2, :],
                                           AL.mult, AL.mult)
            return r
        s.on("vector", v_rnap)
        s.barrier()
        if self.dbg_on:
            self.dbg("onrm", DOUTM[:, :, :], [128, NCH, NT], BF16)
        X1a = self.at("X1a", [128, 4, NT], F32, E)
        X1b = self.at("X1b", [128, 4, NT], F32, E + 65792)
        def x1t(oc):
            return (X1a[:, oc, :] if oc < 4 else X1b[:, oc - 4, :])
        WB, WF = self.sb["WB"], self.sb["WF32"]
        for kc in range(NCH):
            s.dma("sync", WF[:, 0:1024], self.di["wo"][128 * kc:128 * (kc + 1), :])
            s.dma_wait("vector", src="sync")
            def vc(p, kc=kc):
                return p.tensor_copy(WB[:, kc, :], WF[:, 0:1024])
            s.on("vector", vc)
            vv = s.notify("vector", "pw_v")
            s.wait("sync", "pw_v", vv)
        s.wait("tensor", "pw_v")
        for oc in range(NCH):
            s.dma("sync", x1t(oc), self.di["x0"].ap().rearrange(
                "(c p) t -> p c t", p=128)[:, oc, HALO:HALO + NT])
        s.barrier()
        for oc in range(NCH):
            def t_wo(p, oc=oc):
                r = None
                for (t0, t1) in NP4:
                    for kc in range(NCH):
                        r = p.matmul(PSW[:, t0:t1],
                                     WB[:, kc, 128 * oc:128 * oc + 128],
                                     DOUTM[:, kc, t0:t1], start=(kc == 0),
                                     stop=(kc == NCH - 1))
                return r
            s.on("tensor", t_wo)
            tv = s.notify("tensor", "pw_t")
            s.wait("vector", "pw_t", tv)
            def v_wo(p, oc=oc):
                r = None
                for (t0, t1) in NP4:
                    r = p.tensor_tensor(x1t(oc)[:, t0:t1], x1t(oc)[:, t0:t1],
                                        PSW[:, t0:t1], AL.add)
                return r
            s.on("vector", v_wo)
            vv = s.notify("vector", "mx_v")
            s.wait("tensor", "mx_v", vv)
        s.barrier()
        if self.dbg_on:
            self.dbg("x1a", X1a[:, :, :], [128, 4, NT], F32)
            self.dbg("x1b", X1b[:, :, :], [128, 4, NT], F32)

    def bcast_row_dram(self, name, row, dst, width=NT):
        s = self.s
        def dst_ap():
            d = dst() if callable(dst) else dst
            return d[:, HALO:HALO + width]
        s.dma("gpsimd", dst_ap,
              lambda: self.dint[name].ap()[row:row + 1, 0:width]
              .partition_broadcast(128))

    def stage_ffn1(self):
        self.ffn_apply("f1")
        if self.dbg_on:
            X1a, X1b = self.sb["X1a"], self.sb["X1b"]
            self.dbg("x2a", X1a[:, :, :], [128, 4, NT], F32)
            self.dbg("x2b", X1b[:, :, :], [128, 4, NT], F32)

    def ffn_ternarize(self):
        s = self.s
        E = self.ERA
        WE = self.at("WE", [128, NCH, 1024], BF16, E + 98688)
        WF = self.sb["WF32"]
        TN1 = self.at("TN1", [128, 1024], F32, E + 119680)
        TN2 = self.at("TN2", [128, 1024], F32, E + 123776)
        TN3 = self.at("TN3", [128, 1024], F32, E + 127872)
        TB1 = self.at("TB1", [128, 1024], BF16, E + 131968)
        TB2 = self.at("TB2", [128, 1024], BF16, E + 134016)
        PRS = self.sb_off["PRSV"][0]
        SCL2 = self.at("SCL2", [128, 8], F32, E + 152448)
        self.sb["SCL2z"] = SCL2
        GAM = self.at("GAM", [128, NCH], F32, PRS)
        BET = self.at("BET", [128, NCH], F32, PRS + 32)
        AL1 = self.at("AL1", [1, 8], F32, PRS + 64)
        self.at("ROWB", [1, 128], F32, E + 152512)
        self.at("MK8", [128, 4], F32, E + 152480)
        s.dma("sync", GAM[:, :], self.di["gamc"][:, :])
        s.dma("sync", BET[:, :], self.di["betc"][:, :])
        s.barrier()
        PSW = self.ps["PSW"]
        for kc in range(NCH):
            s.dma("sync", WF[:, 0:1024],
                  self.di["ffnw"][128 * kc:128 * (kc + 1), :])
            s.dma_wait("scalar", src="sync")
            def s_a(p):
                return p.activation(TB1[:, :], WF[:, 0:1024], AF.Abs)
            s.on("scalar", s_a)
            sv = s.notify("scalar", "tz_s")
            s.wait("tensor", "tz_s", sv)
            def t_a(p, kc=kc):
                r = None
                for sg in range(2):
                    r = p.matmul(PSW[0:1, 512 * sg:512 * sg + 512],
                                 self.sb["EJ"][:, 0:1],
                                 TB1[:, 512 * sg:512 * sg + 512],
                                 start=(kc == 0), stop=(kc == NCH - 1))
                return r
            s.on("tensor", t_a)
            tv = s.notify("tensor", "tz_t")
            s.wait("sync", "tz_t", tv)
        s.barrier()
        def v_d(p):
            p.tensor_reduce(SCL2[0:1, 4:5], PSW[0:1, 0:1024],
                            mybir.AxisListType.X, AL.add)
            p.drain()
            return p.tensor_scalar_mul(SCL2[0:1, 4:5], SCL2[0:1, 4:5],
                                       0.7 / (1024.0 * 1024.0))
        s.on("vector", v_d)
        s.barrier()
        self.dram("scal2", [1, 8], F32)
        s.dma("gpsimd", self.dint["scal2"][0:1, 0:1], SCL2[0:1, 4:5])
        s.dma_wait("gpsimd")
        s.dma("gpsimd", SCL2[:, 5:6],
              self.dint["scal2"].ap()[0:1, 0:1].partition_broadcast(128))
        s.barrier()
        for kc in range(NCH):
            s.dma("sync", WF[:, 0:1024],
                  self.di["ffnw"][128 * kc:128 * (kc + 1), :])
            s.dma_wait("scalar", src="sync")
            def s_b(p):
                p.activation(TN1[:, :], WF[:, 0:1024], AF.Abs)
                return p.activation(TN2[:, :], WF[:, 0:1024], AF.Sign)
            s.on("scalar", s_b)
            sv = s.notify("scalar", "tz_s")
            s.wait("vector", "tz_s", sv)
            s.wait("sync", "tz_s", sv)
            def v_b(p):
                return p.tensor_scalar_sub(TN3[:, :], TN1[:, :], SCL2[:, 5:6])
            s.on("vector", v_b)
            vv = s.notify("vector", "tz_v")
            s.wait("scalar", "tz_v", vv)
            def s_c(p):
                return p.activation(TN3[:, :], TN3[:, :], AF.Sign)
            s.on("scalar", s_c)
            sv = s.notify("scalar", "tz_s")
            s.wait("vector", "tz_s", sv)
            def v_c(p, kc=kc):
                p.tensor_scalar_max(TN3[:, :], TN3[:, :], 0.0)
                p.drain()
                p.tensor_copy(TB1[:, :], TN3[:, :])
                p.drain()
                p.tensor_tensor(WE[:, kc, :], TN2[:, :], TN3[:, :], AL.mult)
                p.drain()
                return p.tensor_tensor(TB2[:, :], TN1[:, :], TN3[:, :],
                                       AL.mult)
            s.on("vector", v_c)
            vv = s.notify("vector", "tz_v")
            s.wait("tensor", "tz_v", vv)
            def t_b(p, kc=kc):
                r = None
                for sg in range(2):
                    sl_ = slice(512 * sg, 512 * sg + 512)
                    p.matmul(PSW[0:2, 1024 + 512 * sg:1536 + 512 * sg],
                             self.sb["EJ"][:, 0:2], TB2[:, sl_],
                             start=(kc == 0), stop=False)
                    r = p.matmul(PSW[0:2, 1024 + 512 * sg:1536 + 512 * sg],
                                 self.sb["EJ"][:, 16:18], TB1[:, sl_],
                                 start=False, stop=(kc == NCH - 1))
                return r
            s.on("tensor", t_b)
            tv = s.notify("tensor", "tz_t")
            s.wait("sync", "tz_t", tv)
        s.barrier()
        def v_al(p):
            p.tensor_reduce(SCL2[0:2, 6:7], PSW[0:2, 1024:2048],
                            mybir.AxisListType.X, AL.add)
            return p.drain()
        s.on("vector", v_al)
        s.barrier()
        s.dma("gpsimd", AL1[0:1, 2:4], SCL2[0:2, 6:7])
        s.barrier()
        def v_al2(p):
            p.tensor_scalar_max(AL1[0:1, 3:4], AL1[0:1, 3:4], 1.0)
            p.drain()
            p.reciprocal(AL1[0:1, 3:4], AL1[0:1, 3:4])
            p.drain()
            return p.tensor_tensor(AL1[0:1, 0:1], AL1[0:1, 2:3],
                                   AL1[0:1, 3:4], AL.mult)
        s.on("vector", v_al2)
        self.dram("wedr", [128, NCH, 1024], BF16)
        s.dma("gpsimd", self.dint["wedr"].ap(), WE[:, :, :])
        s.barrier()

    def ffn_apply(self, tag):
        s = self.s
        E = self.ERA
        NP4 = self.NP4
        WE = self.at("WEL" + tag, [128, NCH, 1024], BF16, E + 98688)
        s.dma("sync", WE[:, :, :], self.dint["wedr"].ap())
        SCL2 = self.at("SCL2" + tag, [128, 8], F32, E + 152448)
        GAM, BET, AL1 = self.sb["GAM"], self.sb["BET"], self.sb["AL1"]
        MK8, ROWB = self.sb["MK8"], self.sb["ROWB"]
        HS1, HS2 = self.sb["HS1"], self.sb["HS2"]
        INT = self.at("INT" + tag, [128, NCH, NT], BF16, E + 32768)
        S2 = self.at("S2" + tag, [128, NCH, 512], F32, E + 136064)
        X1a, X1b = self.sb["X1a"], self.sb["X1b"]
        def x1t(c):
            return (X1a[:, c, :] if c < 4 else X1b[:, c - 4, :])
        PSR, PSW = self.ps["PSR"], self.ps["PSW"]
        last_t = None
        for (t0, t1) in NP4:
            w = t1 - t0
            if last_t is not None:
                s.wait("vector", tag + "r_t", last_t)
                s.wait("scalar", tag + "r_t", last_t)
            def v_cast(p, t0=t0, w=w):
                r = None
                for c in range(NCH):
                    r = p.tensor_copy(HS1[:, c, 0:w], x1t(c)[:, t0:t0 + w])
                return r
            s.on("vector", v_cast)
            vv = s.notify("vector", tag + "r_v")
            def s_sq(p, t0=t0, w=w):
                r = None
                for c in range(NCH):
                    r = p.activation(HS2[:, c, 0:w], x1t(c)[:, t0:t0 + w],
                                     AF.Square)
                return r
            s.on("scalar", s_sq)
            sv = s.notify("scalar", tag + "r_s")
            s.wait("tensor", tag + "r_v", vv)
            s.wait("tensor", tag + "r_s", sv)
            def t_r(p, w=w):
                r = None
                for c in range(NCH):
                    p.matmul(PSR[0:2, 0:w], self.sb["EJ"][:, 0:2],
                             HS1[:, c, 0:w], start=(c == 0), stop=False)
                for c in range(NCH):
                    r = p.matmul(PSR[0:2, 0:w], self.sb["EJ"][:, 16:18],
                                 HS2[:, c, 0:w], start=False,
                                 stop=(c == NCH - 1))
                return r
            s.on("tensor", t_r)
            tv = s.notify("tensor", tag + "r_t")
            last_t = tv
            s.wait("vector", tag + "r_t", tv)
            def v_cp(p, t0=t0, w=w):
                return p.tensor_copy(self.sb["ROWT"][0:2, HALO + t0:HALO + t0 + w],
                                     PSR[0:2, 0:w])
            s.on("vector", v_cp)
            vv2 = s.notify("vector", tag + "r_v")
            s.wait("tensor", tag + "r_v", vv2)
        def vz(p):
            p.memset(self.sb["ROWT"][0:2, 0:HALO], 0.0)
            return p.memset(self.sb["ROWT"][0:2, HALO + NT:ROWW], 0.0)
        s.on("vector", vz)
        s.barrier()
        self.rows_to_rp([0, 1])
        s.barrier()
        RP = self.sb["RP"]
        mu, msq, var = RP[:, 0:17], RP[:, 17:34], RP[:, 34:51]
        t_, varb = RP[:, 51:68], RP[:, 68:85]
        def v_math(p):
            p.tensor_scalar_mul(mu, mu, 1.0 / D)
            p.tensor_scalar_mul(msq, msq, 1.0 / D)
            p.drain()
            p.tensor_tensor(var, mu, mu, AL.mult)
            p.drain()
            p.tensor_tensor(var, msq, var, AL.subtract)
            p.drain()
            p.tensor_scalar_add(t_, var, LN_EPS)
            p.drain()
            p.reciprocal(varb, t_)
            p.drain()
            p.tensor_tensor(varb, var, varb, AL.mult)
            p.drain()
            p.tensor_scalar_add(varb, varb, RMS_EPS)
            p.drain()
            p.tensor_tensor(t_, t_, varb, AL.mult)
            p.drain()
            return p.reciprocal(t_, t_)
        s.on("vector", v_math)
        vv = s.notify("vector", tag + "m_v")
        s.wait("scalar", tag + "m_v", vv)
        def s_math(p):
            return p.activation(t_, t_, AF.Sqrt)
        s.on("scalar", s_math)
        s.notify("scalar", tag + "m_s")
        s.barrier()
        self.rp_to_rows([0, 3], [0, 1])
        s.barrier()
        self.bcast_row(0, self.sb["BCA"])
        self.bcast_row(1, self.sb["BCB"])
        s.barrier()
        SC = self.sb["SC"]
        for pi, (t0, t1) in enumerate(NP4):
            w = t1 - t0
            def v_a(p, t0=t0, w=w):
                r = None
                for c in range(NCH):
                    p.tensor_tensor(SC[:, 0:w], x1t(c)[:, t0:t0 + w],
                                    self.sb["BCA"][:, HALO + t0:HALO + t0 + w],
                                    AL.subtract)
                    p.drain()
                    r = p.tensor_tensor(S2[:, c, 0:w], SC[:, 0:w],
                                        self.sb["BCB"][:, HALO + t0:HALO + t0 + w],
                                        AL.mult)
                    p.drain()
                return r
            s.on("vector", v_a)
            vv = s.notify("vector", tag + "a_v")
            s.wait("scalar", tag + "a_v", vv)
            def s_a(p, w=w):
                r = None
                for c in range(NCH):
                    r = p.activation(S2[:, c, 0:w], S2[:, c, 0:w], AF.Identity,
                                     bias=BET[:, c:c + 1], scale=GAM[:, c:c + 1])
                return r
            s.on("scalar", s_a)
            sv = s.notify("scalar", tag + "a_s")
            s.wait("vector", tag + "a_s", sv)
            def v_mx(p, pi=pi, w=w):
                return p.tensor_reduce(MK8[:, pi:pi + 1], S2[:, :, 0:w],
                                       mybir.AxisListType.XY, AL.max,
                                       apply_absolute_value=True)
            s.on("vector", v_mx)
            vv = s.notify("vector", tag + "a_v")
            s.wait("scalar", tag + "a_v", vv)
        s.barrier()
        def v_g(p):
            return p.tensor_reduce(SCL2[:, 2:3], MK8[:, 0:4],
                                   mybir.AxisListType.X, AL.max)
        s.on("vector", v_g)
        s.barrier()
        s.dma("gpsimd", ROWB[0:1, 0:128], SCL2[:, 2:3])
        s.barrier()
        def v_g2(p):
            return p.tensor_reduce(SCL2[0:1, 2:3], ROWB[0:1, 0:128],
                                   mybir.AxisListType.X, AL.max)
        s.on("vector", v_g2)
        s.barrier()
        qin = self.dram("qin" + tag, [1, 1], F32)
        qout = self.dram("qout" + tag, [1, 1], F32)
        s.dma("gpsimd", qin[0:1, 0:1], SCL2[0:1, 2:3])
        s.dma_wait("gpsimd")
        def g_ar(p):
            return p.collective_compute(
                "AllReduce", AL.max,
                replica_groups=[[0, 1, 2, 3, 4, 5, 6, 7]],
                ins=[self.dint["qin" + tag].ap().opt()],
                outs=[self.dint["qout" + tag].ap().opt()])
        s.on("gpsimd", g_ar, kind="data")
        s.notify("gpsimd", "ccd")
        s.wait("gpsimd", "ccd")
        s.dma("gpsimd", SCL2[0:1, 2:3], self.dint["qout" + tag].ap()[0:1, 0:1])
        s.barrier()
        def v_q(p):
            p.tensor_scalar_max(SCL2[0:1, 2:3], SCL2[0:1, 2:3], 1e-8)
            p.drain()
            p.tensor_scalar_mul(SCL2[0:1, 3:4], SCL2[0:1, 2:3], 1.0 / 127.0)
            p.drain()
            p.reciprocal(SCL2[0:1, 2:3], SCL2[0:1, 2:3])
            p.drain()
            p.tensor_scalar_mul(SCL2[0:1, 2:3], SCL2[0:1, 2:3], 127.0)
            p.drain()
            return p.tensor_tensor(SCL2[0:1, 3:4], SCL2[0:1, 3:4],
                                   AL1[0:1, 0:1], AL.mult)
        s.on("vector", v_q)
        s.barrier()
        s.dma("gpsimd", self.dint["scal2"][0:1, 2:4], SCL2[0:1, 2:4])
        s.dma_wait("gpsimd")
        s.dma("gpsimd", SCL2[:, 0:1],
              self.dint["scal2"].ap()[0:1, 2:3].partition_broadcast(128))
        s.dma("gpsimd", SCL2[:, 1:2],
              self.dint["scal2"].ap()[0:1, 3:4].partition_broadcast(128))
        s.barrier()
        def v_mg(p):
            return p.memset(SCL2[:, 7:8], MAGIC)
        s.on("vector", v_mg)
        s.barrier()
        for pi, (t0, t1) in enumerate(NP4):
            w = t1 - t0
            def v_a(p, t0=t0, w=w):
                r = None
                for c in range(NCH):
                    p.tensor_tensor(SC[:, 0:w], x1t(c)[:, t0:t0 + w],
                                    self.sb["BCA"][:, HALO + t0:HALO + t0 + w],
                                    AL.subtract)
                    p.drain()
                    r = p.tensor_tensor(S2[:, c, 0:w], SC[:, 0:w],
                                        self.sb["BCB"][:, HALO + t0:HALO + t0 + w],
                                        AL.mult)
                    p.drain()
                return r
            s.on("vector", v_a)
            vv = s.notify("vector", tag + "b_v")
            s.wait("scalar", tag + "b_v", vv)
            def s_a(p, w=w):
                r = None
                for c in range(NCH):
                    p.activation(S2[:, c, 0:w], S2[:, c, 0:w], AF.Identity,
                                 bias=BET[:, c:c + 1], scale=GAM[:, c:c + 1])
                    self.snop(p)
                    r = p.activation(S2[:, c, 0:w], S2[:, c, 0:w], AF.Identity,
                                     bias=SCL2[:, 7:8], scale=SCL2[:, 0:1])
                    self.snop(p)
                return r
            s.on("scalar", s_a)
            sv = s.notify("scalar", tag + "b_s")
            s.wait("vector", tag + "b_s", sv)
            def v_i(p, t0=t0, w=w):
                r = None
                for c in range(NCH):
                    r = p.tensor_scalar_sub(INT[:, c, t0:t0 + w],
                                            S2[:, c, 0:w], MAGIC)
                return r
            s.on("vector", v_i)
            vv = s.notify("vector", tag + "b_v")
            s.wait("scalar", tag + "b_v", vv)
        s.barrier()
        if self.dbg_on:
            self.dbg("ints" + tag, INT[:, :, :], [128, NCH, NT], BF16)
        for oc in range(NCH):
            def t_f(p, oc=oc):
                r = None
                for (t0, t1) in NP4:
                    for kc in range(NCH):
                        r = p.matmul(PSW[:, t0:t1],
                                     WE[:, kc, 128 * oc:128 * oc + 128],
                                     INT[:, kc, t0:t1], start=(kc == 0),
                                     stop=(kc == NCH - 1))
                return r
            s.on("tensor", t_f)
            tv = s.notify("tensor", tag + "o_t")
            s.wait("vector", tag + "o_t", tv)
            def v_f(p, oc=oc):
                r = None
                for (t0, t1) in NP4:
                    r = p.scalar_tensor_tensor(x1t(oc)[:, t0:t1], PSW[:, t0:t1],
                                               SCL2[:, 1:2], x1t(oc)[:, t0:t1],
                                               AL.mult, AL.add)
                return r
            s.on("vector", v_f)
            vv = s.notify("vector", tag + "o_v")
            s.wait("tensor", tag + "o_v", vv)
        s.barrier()

    def stage_attn_b(self):
        s = self.s
        E = self.ERA
        NP4 = self.NP4
        X1a, X1b = self.sb["X1a"], self.sb["X1b"]
        def x1t(c):
            return (X1a[:, c, :] if c < 4 else X1b[:, c - 4, :])
        PSR, PSW = self.ps["PSR"], self.ps["PSW"]
        HS1, HS2 = self.sb["HS1"], self.sb["HS2"]
        XN3 = self.at("XN3", [128, NCH, NT + 4], BF16, E + 32768)
        PB = self.at("PB", [64, NT + 4], F32, E + 119680)
        DD = self.at("DD", [64, NT], F32, E + 127904)
        FE = self.at("FE", [16, NT], BF16, E + 136096)
        Z1 = self.at("Z1", [64, NT + 8], F32, E + 140192)
        Zb = self.at("Zb", [64, NT + 8], F32, E + 148416)
        BW = self.at("BW", [128, NCH, 4, 64], BF16, E + 156640)
        CW3 = self.at("CW3", [128, NCH, 4], F32, E + 160736)
        DTWB = self.at("DTWB", [128, NCH, 16], BF16, E + 160864)
        TRWB = self.at("TRWB", [16, 64], BF16, E + 161120)
        DTB = self.at("DTB", [16, 1], F32, E + 161248)
        INB = self.at("INB", [64, 1], F32, E + 161280)
        OUTB = self.at("OUTB", [128, NCH], F32, E + 161312)
        HM = self.at("HM", [128, 1], F32, E + 161344)
        ZAG = self.at("ZAG", [64, 1], F32, E + 161376)
        WF = self.sb["WF32"]
        last_t = None
        for (t0, t1) in NP4:
            w = t1 - t0
            if last_t is not None:
                s.wait("vector", "b3r_t", last_t)
                s.wait("scalar", "b3r_t", last_t)
            def v_cast(p, t0=t0, w=w):
                r = None
                for c in range(NCH):
                    r = p.tensor_copy(HS1[:, c, 0:w], x1t(c)[:, t0:t0 + w])
                return r
            s.on("vector", v_cast)
            vv = s.notify("vector", "b3r_v")
            def s_sq(p, t0=t0, w=w):
                r = None
                for c in range(NCH):
                    r = p.activation(HS2[:, c, 0:w], x1t(c)[:, t0:t0 + w],
                                     AF.Square)
                return r
            s.on("scalar", s_sq)
            sv = s.notify("scalar", "b3r_s")
            s.wait("tensor", "b3r_v", vv)
            s.wait("tensor", "b3r_s", sv)
            def t_r(p, w=w):
                r = None
                for c in range(NCH):
                    p.matmul(PSR[0:2, 0:w], self.sb["EJ"][:, 0:2],
                             HS1[:, c, 0:w], start=(c == 0), stop=False)
                for c in range(NCH):
                    r = p.matmul(PSR[0:2, 0:w], self.sb["EJ"][:, 16:18],
                                 HS2[:, c, 0:w], start=False,
                                 stop=(c == NCH - 1))
                return r
            s.on("tensor", t_r)
            tv = s.notify("tensor", "b3r_t")
            last_t = tv
            s.wait("vector", "b3r_t", tv)
            def v_cp(p, t0=t0, w=w):
                return p.tensor_copy(self.sb["ROWT"][0:2, HALO + t0:HALO + t0 + w],
                                     PSR[0:2, 0:w])
            s.on("vector", v_cp)
            vv2 = s.notify("vector", "b3r_v")
            s.wait("tensor", "b3r_v", vv2)
        def vz(p):
            p.memset(self.sb["ROWT"][0:2, 0:HALO], 0.0)
            return p.memset(self.sb["ROWT"][0:2, HALO + NT:ROWW], 0.0)
        s.on("vector", vz)
        s.barrier()
        self.rows_to_rp([0, 1])
        s.barrier()
        RP = self.sb["RP"]
        mu, msq, var = RP[:, 0:17], RP[:, 17:34], RP[:, 34:51]
        def v_math(p):
            p.tensor_scalar_mul(mu, mu, 1.0 / D)
            p.tensor_scalar_mul(msq, msq, 1.0 / D)
            p.drain()
            p.tensor_tensor(var, mu, mu, AL.mult)
            p.drain()
            p.tensor_tensor(var, msq, var, AL.subtract)
            p.drain()
            p.tensor_scalar_add(var, var, LN_EPS)
            p.drain()
            return p.reciprocal(var, var)
        s.on("vector", v_math)
        vv = s.notify("vector", "b3m_v")
        s.wait("scalar", "b3m_v", vv)
        def s_math(p):
            return p.activation(var, var, AF.Sqrt)
        s.on("scalar", s_math)
        s.notify("scalar", "b3m_s")
        s.barrier()
        self.rp_to_rows([0, 2], [0, 1])
        s.barrier()
        self.bcast_row(0, self.sb["BCA"])
        self.bcast_row(1, self.sb["BCB"])
        s.barrier()
        SC = self.sb["SC"]
        for (t0, t1) in NP4:
            w = t1 - t0
            def v_x(p, t0=t0, w=w):
                r = None
                for c in range(NCH):
                    p.tensor_tensor(SC[:, 0:w], x1t(c)[:, t0:t0 + w],
                                    self.sb["BCA"][:, HALO + t0:HALO + t0 + w],
                                    AL.subtract)
                    p.drain()
                    r = p.tensor_tensor(XN3[:, c, 4 + t0:4 + t0 + w], SC[:, 0:w],
                                        self.sb["BCB"][:, HALO + t0:HALO + t0 + w],
                                        AL.mult)
                    p.drain()
                return r
            s.on("vector", v_x)
            s.notify("vector", "b3x_v")
        s.barrier()
        self.dram("bhx", [NCH, 128, 4], BF16)
        self.dram("bhy", [2, NCH, 128, 4], BF16)
        s.dma("gpsimd", self.dint["bhx"].ap().rearrange("s p t -> p s t"),
              XN3[:, :, NT:NT + 4])
        s.dma("sync", HM[:, 0:1], self.di["hmask"].ap()[0:1, 0:1]
              .partition_broadcast(128))
        s.barrier()
        def cc_h(p):
            return p.collective_compute(
                "AllGather", AL.bypass,
                replica_groups=[[0, 1], [2, 3], [4, 5], [6, 7]],
                ins=[self.dint["bhx"].ap().opt()],
                outs=[self.dint["bhy"].ap().opt()])
        s.on("gpsimd", cc_h, kind="data")
        s.notify("gpsimd", "ccd")
        s.wait("gpsimd", "ccd")
        s.dma("gpsimd", XN3[:, :, 0:4],
              self.dint["bhy"].ap()[0, :, :, :].rearrange("s p t -> p s t"))
        s.barrier()
        def v_hm(p):
            return p.tensor_scalar_mul(XN3[:, :, 0:4], XN3[:, :, 0:4],
                                       HM[:, 0:1])
        s.on("vector", v_hm)
        s.barrier()
        if self.dbg_on:
            self.dbg("xn3", XN3[:, :, :], [128, NCH, NT + 4], BF16)
        s.dma("sync", CW3[:, :, :], self.di["bconvw"].ap())
        s.dma("sync", DTB[:, :], self.di["dtb"][:, :])
        s.dma("sync", INB[:, :], self.di["inbc"][:, :])
        s.dma("sync", OUTB[:, :], self.di["outbc"][:, :])
        s.dma("sync", WF[0:16, 0:64], self.di["trwt"][:, :])
        s.dma_wait("vector", src="sync")
        def v_trw(p):
            return p.tensor_copy(TRWB[0:16, :], WF[0:16, 0:64])
        s.on("vector", v_trw)
        vv = s.notify("vector", "bw_v")
        s.wait("sync", "bw_v", vv)
        s.dma("sync", WF[:, 0:128].rearrange("p (c r) -> p c r", r=16),
              self.di["dtwt"].ap().rearrange("(c p) r -> p c r", p=128))
        s.dma_wait("vector", src="sync")
        def v_dtw(p):
            return p.tensor_copy(DTWB[:, :, :],
                                 WF[:, 0:128].rearrange("p (c r) -> p c r", r=16))
        s.on("vector", v_dtw)
        vv = s.notify("vector", "bw_v")
        s.wait("sync", "bw_v", vv)
        for kc in range(NCH):
            s.dma("sync", WF[:, 0:64], self.di["inwt"][128 * kc:128 * (kc + 1), :])
            s.dma_wait("vector", src="sync")
            def v_bw(p, kc=kc):
                for j in range(3):
                    p.tensor_scalar_mul(BW[:, kc, j, :], WF[:, 0:64],
                                        CW3[:, kc, j:j + 1])
                p.tensor_scalar_add(SC[:, 0:1], CW3[:, kc, 3:4], 1.0)
                p.drain()
                return p.tensor_scalar_mul(BW[:, kc, 3, :], WF[:, 0:64],
                                           SC[:, 0:1])
            s.on("vector", v_bw)
            vv = s.notify("vector", "bw_v")
            s.wait("sync", "bw_v", vv)
        s.wait("tensor", "bw_v")
        s.barrier()
        for (t0, t1) in NP4:
            w = t1 - t0
            def t_fe(p, t0=t0, w=w):
                r = None
                for kc in range(NCH):
                    r = p.matmul(PSR[0:16, 0:w], DTWB[:, kc, :],
                                 XN3[:, kc, 4 + t0:4 + t0 + w],
                                 start=(kc == 0), stop=(kc == NCH - 1))
                return r
            s.on("tensor", t_fe)
            tv = s.notify("tensor", "bd_t")
            s.wait("scalar", "bd_t", tv)
            def s_fe(p, t0=t0, w=w):
                return p.activation(FE[0:16, t0:t0 + w], PSR[0:16, 0:w],
                                    AF.Tanh, bias=DTB[0:16, 0:1], scale=1.0)
            s.on("scalar", s_fe)
            sv = s.notify("scalar", "bd_s")
            s.wait("tensor", "bd_s", sv)
        s.barrier()
        for (t0, t1) in NP4:
            w = t1 - t0
            def t_d(p, t0=t0, w=w):
                return p.matmul(PSR[0:64, 0:w], TRWB[0:16, :],
                                FE[0:16, t0:t0 + w], start=True, stop=True)
            s.on("tensor", t_d)
            tv = s.notify("tensor", "bd_t")
            s.wait("vector", "bd_t", tv)
            def v_d(p, t0=t0, w=w):
                return p.tensor_copy(DD[0:64, t0:t0 + w], PSR[0:64, 0:w])
            s.on("vector", v_d)
            vv = s.notify("vector", "bd_v")
            s.wait("tensor", "bd_v", vv)
        s.barrier()
        PWP = [(0, 1)] + [(1 + 512 * i, 1 + 512 * (i + 1)) for i in range(4)]
        for (u0, u1) in PWP:
            w = u1 - u0
            def t_p(p, u0=u0, w=w):
                r = None
                first = True
                for kc in range(NCH):
                    for j in range(4):
                        r = p.matmul(PSW[0:64, 0:w], BW[:, kc, j, :],
                                     XN3[:, kc, j + u0:j + u0 + w],
                                     start=first, stop=(kc == NCH - 1 and j == 3))
                        first = False
                return r
            s.on("tensor", t_p)
            tv = s.notify("tensor", "bp_t")
            s.wait("scalar", "bp_t", tv)
            def s_p(p, u0=u0, w=w):
                return p.activation(PB[0:64, u0:u0 + w], PSW[0:64, 0:w],
                                    AF.Identity, bias=INB[0:64, 0:1], scale=1.0)
            s.on("scalar", s_p)
            sv = s.notify("scalar", "bp_s")
            s.wait("tensor", "bp_s", sv)
        s.barrier()
        def v_e(p):
            p.tensor_scalar_mul(PB[0:64, 0:1], PB[0:64, 0:1], HM[0:64, 0:1])
            p.drain()
            return p.tensor_tensor(DD[0:64, 0:NT], DD[0:64, 0:NT],
                                   PB[0:64, 0:NT], AL.add)
        s.on("vector", v_e)
        s.barrier()
        if self.dbg_on:
            self.dbg("bpb", PB[0:64, 0:NT + 1], [64, NT + 1], F32)
            self.dbg("be", DD[0:64, 0:NT], [64, NT], F32)
        def v_z0(p):
            return p.memset(Z1[0:64, 0:1], 0.0)
        s.on("vector", v_z0)
        vv = s.notify("vector", "bs_v")
        s.wait("scalar", "bs_v", vv)
        def s_scan1(p):
            r = None
            for t in range(NT):
                r = p.activation(Z1[0:64, t + 1:t + 2], Z1[0:64, t:t + 1],
                                 AF.Silu, bias=DD[0:64, t:t + 1], scale=1.0)
                self.snop(p)
            return r
        s.on("scalar", s_scan1)
        s.notify("scalar", "bs_s")
        s.barrier()
        self.dram("zin", [1, 64], F32)
        self.dram("zout", [2, 64], F32)
        s.dma("gpsimd", self.dint["zin"][0:1, :], Z1[0:64, NT:NT + 1])
        s.barrier()
        def cc_z(p):
            return p.collective_compute(
                "AllGather", AL.bypass,
                replica_groups=[[0, 1], [2, 3], [4, 5], [6, 7]],
                ins=[self.dint["zin"].ap().opt()],
                outs=[self.dint["zout"].ap().opt()])
        s.on("gpsimd", cc_z, kind="data")
        s.notify("gpsimd", "ccd")
        s.wait("gpsimd", "ccd")
        s.dma("gpsimd", ZAG[0:64, 0:1], self.dint["zout"].ap()[0:1, :])
        s.barrier()
        def v_seed(p):
            p.tensor_scalar_mul(ZAG[0:64, 0:1], ZAG[0:64, 0:1], HM[0:64, 0:1])
            p.drain()
            return p.tensor_copy(Zb[0:64, 0:1], ZAG[0:64, 0:1])
        s.on("vector", v_seed)
        vv = s.notify("vector", "bs_v")
        s.wait("scalar", "bs_v", vv)
        def s_scan2(p):
            r = None
            for t in range(NT):
                r = p.activation(Zb[0:64, t + 1:t + 2], Zb[0:64, t:t + 1],
                                 AF.Silu, bias=DD[0:64, t:t + 1], scale=1.0)
                self.snop(p)
            return r
        s.on("scalar", s_scan2)
        s.notify("scalar", "bs_s")
        s.barrier()
        OWB = self.at("OWB", [64, 1024], BF16, E + 32768)
        SB3 = self.at("SB3", [64, NT], BF16, E + 34816)
        s.dma("sync", WF[0:64, 0:1024], self.di["outwt"][:, :])
        s.dma_wait("vector", src="sync")
        def v_ow(p):
            p.tensor_copy(OWB[0:64, :], WF[0:64, 0:1024])
            p.drain()
            return p.tensor_tensor(SB3[0:64, 0:NT], Zb[0:64, 1:NT + 1],
                                   PB[0:64, 1:NT + 1], AL.add)
        s.on("vector", v_ow)
        s.barrier()
        if self.dbg_on:
            self.dbg("bst", SB3[0:64, 0:NT], [64, NT], BF16)
        for oc in range(NCH):
            def t_o(p, oc=oc):
                r = None
                for (t0, t1) in NP4:
                    r = p.matmul(PSW[:, t0:t1], OWB[0:64, 128 * oc:128 * oc + 128],
                                 SB3[0:64, t0:t1], start=True, stop=True)
                return r
            s.on("tensor", t_o)
            tv = s.notify("tensor", "bo_t")
            s.wait("vector", "bo_t", tv)
            def v_o(p, oc=oc):
                r = None
                for (t0, t1) in NP4:
                    r = p.scalar_tensor_tensor(x1t(oc)[:, t0:t1], PSW[:, t0:t1],
                                               OUTB[:, oc:oc + 1],
                                               x1t(oc)[:, t0:t1],
                                               AL.add, AL.add)
                return r
            s.on("vector", v_o)
            vv = s.notify("vector", "bo_v")
            s.wait("tensor", "bo_v", vv)
        s.barrier()
        if self.dbg_on:
            self.dbg("x3a", X1a[:, :, :], [128, 4, NT], F32)
            self.dbg("x3b", X1b[:, :, :], [128, 4, NT], F32)

    def stage_ffn2(self):
        self.ffn_apply("f2")


    def stage_store(self):
        s = self.s
        X1a, X1b = self.sb["X1a"], self.sb["X1b"]
        for c in range(NCH):
            t = X1a[:, c, :] if c < 4 else X1b[:, c - 4, :]
            s.dma("gpsimd", self.do["y"].ap().rearrange(
                "(c p) t -> p c t", p=128)[:, c, :], t)
        s.barrier()


def _stage_inputs(inputs):
    x = np.asarray(inputs["L0_attn_a_hidden_states"])
    import ml_dtypes
    bf = ml_dtypes.bfloat16
    eye = np.eye(128, dtype=np.float32)
    ej = np.zeros((128, 256), np.float32)
    for j in range(16):
        ej[:, 16 * j + j] = 1.0
    tri = np.triu(np.ones((128, 128), np.float32), 1)
    triincl = np.triu(np.ones((128, 128), np.float32), 0)
    masks = np.concatenate([tri, triincl, np.eye(128, dtype=np.float32),
                            np.tril(np.ones((128, 128), np.float32), -1)], 1)

    maps = []
    for c in range(8):
        b, h = c // 2, c % 2
        t0 = NT * h
        xf = np.zeros((D, NTH), np.float32)
        xf[:, HALO:] = x[b, t0:t0 + NT, :].T
        if h == 1:
            xf[:, 0:HALO] = x[b, t0 - HALO:t0, :].T
        wsm = np.zeros((D, 16), np.float32)
        wsm[:, 0:4] = np.asarray(inputs["L0_attn_a_Wb"])
        wsm[:, 4:8] = np.asarray(inputs["L0_attn_a_Wres"])
        cvw = np.zeros((D, 16), np.float32)
        cvw[:, 0:4] = np.asarray(inputs["L0_attn_a_wq_conv"])
        cvw[:, 4:8] = np.asarray(inputs["L0_attn_a_wk_conv"])
        cvw[:, 8:12] = np.asarray(inputs["L0_attn_a_wv_conv"])
        wl = np.asarray(inputs["L0_attn_a_fir_long"]).reshape(D, 64)
        ws = np.asarray(inputs["L0_attn_a_fir_short"]).reshape(D, 5)
        ii, oo = np.indices((128, 128))
        def toep(w, K, shift):
            j = ii - oo + (K - 1) - shift
            ok = (j >= 0) & (j < K)
            out = np.zeros((w.shape[0], 128, 128), np.float32)
            jj = np.clip(j, 0, K - 1)
            out = np.where(ok[None, :, :], w[:, jj], 0.0)
            return out
        mych = slice(128 * c, 128 * (c + 1))
        lng = np.concatenate([toep(wl[mych], 64, 0),
                              toep(wl[mych], 64, 128)], axis=2)
        sht = np.concatenate([toep(ws[mych], 5, 0),
                              toep(ws[mych], 5, 128)], axis=2)
        firt = np.stack([lng, sht], axis=1)
        wg2 = np.zeros((D, 8), np.float32)
        wg2[:, 0:4] = np.asarray(inputs["L0_attn_a_Wg2"])
        wog2 = np.zeros((512, 8), np.float32)
        wog2[:, 0:1] = np.asarray(inputs["L0_attn_a_Wog2"])
        vec16 = np.zeros((1, 16), np.float32)
        vec16[0, 0:4] = np.asarray(inputs["L0_attn_a_bg2"])
        vec16[0, 4] = np.asarray(inputs["L0_attn_a_bog2"])[0]
        vec16[0, 5] = np.asarray(inputs["L0_attn_a_logit_temp"])[0]
        vec16[0, 6:10] = np.asarray(inputs["L0_attn_a_bres"])
        vec16[0, 10:14] = np.asarray(inputs["L0_attn_a_conv_res_logit"])
        nw = np.asarray(inputs["L0_attn_a_norm_w"])
        nwc = np.zeros((128, NCH), np.float32)
        for cc in range(NCH):
            nwc[:, cc] = nw[(cc % 2) * 128:(cc % 2) * 128 + 128]
        m = {
            "x0": np.ascontiguousarray(xf),
            "firt": np.ascontiguousarray(firt).astype(bf),
            "wq": np.ascontiguousarray(np.asarray(inputs["L0_attn_a_Wq"])),
            "wk": np.ascontiguousarray(np.asarray(inputs["L0_attn_a_Wk"])),
            "wv": np.ascontiguousarray(np.asarray(inputs["L0_attn_a_Wv"])),
            "wo": np.ascontiguousarray(np.asarray(inputs["L0_attn_a_Wo"])),
            "ffnw": np.ascontiguousarray(np.asarray(inputs["L0_ffn_weight"]).T),
            "gamc": np.ascontiguousarray(
                np.asarray(inputs["L0_ffn_gamma"]).reshape(NCH, 128).T),
            "betc": np.ascontiguousarray(
                np.asarray(inputs["L0_ffn_beta"]).reshape(NCH, 128).T),
            "inwt": np.ascontiguousarray(np.asarray(inputs["L0_attn_b_in_w"]).T),
            "bconvw": np.ascontiguousarray(
                np.asarray(inputs["L0_attn_b_conv_w"]).reshape(NCH, 128, 4)
                .transpose(1, 0, 2)),
            "dtwt": np.ascontiguousarray(np.asarray(inputs["L0_attn_b_dt_w"]).T),
            "dtb": np.ascontiguousarray(
                np.asarray(inputs["L0_attn_b_dt_b"]).reshape(16, 1)),
            "trwt": np.ascontiguousarray(np.asarray(inputs["L0_attn_b_tr_w"]).T),
            "outwt": np.ascontiguousarray(np.asarray(inputs["L0_attn_b_out_w"]).T),
            "outbc": np.ascontiguousarray(
                np.asarray(inputs["L0_attn_b_out_b"]).reshape(NCH, 128).T),
            "inbc": np.ascontiguousarray(
                np.asarray(inputs["L0_attn_b_in_b"]).reshape(64, 1)),
            "hmask": np.array([[float(h)]], np.float32),
            "wg1": np.ascontiguousarray(np.asarray(inputs["L0_attn_a_Wg1"])),
            "wog1": np.ascontiguousarray(np.asarray(inputs["L0_attn_a_Wog1"])),
            "wg2b": np.ascontiguousarray(
                wg2.reshape(NCH, 128, 8).transpose(1, 0, 2)).astype(bf),
            "wog2b": np.ascontiguousarray(
                wog2.reshape(4, 128, 8).transpose(1, 0, 2)).astype(bf),
            "bg1c": np.ascontiguousarray(
                np.asarray(inputs["L0_attn_a_bg1"]).reshape(NCH, 128).T),
            "bog1c": np.ascontiguousarray(
                np.asarray(inputs["L0_attn_a_bog1"]).reshape(4, 128).T),
            "normwc": nwc,
            "vec16": vec16,
            "bg2row": np.tile(np.asarray(inputs["L0_attn_a_bg2"]),
                              16).reshape(1, 64).astype(np.float32),
            "wsmallb": np.ascontiguousarray(
                wsm.reshape(NCH, 128, 16).transpose(1, 0, 2)).astype(bf),
            "convw": np.ascontiguousarray(
                cvw.reshape(NCH, 128, 16).transpose(1, 0, 2)),
            "eye128": eye,
            "ejpat": ej.astype(bf),
            "masks": masks.astype(bf),
        }
        maps.append(m)
    return maps


def kernel(**inputs):
    mk = MK(stage=_STAGE, dbg=_DBG).build()
    maps = _stage_inputs(inputs)
    res = run_bass_kernel_spmd(mk.nc, maps, core_ids=list(range(8)))
    if _STAGE < 11:
        return res
    outs = [r["y"] for r in res.results]
    y = np.empty((B, L, D), np.float32)
    for c in range(8):
        b, h = c // 2, c % 2
        y[b, NT * h:NT * (h + 1), :] = outs[c].T
    return y


# version 09
# speedup vs baseline: 1.0288x; 1.0227x over previous
import numpy as np
import concourse.bass as bass
import concourse.mybir as mybir
from concourse.bass_utils import run_bass_kernel_spmd

F32 = mybir.dt.float32
BF16 = mybir.dt.bfloat16
AF = mybir.ActivationFunctionType
AL = mybir.AluOpType

B, L, D = 4, 4096, 1024
NT = 2048
HALO = 8
NTH = NT + HALO
NCH = 8
H = 4
DK = 256
MAGIC = 12582912.0
LN_EPS = 1e-5
L2_EPS = 1e-6
RMS_EPS = 1e-5
CHUNK = 128
NCK = L // CHUNK
PIECES = [(0, HALO)] + [(HALO + 512 * i, HALO + 512 * (i + 1)) for i in range(4)]
RPIECES = PIECES[1:]
ROWW = 2176
ENGINES = ("sync", "tensor", "vector", "scalar", "gpsimd")

_STAGE = 99
_DCK = 99999
_DBG = False


class Sched:
    def __init__(self, nc):
        self.nc = nc
        self.ops = {e: [] for e in ENGINES}
        self._sems = {}
        self.cnt = {}
        self.ph = self.sem("ph")
        self.nbar = 0
        self.dsem = {e: self.sem("dma_" + e) for e in ENGINES}
        self.dcnt = {e: 0 for e in ENGINES}
        self.dwait = {e: 0 for e in ENGINES}

    def canon(self, name):
        if len(name) > 2 and name[-2] == "_" and name[-1] in "vst":
            return "S_" + name[-1]
        return name

    def sem(self, name):
        name = self.canon(name)
        if name not in self._sems:
            self._sems[name] = self.nc.semaphore(name).__enter__()
            self.cnt[name] = 0
        return self._sems[name]

    def on(self, eng, fn, kind="data"):
        self.ops[eng].append([fn, kind, False])

    def raw(self, eng, method, *a, **kw):
        self.on(eng, lambda p: getattr(p, method)(*a, **kw))

    def carrier(self, eng, p):
        ca = self.carr
        if eng == "vector":
            return p.memset(ca[0:1, 0:1], 0.0)
        if eng == "scalar":
            return p.memzero(ca[0:1, 1:2])
        if eng == "gpsimd":
            return p.memset(ca[0:1, 2:3], 0.0)
        if eng == "tensor":
            return p.ldweights(self.eyeb_carr)
        return p.nop()

    def notify(self, eng, name, by=1):
        sm = self.sem(name)
        name = self.canon(name)
        self.cnt[name] += by
        ent = self.ops[eng][-1] if self.ops[eng] else None
        if ent is not None and ent[1] == "data" and not ent[2]:
            prev = ent[0]
            def fn(p, prev=prev, sm=sm, by=by):
                r = prev(p)
                (r if r is not None else self.carrier(eng, p)).then_inc(sm, by)
            ent[0], ent[2] = fn, True
        else:
            def fn(p, sm=sm, by=by):
                self.carrier(eng, p).then_inc(sm, by)
            self.on(eng, fn, kind="seq")
        return self.cnt[name]

    def sep(self, eng):
        self.on(eng, lambda p: p.drain(), kind="seq")

    def wait(self, eng, name, val=None):
        sm = self.sem(name)
        v = self.cnt[self.canon(name)] if val is None else val
        self.on(eng, lambda p: p.wait_ge(sm, v), kind="seq")
        return v

    def dma(self, eng, out, in_):
        def fn(p, _o=out, _i=in_):
            o = _o() if callable(_o) else _o
            i = _i() if callable(_i) else _i
            return p.dma_start(out=o, in_=i).then_inc(self.dsem[eng], 16)
        self.on(eng, fn, kind="seq")
        self.dcnt[eng] += 16
        return self.dcnt[eng]

    def dma_wait(self, eng, upto=None, src=None):
        q = src or eng
        val = self.dcnt[q] if upto is None else upto
        if q == eng:
            if val > self.dwait[q]:
                self.dwait[q] = val
                self.on(eng, lambda p, v=val: p.wait_ge(self.dsem[q], v),
                        kind="seq")
        else:
            self.on(eng, lambda p, v=val: p.wait_ge(self.dsem[q], v),
                    kind="seq")
        return val

    def barrier(self):
        self.nbar += 1
        tgt = 5 * self.nbar
        for e in ENGINES:
            self.dma_wait(e)
            self.notify(e, "ph")
            self.wait(e, "ph", tgt)
        assert self.cnt["ph"] == tgt

    def emit(self):
        with self.nc.Block() as block:
            def mk(eng_name):
                def body(proxy):
                    for ent in self.ops[eng_name]:
                        ent[0](proxy)
                body.__name__ = "stream_" + eng_name
                return body
            block.sync(mk("sync"))
            block.tensor(mk("tensor"))
            block.vector(mk("vector"))
            block.scalar(mk("scalar"))
            block.gpsimd(mk("gpsimd"))


class MK:
    def __init__(self, stage=99, dbg=False, sim=False):
        self.sim_mode = sim
        self.nc = bass.Bass("TRN2", target_bir_lowering=False)
        self.snop = ((lambda p: p.drain()) if sim else
                     (lambda p: p.nop(cycle_cnt=128, nofuse=True)))
        self.s = Sched(self.nc)
        self.stage = stage
        self.dbg_on = dbg
        self.di, self.do, self.dint, self.sb, self.ps = {}, {}, {}, {}, {}
        self.sb_off = {}
        self.cursor = {}
        self.dbg_list = []
        self.dt_done = {}

    def inp(self, name, shape, dtype=F32):
        self.di[name] = self.nc.dram_tensor(name, list(shape), dtype,
                                            kind="ExternalInput")
        return self.di[name]

    def outp(self, name, shape, dtype=F32):
        self.do[name] = self.nc.dram_tensor(name, list(shape), dtype,
                                            kind="ExternalOutput")
        return self.do[name]

    def dram(self, name, shape, dtype=F32):
        self.dint[name] = self.nc.dram_tensor(name, list(shape), dtype)
        return self.dint[name]

    def at(self, name, shape, dtype, off):
        nbytes = int(np.prod(shape[1:])) * mybir.dt.size(dtype)
        assert off % 32 == 0, (name, off)
        assert off >= 24576 and off + nbytes <= 224 * 1024, (name, off, nbytes)
        self.sb[name] = self.nc.alloc_sbuf_tensor_at(name, list(shape), dtype,
                                                     offset=off)
        self.sb_off[name] = (off, off + nbytes)
        return self.sb[name]

    def fix(self, name, shape, dtype):
        off = self.cursor.setdefault("fix", 24576)
        nbytes = int(np.prod(shape[1:])) * mybir.dt.size(dtype)
        nbytes = (nbytes + 31) // 32 * 32
        self.cursor["fix"] = off + nbytes
        return self.at(name, shape, dtype, off)

    def psum(self, name, shape, dtype=F32):
        self.ps[name] = self.nc.alloc_psum_tensor(name, list(shape), dtype)
        return self.ps[name]

    def dbg(self, name, ap, shape, dtype=F32):
        if not self.dbg_on:
            return
        t = self.outp("dbg_" + name, shape, dtype)
        self.s.barrier()
        self.s.dma("gpsimd", t[:, :] if len(shape) == 2 else t[:], ap)
        self.s.barrier()

    ERA = None

    def build(self):
        s = self.s
        self.declare_common()
        self.load_constants()
        self.stage_ln1()
        if self.stage >= 2:
            self.stage_proj_qkv()
        if self.stage >= 3:
            self.stage_delta_prep()
        if self.stage >= 4:
            self.stage_delta()
        if self.stage >= 5:
            self.stage_fir()
        if self.stage >= 4:
            self.stage_delta_finish()
        if self.stage >= 8:
            self.ffn_ternarize()
        if self.stage >= 5:
            self.stage_fir_finish()
        if self.stage >= 6:
            self.stage_gates()
        if self.stage >= 7:
            self.stage_mix_out()
        if self.stage >= 8:
            self.stage_ffn1()
        if self.stage >= 9:
            self.stage_attn_b()
        if self.stage >= 10:
            self.stage_ffn2()
        if self.stage >= 11:
            self.stage_store()
        s.barrier()
        s.emit()
        return self

    def declare_common(self):
        self.inp("x0", [D, NTH])
        self.inp("eye128", [128, 128])
        self.inp("ejpat", [128, 256], BF16)
        self.inp("masks", [128, 512], BF16)
        self.inp("wq", [D, D])
        self.inp("wk", [D, D])
        self.inp("wv", [D, D])
        self.inp("wsmallb", [128, NCH, 16], BF16)
        self.inp("convw", [128, NCH, 16])
        self.inp("firt", [128, 2, 128, 256], BF16)
        self.inp("wo", [D, D])
        self.inp("ffnw", [D, D])
        self.inp("gamc", [128, NCH])
        self.inp("inwt", [D, 64])
        self.inp("bconvw", [128, NCH, 4])
        self.inp("dtwt", [D, 16])
        self.inp("dtb", [16, 1])
        self.inp("trwt", [16, 64])
        self.inp("outwt", [64, D])
        self.inp("outbc", [128, NCH])
        self.inp("inbc", [64, 1])
        self.inp("hmask", [1, 1])
        self.inp("betc", [128, NCH])
        self.inp("wg1", [1040, D])
        self.inp("wg2b", [128, NCH, 8], BF16)
        self.inp("wog1", [1032, 512])
        self.inp("wog2b", [128, 4, 8], BF16)
        self.inp("bg1c", [128, NCH])
        self.inp("bog1c", [128, 4])
        self.inp("normwc", [128, NCH])
        self.inp("vec16", [1, 16])
        self.inp("bg2row", [1, 64])
        self.outp("y", [D, NT])

    def load_constants(self):
        s = self.s
        s.carr = self.fix("CARR", [128, 8], F32)
        s.eyeb_carr = None
        self.fix("EYE", [128, 128], F32)
        self.fix("MASKS", [128, 512], BF16)
        self.fix("EJ", [128, 256], BF16)
        self.fix("ROWT", [16, ROWW], F32)
        self.fix("ROWP", [8, ROWW], F32)
        self.fix("RP", [128, 512], F32)
        self.fix("BCA", [128, NTH], F32)
        self.fix("BCB", [128, NTH], F32)
        self.fix("SC", [128, 1056], F32)
        self.fix("PRSV", [128, 32], F32)
        self.ERA = (self.cursor["fix"] + 127) // 128 * 128
        self.psum("PSW", [128, 2048])
        self.psum("PSR", [128, 512])
        self.psum("PST", [128, 1024], BF16)
        self.psum("PSS", [128, 1024])
        s.dma("sync", self.sb["EYE"][:, :], self.di["eye128"][:, :])
        s.dma("sync", self.sb["MASKS"][:, :], self.di["masks"][:, :])
        s.dma("sync", self.sb["EJ"][:, :], self.di["ejpat"][:, :])
        def v_init0(p):
            p.memset(self.sb["ROWT"][0:16, 0:ROWW], 0.0)
            return p.memset(self.sb["RP"][:, :], 0.0)
        s.on("vector", v_init0)
        s.eyeb_carr = self.sb["MASKS"][:, 256:384]
        self.dyx = {}
        def g_base(p):
            cid = p.compute_val(self.nc.partition_id())
            base = p.snap((cid % 2) * 512)
            for i in range(2):
                for kc in range(2):
                    self.dyx[(i, kc)] = base + (256 * i + 128 * kc)
            self.dyx["tok"] = base * 4
            self.dyx["rank"] = p.snap(cid % 8)
            self.dyx["oth"] = p.snap((cid + 1) % 2)
            self.dyx["o512"] = p.snap(((cid + 1) % 2) * 512)
            self.dyx["otok"] = p.snap(((cid + 1) % 2) * 2048)
            return None
        s.on("gpsimd", g_base, kind="seq")
        self.dys = {}
        def s_base(p):
            cid = p.compute_val(self.nc.partition_id())
            base = p.snap((cid % 2) * 512)
            self.dys[(0, 0)] = base
            self.dys["tok"] = base * 4
            self.dys["oth"] = p.snap((cid + 1) % 2)
            self.dys["o512"] = p.snap(((cid + 1) % 2) * 512)
            self.dys["otok"] = p.snap(((cid + 1) % 2) * 2048)
            return None
        s.on("sync", s_base, kind="seq")
        s.barrier()

    def row_reduce(self, jobs, prep=None):
        s = self.s
        PSR = self.ps["PSR"]
        n = 1 + max(j[0] for j in jobs)
        assert n <= 16
        last_t = None
        for (t0, t1) in PIECES:
            w = t1 - t0
            if prep is not None:
                pres = prep(t0, w)
                vals = {}
                for eng, fn in pres:
                    if last_t is not None:
                        s.wait(eng, "rr_t", last_t)
                    s.on(eng, fn)
                    vals[eng] = s.notify(eng, "rr_" + eng[0])
                for eng, v in vals.items():
                    s.wait("tensor", "rr_" + eng[0], v)
            def t_fn(p, t0=t0, w=w):
                pairs = []
                for (pr, mf) in jobs:
                    for c in range(NCH):
                        m = mf(c, t0, w)
                        if m is not None:
                            pairs.append((pr, m))
                r = None
                for i, (pr, m) in enumerate(pairs):
                    r = p.matmul(PSR[0:n, 0:w],
                                 self.sb["EJ"][:, 16 * pr:16 * pr + n],
                                 m, start=(i == 0), stop=(i == len(pairs) - 1))
                return r
            s.on("tensor", t_fn)
            tv = s.notify("tensor", "rr_t")
            last_t = tv
            s.wait("vector", "rr_t", tv)
            def v_fn(p, t0=t0, w=w, n=n):
                return p.tensor_copy(self.sb["ROWT"][0:n, t0:t0 + w],
                                     PSR[0:n, 0:w])
            s.on("vector", v_fn)
            vt = s.notify("vector", "rr_v")
            s.wait("tensor", "rr_v", vt)
        def vz(p, n=n):
            p.memset(self.sb["ROWT"][0:n, NTH:ROWW], 0.0)
        s.on("vector", vz)

    def rows_to_rp(self, rows, rp_slot0=0):
        s = self.s
        for i, r in enumerate(rows):
            sl = rp_slot0 + i
            s.dma("gpsimd", self.sb["RP"][:, 17 * sl:17 * sl + 17],
                  self.sb["ROWT"][r:r + 1, 0:ROWW])

    def rp_to_rows(self, slots, rows):
        s = self.s
        for sl, r in zip(slots, rows):
            s.dma("gpsimd", self.sb["ROWT"][r:r + 1, 0:ROWW],
                  self.sb["RP"][:, 17 * sl:17 * sl + 17])

    def bcast_row(self, row, dst, width=NTH, tile=None):
        s = self.s
        tile = self.sb["ROWT"] if tile is None else tile
        rb = self.dint.get("rowbounce")
        if rb is None:
            rb = self.dram("rowbounce", [16, ROWW], F32)
        s.dma("gpsimd", rb[row:row + 1, 0:width],
              tile[row:row + 1, 0:width])
        s.dma_wait("gpsimd")
        def dst_ap():
            d = dst() if callable(dst) else dst
            return d[:, 0:width]
        s.dma("gpsimd", dst_ap,
              lambda: rb[row:row + 1, 0:width].partition_broadcast(128))

    def stage_ln1(self):
        s = self.s
        X = self.at("X", [128, NCH, NTH], F32, self.ERA)
        XN = self.at("XN", [128, NCH, NTH], BF16, self.ERA + 65792)
        self.at("SCB1", [128, NCH, 512], BF16, self.ERA + 98688)
        self.at("SCB2", [128, NCH, 512], BF16, self.ERA + 106880)
        s.dma("sync", X[:, :, :],
              self.di["x0"].ap().rearrange("(c p) t -> p c t", p=128))
        s.barrier()
        SCB1, SCB2 = self.sb["SCB1"], self.sb["SCB2"]
        def prep(t0, w):
            def v_fn(p, t0=t0, w=w):
                return p.tensor_copy(SCB1[:, :, 0:w], X[:, :, t0:t0 + w])
            def s_fn(p, t0=t0, w=w):
                return p.activation(SCB2[:, :, 0:w], X[:, :, t0:t0 + w],
                                    AF.Square)
            return [("vector", v_fn), ("scalar", s_fn)]
        self.row_reduce([(0, lambda c, t0, w: SCB1[:, c, 0:w]),
                         (1, lambda c, t0, w: SCB2[:, c, 0:w])], prep=prep)
        s.barrier()
        self.rows_to_rp([0, 1])
        s.barrier()
        RP = self.sb["RP"]
        mu, msq, var = RP[:, 0:17], RP[:, 17:34], RP[:, 34:51]
        t2 = RP[:, 51:68]
        def v_math(p):
            p.tensor_scalar_mul(mu, mu, 1.0 / D)
            p.tensor_scalar_mul(msq, msq, 1.0 / D)
            p.drain()
            p.tensor_tensor(t2, mu, mu, AL.mult)
            p.drain()
            p.tensor_tensor(var, msq, t2, AL.subtract)
            p.drain()
            p.tensor_scalar_add(var, var, LN_EPS)
            p.drain()
            return p.reciprocal(var, var)
        s.on("vector", v_math)
        sv = s.notify("vector", "ln_v")
        s.wait("scalar", "ln_v", sv)
        def sc_math(p):
            return p.activation(var, var, AF.Sqrt)
        s.on("scalar", sc_math)
        s.notify("scalar", "ln_s")
        s.barrier()
        self.rp_to_rows([0, 2], [0, 1])
        s.barrier()
        self.bcast_row(0, self.sb["BCA"])
        self.bcast_row(1, self.sb["BCB"])
        s.barrier()
        SC = self.sb["SC"]
        def v_apply(p):
            r = None
            for c in range(NCH):
                for (t0, t1) in ((0, 1024), (1024, NTH)):
                    w = t1 - t0
                    p.tensor_tensor(SC[:, 0:w], X[:, c, t0:t1],
                                    self.sb["BCA"][:, t0:t1], AL.subtract)
                    p.drain()
                    r = p.tensor_tensor(XN[:, c, t0:t1], SC[:, 0:w],
                                        self.sb["BCB"][:, t0:t1], AL.mult)
                    p.drain()
            return r
        s.on("vector", v_apply)
        s.barrier()
        self.dbg("xn", XN[:, :, :], [128, NCH, NTH], BF16)

    def proj_big(self, w_dram, out_fn, oc_range=range(NCH), halo=True,
                 extra=None, xn=None):
        s = self.s
        WB, WF = self.sb["WB"], self.sb["WF32"]
        XN = self.sb["XN"] if xn is None else xn
        PSW, PSR = self.ps["PSW"], self.ps["PSR"]
        for kc in range(NCH):
            s.dma("sync", WF[:, 0:1024],
                  w_dram[128 * kc:128 * (kc + 1), :])
            s.dma_wait("vector", src="sync")
            def vc(p, kc=kc):
                return p.tensor_copy(WB[:, kc, :], WF[:, 0:1024])
            s.on("vector", vc)
            vv = s.notify("vector", "pw_v")
            if kc < NCH - 1:
                s.wait("sync", "pw_v", vv)
        s.wait("tensor", "pw_v")
        pieces = PIECES if halo else RPIECES
        for oc in oc_range:
            def t_fn(p, oc=oc):
                r = None
                for kc in range(NCH):
                    lhs = WB[:, kc, 128 * oc:128 * (oc + 1)]
                    for pi, (t0, t1) in enumerate(pieces):
                        w = t1 - t0
                        if w <= 16:
                            dst = PSR[0:128, 384:384 + w]
                        else:
                            dst = PSW[:, (t0 - HALO):(t0 - HALO) + w]
                        r = p.matmul(dst, lhs, XN[:, kc, t0:t1],
                                     start=(kc == 0), stop=(kc == NCH - 1))
                return r
            s.on("tensor", t_fn)
            tv = s.notify("tensor", "pw_t")
            out_fn(oc, tv)

    def conv4_silu(self, taps_base, src, dst, sign_extra=None):
        s = self.s
        PSW = self.ps["PSW"]
        EYEB = self.sb["EYEB"]
        CW = self.sb["CONVW"]
        DIAG = self.sb["DIAG"]
        for c in range(NCH):
            def v_diag(p, c=c):
                r = None
                for j in range(4):
                    r = p.tensor_scalar(DIAG[:, j, :], EYEB[:, :],
                                        CW[:, c, taps_base + j:taps_base + j + 1],
                                        None, AL.mult)
                return r
            s.on("vector", v_diag)
            dv = s.notify("vector", "cv_v")
            s.wait("tensor", "cv_v", dv)
            def t_fn(p, c=c):
                r = None
                for pi, (t0, t1) in enumerate(RPIECES):
                    w = t1 - t0
                    for j in range(4):
                        sh = t0 - 3 + j
                        r = p.matmul(PSW[:, (t0 - HALO):(t0 - HALO) + w],
                                     DIAG[:, j, :], src[:, c, sh:sh + w],
                                     start=(j == 0), stop=(j == 3))
                return r
            s.on("tensor", t_fn)
            tv = s.notify("tensor", "cv_t")
            s.wait("scalar", "cv_t", tv)
            def sc_fn(p, c=c):
                r = None
                for (t0, t1) in RPIECES:
                    w = t1 - t0
                    r = p.activation(dst[:, c, t0:t1],
                                     PSW[:, (t0 - HALO):(t0 - HALO) + w],
                                     AF.Silu)
                return r
            s.on("scalar", sc_fn)
            sv = s.notify("scalar", "cv_s")
            s.wait("tensor", "cv_s", sv)
            s.wait("vector", "cv_s", sv)

    def stage_proj_qkv(self):
        s = self.s
        E = self.ERA
        QL = self.at("QL", [128, NCH, NTH], BF16, E)
        TIN = self.at("TIN", [128, NCH, NTH], BF16, E + 32896)
        WB = self.at("WB", [128, NCH, 1024], BF16, E + 98688)
        WF32 = self.at("WF32", [128, 1024], F32, E + 115072)
        BB4 = self.at("BB4", [128, 4, NTH], BF16, E + 119168)
        SCL = self.at("SCL", [128, 4, NTH], BF16, E + 135616)
        EYEB = self.at("EYEB", [128, 128], BF16, E + 152064)
        CONVW = self.at("CONVW", [128, NCH, 16], F32, E + 152320)
        WSM = self.at("WSM", [128, NCH, 16], BF16, E + 152832)
        DIAG = self.at("DIAG", [128, 4, 128], BF16, E + 153088)
        self.at("L2SC", [128, NCH, 512], BF16, E + 154112)
        self.dram("xin", [4, D, NT], BF16)
        self.dram("xout", [2, 4, D, NT], BF16)
        self.dram("fir_in", [NCH, 128, NT], BF16)
        self.dram("vdir", [NCH, 128, NT], BF16)
        s.dma("sync", EYEB[:, :], self.di["masks"][:, 256:384])
        s.dma("sync", CONVW[:, :, :], self.di["convw"].ap())
        s.dma("sync", WSM[:, :, :], self.di["wsmallb"].ap())
        s.barrier()
        def v_init(p):
            p.memset(QL[:, :, 0:HALO], 0.0)
            p.memset(self.sb["ROWP"][0:8, 0:ROWW], 0.0)
            return p.memset(TIN[:, :, 0:HALO], 0.0)
        s.on("vector", v_init)
        s.barrier()

        def evac_plain(oc, tv):
            s.wait("scalar", "pw_t", tv)
            def sc(p, oc=oc):
                r = None
                for (t0, t1) in PIECES:
                    w = t1 - t0
                    srcp = (self.ps["PSR"][0:128, 384:384 + w] if w <= 16 else
                            self.ps["PSW"][:, (t0 - HALO):(t0 - HALO) + w])
                    r = p.copy(QL[:, oc, t0:t1], srcp)
                return r
            s.on("scalar", sc)
            sv = s.notify("scalar", "pw_s")
            s.wait("tensor", "pw_s", sv)

        self.proj_big(self.di["wv"].ap(), evac_plain)
        PSR = self.ps["PSR"]
        for (t0, t1) in RPIECES:
            w = t1 - t0
            def t_fn(p, t0=t0, w=w):
                r = None
                for kc in range(NCH):
                    r = p.matmul(PSR[0:16, 0:w], WSM[:, kc, :],
                                 self.sb["XN"][:, kc, t0:t0 + w],
                                 start=(kc == 0), stop=(kc == NCH - 1))
                return r
            s.on("tensor", t_fn)
            tv = s.notify("tensor", "sm_t")
            s.wait("vector", "sm_t", tv)
            def v_fn(p, t0=t0, w=w):
                return p.tensor_copy(self.sb["ROWP"][0:8, t0:t0 + w],
                                     PSR[0:8, 0:w])
            s.on("vector", v_fn)
            vv = s.notify("vector", "sm_v")
            s.wait("tensor", "sm_v", vv)
        s.barrier()
        def sc_beta(p):
            return p.activation(self.sb["ROWP"][0:4, 0:NTH], self.sb["ROWP"][0:4, 0:NTH],
                                AF.Sigmoid)
        s.on("scalar", sc_beta)
        s.notify("scalar", "sm_s")
        s.barrier()
        for h4 in range(4):
            self.bcast_row(h4, self.sb["BCA"], tile=self.sb["ROWP"])
            s.barrier()
            def vc(p, h4=h4):
                return p.tensor_copy(BB4[:, h4, :], self.sb["BCA"][:, :])
            s.on("vector", vc)
            s.barrier()
        self.conv4_silu(8, QL, TIN)
        s.barrier()
        for c in range(NCH):
            s.dma("gpsimd", self.dint["fir_in"][c, :, :], TIN[:, c, HALO:NTH])
            s.dma("gpsimd", self.dint["vdir"][c, :, :], TIN[:, c, HALO:NTH])
        def v_vb(p):
            r = None
            for c in range(NCH):
                r = p.tensor_tensor(QL[:, c, :], TIN[:, c, :],
                                    BB4[:, c // 2, :], AL.mult)
            return r
        s.on("vector", v_vb)
        s.barrier()
        for c in range(NCH):
            s.dma("gpsimd", self.dint["xin"][3, 128 * c:128 * (c + 1), :],
                  QL[:, c, HALO:NTH])
        s.barrier()

        self.proj_big(self.di["wq"].ap(), evac_plain)
        s.barrier()
        self.conv4_silu(0, QL, TIN)
        s.barrier()
        self.l2rows(TIN)
        s.barrier()
        for h4 in range(4):
            self.bcast_row(h4, self.sb["BCA"])
            s.barrier()
            def vc(p, h4=h4):
                return p.tensor_copy(SCL[:, h4, :], self.sb["BCA"][:, :])
            s.on("vector", vc)
            s.barrier()
        def v_qn(p):
            r = None
            for c in range(NCH):
                r = p.tensor_tensor(QL[:, c, :], TIN[:, c, :],
                                    SCL[:, c // 2, :], AL.mult)
            return r
        s.on("vector", v_qn)
        s.barrier()
        for c in range(NCH):
            s.dma("gpsimd", self.dint["xin"][0, 128 * c:128 * (c + 1), :],
                  QL[:, c, HALO:NTH])
        s.barrier()

        self.proj_big(self.di["wk"].ap(), evac_plain)
        s.barrier()
        self.conv4_silu(4, QL, TIN)
        s.barrier()
        self.l2rows(TIN)
        s.barrier()
        for h4 in range(4):
            self.bcast_row(h4, self.sb["BCA"])
            s.barrier()
            def vc(p, h4=h4):
                return p.tensor_copy(SCL[:, h4, :], self.sb["BCA"][:, :])
            s.on("vector", vc)
            s.barrier()
        def v_kn(p):
            r = None
            for c in range(NCH):
                r = p.tensor_tensor(TIN[:, c, :], TIN[:, c, :],
                                    SCL[:, c // 2, :], AL.mult)
            return r
        s.on("vector", v_kn)
        s.barrier()
        for c in range(NCH):
            s.dma("gpsimd", self.dint["xin"][1, 128 * c:128 * (c + 1), :],
                  TIN[:, c, HALO:NTH])
        def v_kb(p):
            r = None
            for c in range(NCH):
                r = p.tensor_tensor(QL[:, c, :], TIN[:, c, :],
                                    BB4[:, c // 2, :], AL.mult)
            return r
        s.on("vector", v_kb)
        s.barrier()
        for c in range(NCH):
            s.dma("gpsimd", self.dint["xin"][2, 128 * c:128 * (c + 1), :],
                  QL[:, c, HALO:NTH])
        s.barrier()
        if self.dbg_on:
            self.dbg("qn", self.dint["xin"][0, :, :], [D, NT], BF16)
            self.dbg("kn", self.dint["xin"][1, :, :], [D, NT], BF16)
            self.dbg("kb", self.dint["xin"][2, :, :], [D, NT], BF16)
            self.dbg("vb", self.dint["xin"][3, :, :], [D, NT], BF16)
            self.dbg("vin", self.dint["fir_in"].ap(), [NCH, 128, NT], BF16)
            self.dbg("beta", self.sb["ROWP"][0:4, 0:NTH], [4, NTH], F32)

    def l2rows(self, TIN):
        s = self.s
        PSR = self.ps["PSR"]
        SCB2 = self.sb["L2SC"]
        last_t = None
        for (t0, t1) in RPIECES:
            w = t1 - t0
            if last_t is not None:
                s.wait("vector", "l2_t", last_t)
            def v_sq(p, t0=t0, w=w):
                return p.tensor_tensor(SCB2[:, :, 0:w], TIN[:, :, t0:t0 + w],
                                       TIN[:, :, t0:t0 + w], AL.mult)
            s.on("vector", v_sq)
            vv = s.notify("vector", "l2_v")
            s.wait("tensor", "l2_v", vv)
            def t_fn(p, t0=t0, w=w):
                r = None
                for c in range(NCH):
                    r = p.matmul(PSR[0:4, 0:w],
                                 self.sb["EJ"][:, 16 * (c // 2):16 * (c // 2) + 4],
                                 SCB2[:, c, 0:w],
                                 start=(c == 0), stop=(c == NCH - 1))
                return r
            s.on("tensor", t_fn)
            tv = s.notify("tensor", "l2_t")
            last_t = tv
            s.wait("vector", "l2_t", tv)
            def v_cp(p, t0=t0, w=w):
                return p.tensor_copy(self.sb["ROWT"][0:4, t0:t0 + w],
                                     PSR[0:4, 0:w])
            s.on("vector", v_cp)
            vv2 = s.notify("vector", "l2_v")
            s.wait("tensor", "l2_v", vv2)
        s.barrier()
        def vz2(p):
            return p.memset(self.sb["ROWT"][0:4, 0:HALO], 0.0)
        s.on("vector", vz2)
        s.barrier()
        self.rows_to_rp([0, 1, 2, 3])
        s.barrier()
        RP = self.sb["RP"]
        def v_m(p):
            p.tensor_scalar_add(RP[:, 0:68], RP[:, 0:68], L2_EPS)
            p.drain()
            return p.reciprocal(RP[:, 0:68], RP[:, 0:68])
        s.on("vector", v_m)
        vv = s.notify("vector", "l2m_v")
        s.wait("scalar", "l2m_v", vv)
        def s_m(p):
            return p.activation(RP[:, 0:68], RP[:, 0:68], AF.Sqrt)
        s.on("scalar", s_m)
        s.notify("scalar", "l2m_s")
        s.barrier()
        self.rp_to_rows([0, 1, 2, 3], [0, 1, 2, 3])
        s.barrier()

    def stage_delta_prep(self):
        s = self.s
        self.dram("xsend", [4, 512, NT], BF16)
        self.dram("xrecv", [2, 4, 512, NT], BF16)
        def g_xs(p):
            src = self.dint["xin"].ap()[:, bass.ds(self.dys["o512"], 512), :]
            return p.dma_start(out=self.dint["xsend"].ap(),
                               in_=src).then_inc(s.dsem["sync"], 16)
        s.on("sync", g_xs, kind="seq")
        s.dcnt["sync"] += 16
        s.dma_wait("gpsimd", src="sync")
        def cc(p):
            return p.collective_compute(
                "AllGather", AL.bypass,
                replica_groups=[[0, 1], [2, 3], [4, 5], [6, 7]],
                ins=[self.dint["xsend"].ap().opt()],
                outs=[self.dint["xrecv"].ap().opt()])
        s.on("gpsimd", cc, kind="data")
        self.ccv_xin = s.notify("gpsimd", "ccd")
        self.dram("fir_x8", [8, NCH, 128, NT], BF16)
        def cc2(p):
            return p.collective_compute(
                "AllGather", AL.bypass,
                replica_groups=[list(range(8))],
                ins=[self.dint["fir_in"].ap().opt()],
                outs=[self.dint["fir_x8"].ap().opt()])
        s.on("gpsimd", cc2, kind="data")
        self.ccv_firx8 = s.notify("gpsimd", "ccd")
        s.wait("sync", "ccd", self.ccv_xin)

    def stage_delta(self):
        s, nc = self.s, self.nc
        E = self.ERA
        MB = self.sb["MASKS"]
        M_SU, M_UI, M_EYE, M_SL = (MB[:, 0:128], MB[:, 128:256],
                                   MB[:, 256:384], MB[:, 384:512])
        DOUT = self.at("DOUT", [128, 4, 4096], BF16, E)
        RING = self.at("RING", [128, 4, 4, 2, 128], BF16, E + 98688)
        W16 = self.at("W16", [128, 16, 128], BF16, E + 106880)
        St = self.at("St", [128, 2, 512], BF16, E + 110976)
        SF = self.at("SF", [128, 2, 512], F32, E + 119232)
        U2T = self.at("U2T", [128, 2, 256], BF16, E + 113024)
        KTM = self.at("KTM", [128, 2, 256], BF16, E + 114048)
        YPP = self.at("YPP", [128, 2, 2, 256], BF16, E + 115072)
        OTM = self.at("OTM", [128, 2, 256], BF16, E + 117120)
        KBSS = self.at("KBSS", [128, 256], F32, E + 118144)
        self.dram("dret_in", [512, 4096], BF16)
        self.dram("dret_out", [2, 512, 4096], BF16)
        self.dram("xmy", [4, 512, 4096], BF16)
        PSW, PSR, PSS = self.ps["PSW"], self.ps["PSR"], self.ps["PSS"]
        psSQ = PSW[:, 0:128]
        psAP = PSW[:, 512:768]
        psKBS = PSW[:, 1024:1280]
        psT = self.ps["PST"][:, 0:256]
        psKT = self.ps["PST"][:, 512:768]
        psGA = PSR[:, 0:384]

        dyx = self.dyx

        xmy = self.dint["xmy"]
        def g_st1(p):
            dd = self.dys
            src = self.dint["xin"].ap()[:, bass.ds(dd[(0, 0)], 512), :]
            return p.dma_start(out=xmy.ap()[:, :, bass.ds(dd["tok"], 2048)],
                               in_=src).then_inc(s.dsem["sync"], 16)
        s.on("sync", g_st1, kind="seq")
        s.dcnt["sync"] += 16
        def g_st2(p):
            dd = self.dys
            src = self.dint["xrecv"].ap()[bass.ds(dd["oth"], 1), :, :, :]
            src = src.rearrange("s a c t -> (s a) c t")
            return p.dma_start(out=xmy.ap()[:, :, bass.ds(dd["otok"], 2048)],
                               in_=src).then_inc(s.dsem["sync"], 16)
        s.on("sync", g_st2, kind="seq")
        s.dcnt["sync"] += 16
        s.barrier()

        def load_chunk2(i, ck, slot):
            sm = s.sem("rs%d" % slot)
            for kc in range(2):
                def fn(p, i=i, ck=ck, kc=kc, slot=slot, sm=sm):
                    src = xmy.ap()[:, 256 * i + 128 * kc:256 * i + 128 * kc + 128,
                                   128 * ck:128 * ck + 128]
                    src = src.transpose([1, 0, 2])
                    return p.dma_start(out=RING[:, slot, :, kc, :],
                                       in_=src).then_inc(sm, 16)
                s.on("sync", fn, kind="seq")
                s.cnt["rs%d" % slot] += 16
            return s.cnt["rs%d" % slot]

        NH = 2
        loaded = {}
        ring_of = {}
        nslot = [0]
        def ensure_loaded(i, ck):
            key = (i, ck)
            if key not in loaded:
                slot = nslot[0] % 4
                nslot[0] += 1
                ring_of[key] = slot
                loaded[key] = load_chunk2(i, ck, slot)
            return loaded[key], ring_of[key]

        order = [(ck, i) for ck in range(NCK) for i in range(NH)]
        order = order[:_DCK]
        for (ck, i) in order[:3]:
            ensure_loaded(i, ck)

        for oi, (ck, i) in enumerate(order):
            dv, slot = ensure_loaded(i, ck)
            if oi + 3 < len(order):
                cka, ia = order[oi + 3]
                if oi >= 1:
                    s.wait("sync", "dt", self.dt_done.get(order[oi - 1], 0))
                ensure_loaded(ia, cka)
            Q = lambda kc, slot=slot: RING[:, slot, 0, kc, :]
            K = lambda kc, slot=slot: RING[:, slot, 1, kc, :]
            KB = lambda kc, slot=slot: RING[:, slot, 2, kc, :]
            VB = lambda kc, slot=slot: RING[:, slot, 3, kc, :]
            first = (ck == 0)

            s.wait("tensor", "rs%d" % slot, dv)
            def t1(p, i=i, ck=ck, slot=slot, first=first, Q=Q, K=K, KB=KB, VB=VB):
                r = None
                for kc in range(2):
                    p.matmul(psGA[0:128, 0:128], K(kc), Q(kc), start=(kc == 0),
                             stop=(kc == 1))
                for kc in range(2):
                    p.matmul(psGA[0:128, 128:256], K(kc), KB(kc),
                             start=(kc == 0), stop=(kc == 1))
                for kc in range(2):
                    p.matmul(psGA[0:128, 256:384], KB(kc), K(kc),
                             start=(kc == 0), stop=(kc == 1))
                for kc in range(2):
                    p.transpose(psT[:, 128 * kc:128 * kc + 128], VB(kc),
                                self.sb["EYEB"][:, :])
                for kc in range(2):
                    r = p.transpose(psKT[:, 128 * kc:128 * kc + 128],
                                    K(kc), self.sb["EYEB"][:, :])
                if not first:
                    for kc in range(2):
                        r = p.matmul(psKBS, KB(kc),
                                     St[:, i, :][:, 256 * kc:256 * kc + 256],
                                     start=(kc == 0), stop=(kc == 1))
                return r
            s.on("tensor", t1)
            tv = s.notify("tensor", "dt")
            s.wait("vector", "dt", tv)
            def v1(p, i=i, first=first):
                p.tensor_tensor(W16[:, 0, :], psGA[0:128, 0:128], M_UI, AL.mult)
                p.tensor_tensor(W16[:, 1, :], psGA[0:128, 128:256], M_SU, AL.mult)
                p.tensor_tensor(W16[:, 2, :], psGA[0:128, 256:384], M_SL, AL.mult)
                p.tensor_copy(KTM[:, i, :], psKT)
                if first:
                    return p.tensor_copy(YPP[:, i, 0, :], psT[0:128, 0:256])
                p.tensor_copy(KBSS[:, :], psKBS)
                p.drain()
                return p.tensor_tensor(YPP[:, i, 0, :], psT[0:128, 0:256],
                                       KBSS[:, :], AL.subtract)
            s.on("vector", v1)
            vv = s.notify("vector", "dv")
            s.wait("tensor", "dv", vv)
            xs = {1: 1}
            xts = {1: 2}
            lv = [2, 4, 8, 16, 32, 64]
            for li, lev in enumerate(lv):
                xs[lev] = 3 + li
                if lev != 64:
                    xts[lev] = 9 + li
            for li, lev in enumerate(lv):
                prev = lev // 2
                def t_sq(p, lev=lev, prev=prev):
                    r = p.matmul(psSQ, W16[:, xts[prev], :], W16[:, xs[prev], :],
                                 start=True, stop=True)
                    if lev != 64:
                        r = p.matmul(PSW[:, 128:256], W16[:, xs[prev], :],
                                     W16[:, xts[prev], :], start=True,
                                     stop=True)
                    return r
                s.on("tensor", t_sq)
                tv = s.notify("tensor", "dt")
                s.wait("vector", "dt", tv)
                def v_sq(p, lev=lev):
                    p.tensor_copy(W16[:, xs[lev], :], psSQ)
                    if lev != 64:
                        return p.tensor_copy(W16[:, xts[lev], :], PSW[:, 128:256])
                    return None
                s.on("vector", v_sq)
                vv = s.notify("vector", "dv")
                s.wait("tensor", "dv", vv)
            pp = 0
            for li, lev in enumerate([64, 32, 16, 8, 4, 2]):
                def t_ap(p, lev=lev, pp=pp, i=i):
                    return p.matmul(psAP, W16[:, xs[lev], :], YPP[:, i, pp, :],
                                    start=True, stop=True)
                s.on("tensor", t_ap)
                tv = s.notify("tensor", "dt")
                s.wait("vector", "dt", tv)
                def v_ap(p, pp=pp, i=i):
                    return p.tensor_tensor(YPP[:, i, 1 - pp, :], YPP[:, i, pp, :],
                                           psAP, AL.add)
                s.on("vector", v_ap)
                vv = s.notify("vector", "dv")
                s.wait("tensor", "dv", vv)
                pp = 1 - pp
            def t_lf(p, pp=pp, i=i):
                return p.matmul(psAP, W16[:, xs[1], :], YPP[:, i, pp, :],
                                start=True, stop=True)
            s.on("tensor", t_lf)
            tv = s.notify("tensor", "dt")
            s.wait("vector", "dt", tv)
            def v_lf(p, pp=pp, i=i):
                return p.tensor_tensor(U2T[:, i, :], YPP[:, i, pp, :], psAP,
                                       AL.subtract)
            s.on("vector", v_lf)
            vv = s.notify("vector", "dv")
            s.wait("tensor", "dv", vv)
            def t_o(p, i=i, ck=ck, slot=slot, first=first, Q=Q):
                st = (True, False) if not first else None
                if not first:
                    for kc in range(2):
                        p.matmul(psKBS, Q(kc), St[:, i, :][:, 256 * kc:256 * kc + 256],
                                 start=(kc == 0), stop=False)
                    p.matmul(psKBS, W16[:, 0, :], U2T[:, i, :], start=False,
                             stop=True)
                else:
                    p.matmul(psKBS, W16[:, 0, :], U2T[:, i, :], start=True,
                             stop=True)
                for kc in range(2):
                    r = p.matmul(PSS[:, 512 * i + 256 * kc:512 * i + 256 * kc + 256],
                                 KTM[:, i, :][:, 128 * kc:128 * kc + 128],
                                 U2T[:, i, :],
                                 start=(kc == 0), stop=(kc == 1))
                return r
            s.on("tensor", t_o)
            tv = s.notify("tensor", "dt")
            s.wait("vector", "dt", tv)
            def v_o(p, i=i, first=first):
                p.tensor_copy(OTM[:, i, :], psKBS)
                if first:
                    p.tensor_copy(SF[:, i, :], PSS[:, 512 * i:512 * i + 512])
                else:
                    p.tensor_tensor(SF[:, i, :], SF[:, i, :],
                                    PSS[:, 512 * i:512 * i + 512], AL.add)
                p.drain()
                return p.tensor_copy(St[:, i, :], SF[:, i, :])
            s.on("vector", v_o)
            vv = s.notify("vector", "dv")
            s.wait("tensor", "dv", vv)
            def t_ot(p, i=i):
                r = None
                for kc in range(2):
                    r = p.transpose(psT[:, 128 * kc:128 * kc + 128],
                                    OTM[:, i, :][:, 128 * kc:128 * kc + 128],
                                    self.sb["EYEB"][:, :])
                return r
            s.on("tensor", t_ot)
            tv = s.notify("tensor", "dt")
            s.wait("vector", "dt", tv)
            def v_ot(p, i=i, ck=ck):
                r = None
                for kc in range(2):
                    r = p.tensor_copy(DOUT[:, 2 * i + kc, 128 * ck:128 * ck + 128],
                                      psT[:, 128 * kc:128 * kc + 128])
                return r
            s.on("vector", v_ot)
            vv = s.notify("vector", "dv")
            s.wait("tensor", "dv", vv)
            self.dt_done[(ck, i)] = s.cnt["dt"]
        s.barrier()
        for c in range(4):
            s.dma("gpsimd", self.dint["dret_in"][128 * c:128 * (c + 1), :],
                  DOUT[:, c, :])
        s.dma_wait("gpsimd")
        def cc(p):
            return p.collective_compute(
                "AllGather", AL.bypass,
                replica_groups=[[0, 1], [2, 3], [4, 5], [6, 7]],
                ins=[self.dint["dret_in"].ap().opt()],
                outs=[self.dint["dret_out"].ap().opt()])
        s.on("gpsimd", cc, kind="data")
        self.ccv_dret = s.notify("gpsimd", "ccd")

    def stage_delta_finish(self):
        s = self.s
        E = self.ERA
        dyx = self.dyx
        s.wait("gpsimd", "ccd", self.ccv_dret)
        DOUTM = self.at("DOUTM", [128, NCH, NT], BF16, E + 32768)
        for sl in range(2):
            def g_read(p, sl=sl):
                src = self.dint["dret_out"].ap()[sl, :, bass.ds(dyx["tok"], NT)]
                src = src.rearrange("(c p) t -> p c t", p=128)
                return p.dma_start(out=DOUTM[:, 4 * sl:4 * sl + 4, :],
                                   in_=src).then_inc(s.dsem["gpsimd"], 16)
            s.on("gpsimd", g_read, kind="seq")
            s.dcnt["gpsimd"] += 16
        s.barrier()
        if self.dbg_on:
            self.dbg("dout", DOUTM[:, :, :], [128, NCH, NT], BF16)

    def stage_fir(self):
        s = self.s
        E = self.ERA
        self.dram("fir_rin", [NCH, 2, 128, NT], BF16)
        self.dram("fir_r8", [8, NCH, 2, 128, NT], BF16)
        s.wait("gpsimd", "ccd", self.ccv_firx8)
        VT = self.at("VT", [128, 128, 128], BF16, E)
        VCH = self.at("VCH", [128, NCH, NT], BF16, E + 98688)
        YT = self.at("YT", [128, 128, 128], BF16, E + 98688)
        FIRW = self.at("FIRW", [128, 2, 256], BF16, E + 131456)
        STG = self.at("STG", [128, 16, 128], BF16, E + 132480)
        PST, PSW = self.ps["PST"], self.ps["PSW"]
        def g_vch(p):
            src_ = self.dint["fir_x8"].ap()[:, bass.ds(self.dyx["rank"], 1), :, :]
            src_ = src_.rearrange("s o p t -> p (s o) t")
            return p.dma_start(out=VCH[:, :, :], in_=src_).then_inc(
                s.dsem["gpsimd"], 16)
        s.on("gpsimd", g_vch, kind="seq")
        s.dcnt["gpsimd"] += 16
        s.barrier()
        for grp in range(32):
            def t_tr(p, grp=grp):
                r = None
                for k4 in range(4):
                    gi = 4 * grp + k4
                    b4, m = gi // 32, gi % 32
                    src = VCH[:, 2 * b4 + m // 16,
                              128 * (m % 16):128 * (m % 16) + 128]
                    r = p.transpose(PST[:, 128 * k4:128 * k4 + 128], src,
                                    self.sb["EYEB"][:, :])
                return r
            s.on("tensor", t_tr)
            tv = s.notify("tensor", "fi_t")
            s.wait("vector", "fi_t", tv)
            def v_tr(p, grp=grp):
                r = None
                for k4 in range(4):
                    gi = 4 * grp + k4
                    r = p.tensor_copy(VT[:, :, gi], PST[:, 128 * k4:128 * k4 + 128])
                return r
            s.on("vector", v_tr)
            vv = s.notify("vector", "fi_v")
            s.wait("tensor", "fi_v", vv)
        s.barrier()
        lastv = {}
        for pss in range(2):
            for c in range(128):
                buf = c % 2
                if (pss * 128 + c) >= 1:
                    s.wait("sync", "fi_t", lastv[(pss * 128 + c) - 1])
                s.dma("sync", FIRW[:, buf, :],
                      self.di["firt"].ap()[c, pss, :, :])
                s.dma_wait("tensor", src="sync")
                def t_f(p, c=c, buf=buf):
                    p.matmul(PSW[:, 0:128], FIRW[:, buf, 0:128], VT[:, c, :],
                             start=True, stop=False)
                    r = None
                    for b4 in range(4):
                        r = p.matmul(PSW[:, 32 * b4 + 1:32 * b4 + 32],
                                     FIRW[:, buf, 128:256],
                                     VT[:, c, 32 * b4:32 * b4 + 31],
                                     start=False, stop=(b4 == 3))
                    return r
                s.on("tensor", t_f)
                tv = s.notify("tensor", "fi_t")
                lastv[pss * 128 + c] = tv
                s.wait("vector", "fi_t", tv)
                def v_f(p, c=c):
                    return p.tensor_copy(YT[:, c, :], PSW[:, 0:128])
                s.on("vector", v_f)
                vv = s.notify("vector", "fi_v")
                s.wait("tensor", "fi_v", vv)
            s.barrier()
            for sl in range(NCH):
                for grp in range(4):
                    def t_tb(p, sl=sl, grp=grp):
                        r = None
                        for k4 in range(4):
                            m16 = 4 * grp + k4
                            b4, m = sl // 2, 16 * (sl % 2) + m16
                            gi = 32 * b4 + m
                            r = p.transpose(PST[:, 128 * k4:128 * k4 + 128],
                                            YT[:, :, gi], self.sb["EYEB"][:, :])
                        return r
                    s.on("tensor", t_tb)
                    tv = s.notify("tensor", "fi_t")
                    s.wait("vector", "fi_t", tv)
                    def v_tb(p, grp=grp):
                        r = None
                        for k4 in range(4):
                            m16 = 4 * grp + k4
                            r = p.tensor_copy(STG[:, m16, :],
                                              PST[:, 128 * k4:128 * k4 + 128])
                        return r
                    s.on("vector", v_tb)
                    vv = s.notify("vector", "fi_v")
                    s.wait("tensor", "fi_v", vv)
                s.wait("gpsimd", "fi_v", s.cnt[s.canon("fi_v")])
                s.dma("gpsimd", self.dint["fir_rin"][sl, pss, :, :],
                      STG[:, :, :])
                dvv = s.dcnt["gpsimd"]
                s.dma_wait("vector", upto=dvv, src="gpsimd")
        s.barrier()
        def cc2(p):
            return p.collective_compute(
                "AllGather", AL.bypass,
                replica_groups=[list(range(8))],
                ins=[self.dint["fir_rin"].ap().opt()],
                outs=[self.dint["fir_r8"].ap().opt()])
        s.on("gpsimd", cc2, kind="data")
        self.ccv_firr8 = s.notify("gpsimd", "ccd")

    def stage_fir_finish(self):
        s = self.s
        s.wait("gpsimd", "ccd", self.ccv_firr8)
        self.dram("lsll", [2, NCH, 128, NT], BF16)
        for pss in range(2):
            def g_r(p, pss=pss):
                src_ = self.dint["fir_r8"].ap()[:, bass.ds(self.dyx["rank"], 1),
                                                pss, :, :]
                src_ = src_.rearrange("s o p t -> (s o) p t")
                return p.dma_start(out=self.dint["lsll"].ap()[1 - pss, :, :, :],
                                   in_=src_).then_inc(s.dsem["gpsimd"], 16)
            s.on("gpsimd", g_r, kind="seq")
            s.dcnt["gpsimd"] += 16
        s.barrier()
        if self.dbg_on:
            self.dbg("ls", self.dint["lsll"].ap()[0, :, :, :], [NCH, 128, NT], BF16)
            self.dbg("ll", self.dint["lsll"].ap()[1, :, :, :], [NCH, 128, NT], BF16)

    def stage_gates(self):
        s = self.s
        E = self.ERA
        NP4 = [(0, 512), (512, 1024), (1024, 1536), (1536, 2048)]
        self.NP4 = NP4
        DOUTM = self.sb["DOUTM"]
        PSR = self.ps["PSR"]
        STATR = self.at("STATR", [16, ROWW], F32, E + 131456)
        OGR = self.at("OGR", [8, ROWW], F32, E + 140160)
        VEC = self.at("VEC", [1, 16], F32, E + 148864)
        RT1 = self.at("RT1", [128, 8], F32, E + 148928)
        W9 = self.at("W9", [16, 1024], BF16, E + 149728)
        BG1 = self.at("BG1", [128, NCH], F32, E + 151776)
        WG2 = self.at("WG2", [128, NCH, 8], BF16, E + 151808)
        STATB = self.at("STATB", [16, ROWW], BF16, E + 151936)
        STR = self.at("STR", [128, 2, NCH, 512], BF16, E + 98688)
        self.at("HS1", [128, NCH, 512], BF16, E + 115072)
        self.at("HS2", [128, NCH, 512], BF16, E + 123264)
        self.dram("prow", [8, NT], F32)
        s.dma("sync", VEC[:, :], self.di["vec16"][:, :])
        s.barrier()
        def v_zero(p):
            p.memset(STATR[0:16, 0:ROWW], 0.0)
            return p.memset(OGR[0:8, 0:ROWW], 0.0)
        s.on("vector", v_zero)
        s.barrier()

        self.head_stats(DOUTM, 8, STATR, None, "hsdo")
        self.head_stats(None, 0, STATR, ("lsll", 0), "hsls")
        self.head_stats(None, 4, STATR, ("lsll", 1), "hsll")
        self.head_stats(None, 12, STATR, ("vdir", None), "hsvd")
        if self.dbg_on:
            self.dbg("statr", STATR[0:16, 0:ROWW], [16, ROWW], F32)

        GI = self.at("GI", [128, NCH, NT], BF16, E)
        def v_statb(p):
            return p.tensor_copy(STATB[0:16, 0:ROWW], STATR[0:16, 0:ROWW])
        s.on("vector", v_statb)
        s.barrier()
        WB, WF = self.sb["WB"], self.sb["WF32"]
        for kc in range(NCH):
            s.dma("sync", WF[:, 0:1024], self.di["wg1"][128 * kc:128 * (kc + 1), :])
            s.dma_wait("vector", src="sync")
            def vc(p, kc=kc):
                return p.tensor_copy(WB[:, kc, :], WF[:, 0:1024])
            s.on("vector", vc)
            vv = s.notify("vector", "pw_v")
            s.wait("sync", "pw_v", vv)
        s.dma("sync", WF[0:16, 0:1024], self.di["wg1"][1024:1040, :])
        s.dma_wait("vector", src="sync")
        def vc9(p):
            return p.tensor_copy(W9[0:16, :], WF[0:16, 0:1024])
        s.on("vector", vc9)
        vv9 = s.notify("vector", "pw_v")
        s.wait("tensor", "pw_v")
        s.wait("sync", "pw_v", vv9)
        s.dma("sync", BG1[:, :], self.di["bg1c"][:, :])
        s.dma("sync", WG2[:, :, :], self.di["wg2b"].ap())
        s.barrier()
        PSW = self.ps["PSW"]
        for oc in range(NCH):
            def t_g(p, oc=oc):
                r = None
                for (t0, t1) in NP4:
                    for kc in range(NCH):
                        p.matmul(PSW[:, t0:t1], WB[:, kc, 128 * oc:128 * oc + 128],
                                 self.sb["XN"][:, kc, HALO + t0:HALO + t1],
                                 start=(kc == 0), stop=False)
                    r = p.matmul(PSW[:, t0:t1], W9[0:16, 128 * oc:128 * oc + 128],
                                 STATB[0:16, HALO + t0:HALO + t1],
                                 start=False, stop=True)
                return r
            s.on("tensor", t_g)
            tv = s.notify("tensor", "pw_t")
            s.wait("scalar", "pw_t", tv)
            def s_g(p, oc=oc):
                r = None
                for (t0, t1) in NP4:
                    r = p.activation(GI[:, oc, t0:t1], PSW[:, t0:t1], AF.Gelu,
                                     bias=BG1[:, oc:oc + 1], scale=1.0)
                return r
            s.on("scalar", s_g)
            sv = s.notify("scalar", "pw_s")
            s.wait("tensor", "pw_s", sv)
        s.barrier()
        for (t0, t1) in NP4:
            w = t1 - t0
            def t_l(p, t0=t0, w=w):
                r = None
                for kc in range(NCH):
                    r = p.matmul(PSR[0:8, 0:w], WG2[:, kc, :], GI[:, kc, t0:t0 + w],
                                 start=(kc == 0), stop=(kc == NCH - 1))
                return r
            s.on("tensor", t_l)
            tv = s.notify("tensor", "lg_t")
            s.wait("vector", "lg_t", tv)
            def v_l(p, t0=t0, w=w):
                return p.tensor_copy(self.sb["ROWT"][0:8, HALO + t0:HALO + t0 + w],
                                     PSR[0:8, 0:w])
            s.on("vector", v_l)
            vv = s.notify("vector", "lg_v")
            s.wait("tensor", "lg_v", vv)
        s.barrier()
        def s_T2(p):
            p.activation(RT1[0:1, 1:2], VEC[0:1, 5:6], AF.Exp)
            p.drain()
            return p.activation(RT1[0:1, 2:3], RT1[0:1, 1:2], AF.Ln, bias=1.0,
                                scale=1.0)
        s.on("scalar", s_T2)
        sv = s.notify("scalar", "lg_s")
        s.wait("vector", "lg_s", sv)
        def v_T(p):
            return p.reciprocal(RT1[0:1, 3:4], RT1[0:1, 2:3])
        s.on("vector", v_T)
        s.notify("vector", "lg_v")
        s.barrier()
        self.dram("scal1", [1, 8], F32)
        s.dma("gpsimd", self.dint["scal1"][0:1, 0:1], RT1[0:1, 3:4])
        s.dma_wait("gpsimd")
        s.dma("gpsimd", RT1[:, 0:1],
              self.dint["scal1"].ap()[0:1, 0:1].partition_broadcast(128))
        for h in range(4):
            s.dma("gpsimd", RT1[:, 4 + h:5 + h],
                  self.di["vec16"][0:1, h:h + 1].partition_broadcast(128))
        self.rows_to_rp([0, 1, 2, 3])
        s.barrier()
        RP = self.sb["RP"]
        SL = [RP[:, 17 * i:17 * i + 17] for i in range(4)]
        MXs = RP[:, 68:85]
        def v_sm1(p):
            for h in range(4):
                p.tensor_scalar_add(SL[h], SL[h], RT1[:, 4 + h:5 + h])
            p.drain()
            p.tensor_scalar_mul(RP[:, 0:68], RP[:, 0:68], RT1[:, 0:1])
            p.drain()
            p.tensor_tensor(MXs, SL[0], SL[1], AL.max)
            p.drain()
            p.tensor_tensor(MXs, MXs, SL[2], AL.max)
            p.drain()
            p.tensor_tensor(MXs, MXs, SL[3], AL.max)
            p.drain()
            r = None
            for h in range(4):
                r = p.tensor_tensor(SL[h], SL[h], MXs, AL.subtract)
            return r
        s.on("vector", v_sm1)
        vv = s.notify("vector", "lg_v")
        s.wait("scalar", "lg_v", vv)
        def s_exp(p):
            return p.activation(RP[:, 0:68], RP[:, 0:68], AF.Exp)
        s.on("scalar", s_exp)
        sv = s.notify("scalar", "lg_s")
        s.wait("vector", "lg_s", sv)
        def v_sm2(p):
            r = None
            for pas in range(2):
                p.tensor_tensor(MXs, SL[0], SL[1], AL.add)
                p.drain()
                p.tensor_tensor(MXs, MXs, SL[2], AL.add)
                p.drain()
                p.tensor_tensor(MXs, MXs, SL[3], AL.add)
                p.drain()
                p.reciprocal(MXs, MXs)
                p.drain()
                for h in range(4):
                    r = p.tensor_tensor(SL[h], SL[h], MXs, AL.mult)
                p.drain()
                if pas == 0:
                    p.tensor_scalar_max(RP[:, 0:68], RP[:, 0:68], 0.02)
                    p.drain()
            return r
        s.on("vector", v_sm2)
        s.notify("vector", "lg_v")
        s.barrier()
        self.rp_to_rows([0, 1, 2, 3], [0, 1, 2, 3])
        s.barrier()
        s.dma("gpsimd", self.dint["prow"].ap()[0:4, :],
              self.sb["ROWT"][0:4, HALO:HALO + NT])
        s.barrier()
        for h in range(4):
            s.dma("gpsimd", self.sb["RP"][:, 17 * h:17 * h + 17],
                  self.sb["ROWP"][4 + h:5 + h, 0:ROWW])
            s.dma("gpsimd", RT1[:, 4 + h:5 + h],
                  self.di["vec16"][0:1, 6 + h:7 + h].partition_broadcast(128))
        s.barrier()
        def v_res(p):
            r = None
            for h in range(4):
                r = p.tensor_scalar_add(self.sb["RP"][:, 17 * h:17 * h + 17],
                                        self.sb["RP"][:, 17 * h:17 * h + 17],
                                        RT1[:, 4 + h:5 + h])
            return r
        s.on("vector", v_res)
        vv = s.notify("vector", "rg_v")
        s.wait("scalar", "rg_v", vv)
        def s_res(p):
            return p.activation(self.sb["RP"][:, 0:68], self.sb["RP"][:, 0:68],
                                AF.Sigmoid)
        s.on("scalar", s_res)
        s.notify("scalar", "rg_s")
        s.barrier()
        for h in range(4):
            s.dma("gpsimd", RT1[:, 4 + h:5 + h],
                  self.di["vec16"][0:1, 10 + h:11 + h].partition_broadcast(128))
        s.barrier()
        def s_crl(p):
            return p.activation(RT1[:, 4:8], RT1[:, 4:8], AF.Sigmoid)
        s.on("scalar", s_crl)
        sv = s.notify("scalar", "rg_s")
        s.wait("vector", "rg_s", sv)
        def v_t5(p):
            r = None
            for h in range(4):
                sl_ = self.sb["RP"][:, 17 * h:17 * h + 17]
                r = p.tensor_scalar_mul(sl_, sl_, RT1[:, 4 + h:5 + h])
            return r
        s.on("vector", v_t5)
        s.notify("vector", "rg_v")
        s.barrier()
        for h in range(4):
            s.dma("gpsimd", self.sb["ROWT"][4 + h:5 + h, 0:ROWW],
                  self.sb["RP"][:, 17 * h:17 * h + 17])
        s.barrier()
        if self.dbg_on:
            self.dbg("prow", self.dint["prow"].ap()[0:4, :], [4, NT], F32)
            self.dbg("t5", self.sb["ROWT"][0:8, 0:ROWW], [8, ROWW], F32)

    def head_stats(self, tile, dst_rows, dst_tile, stream, tag):
        s = self.s
        NP4 = self.NP4
        SCB1, SCB2 = self.sb["HS1"], self.sb["HS2"]
        STR = self.sb["STR"]
        PSR = self.ps["PSR"]
        last_t = None
        tvals = []
        for pi, (t0, t1) in enumerate(NP4):
            w = t1 - t0
            if stream is not None:
                nm, idx = stream
                ap = (self.dint[nm].ap()[idx, :, :, t0:t1] if idx is not None
                      else self.dint[nm].ap()[:, :, t0:t1])
                qq = "gpsimd" if pi % 2 == 0 else "sync"
                if pi >= 2:
                    s.wait(qq, tag + "_t", tvals[pi - 2])
                s.dma(qq, STR[:, pi % 2, :, 0:w],
                      ap.rearrange("s p t -> p s t"))
                dv = s.dcnt[qq]
                for e in ("vector", "scalar", "tensor"):
                    s.dma_wait(e, upto=dv, src=qq)
                srcf = lambda c, pi=pi, w=w: STR[:, pi % 2, c, 0:w]
            else:
                srcf = lambda c, t0=t0, w=w: tile[:, c, t0:t0 + w]
            if last_t is not None:
                s.wait("vector", tag + "_t", last_t)
                s.wait("scalar", tag + "_t", last_t)
            def v_sq(p, srcf=srcf, w=w):
                r = None
                for c in range(NCH):
                    r = p.tensor_tensor(SCB1[:, c, 0:w], srcf(c), srcf(c),
                                        AL.mult)
                return r
            s.on("vector", v_sq)
            vv = s.notify("vector", tag + "_v")
            def s_ab(p, srcf=srcf, w=w):
                r = None
                for c in range(NCH):
                    r = p.activation(SCB2[:, c, 0:w], srcf(c), AF.Abs)
                return r
            s.on("scalar", s_ab)
            sv = s.notify("scalar", tag + "_s")
            s.wait("tensor", tag + "_v", vv)
            s.wait("tensor", tag + "_s", sv)
            def t_fn(p, srcf=srcf, w=w):
                r = None
                for c in range(NCH):
                    h = c // 2
                    r = p.matmul(PSR[0:12, 0:w], self.sb["EJ"][:, 16 * h:16 * h + 12],
                                 srcf(c), start=(c == 0), stop=False)
                for c in range(NCH):
                    h = c // 2
                    r = p.matmul(PSR[0:12, 0:w],
                                 self.sb["EJ"][:, 16 * (4 + h):16 * (4 + h) + 12],
                                 SCB1[:, c, 0:w], start=False, stop=False)
                for c in range(NCH):
                    h = c // 2
                    r = p.matmul(PSR[0:12, 0:w],
                                 self.sb["EJ"][:, 16 * (8 + h):16 * (8 + h) + 12],
                                 SCB2[:, c, 0:w], start=False,
                                 stop=(c == NCH - 1))
                return r
            s.on("tensor", t_fn)
            tv = s.notify("tensor", tag + "_t")
            last_t = tv
            tvals.append(tv)
            s.wait("vector", tag + "_t", tv)
            def v_cp(p, t0=t0, w=w):
                return p.tensor_copy(
                    self.sb["ROWT"][0:12, HALO + t0:HALO + t0 + w],
                    PSR[0:12, 0:w])
            s.on("vector", v_cp)
            vv2 = s.notify("vector", tag + "_v")
            s.wait("tensor", tag + "_v", vv2)
        def vz(p):
            p.memset(self.sb["ROWT"][0:12, 0:HALO], 0.0)
            return p.memset(self.sb["ROWT"][0:12, HALO + NT:ROWW], 0.0)
        s.on("vector", vz)
        s.barrier()
        self.rows_to_rp(list(range(12)))
        s.barrier()
        RP = self.sb["RP"]
        def sl(i):
            return RP[:, 17 * i:17 * i + 17]
        def v_m1(p):
            for h in range(4):
                p.tensor_copy(RP[:, 204 + 17 * h:204 + 17 * h + 17], sl(4 + h))
            p.drain()
            for i in range(12):
                p.tensor_scalar_mul(sl(i), sl(i), 1.0 / 256.0)
            p.drain()
            for h in range(4):
                p.tensor_tensor(RP[:, 272 + 17 * h:272 + 17 * h + 17],
                                sl(h), sl(h), AL.mult)
            p.drain()
            r = None
            for h in range(4):
                r = p.tensor_tensor(sl(4 + h), sl(4 + h),
                                    RP[:, 272 + 17 * h:272 + 17 * h + 17],
                                    AL.subtract)
            return r
        s.on("vector", v_m1)
        vv = s.notify("vector", tag + "m_v")
        s.wait("scalar", tag + "m_v", vv)
        def s_m(p):
            return p.activation(RP[:, 204:272], RP[:, 204:272], AF.Sqrt)
        s.on("scalar", s_m)
        sv = s.notify("scalar", tag + "m_s")
        s.wait("vector", tag + "m_s", sv)
        def v_m2(p):
            for kind in range(4):
                if kind < 3:
                    bases = [sl(4 * kind + h) for h in range(4)]
                else:
                    bases = [RP[:, 204 + 17 * h:204 + 17 * h + 17] for h in range(4)]
                d = RP[:, 340 + 17 * kind:340 + 17 * kind + 17]
                p.tensor_tensor(d, bases[0], bases[1], AL.add)
                p.drain()
                p.tensor_tensor(d, d, bases[2], AL.add)
                p.drain()
                p.tensor_tensor(d, d, bases[3], AL.add)
                p.drain()
                r = p.tensor_scalar_mul(d, d, 0.25)
                p.drain()
            return r
        s.on("vector", v_m2)
        s.notify("vector", tag + "m_v")
        s.barrier()
        for kind in range(4):
            s.dma("gpsimd", dst_tile[dst_rows + kind:dst_rows + kind + 1, 0:ROWW],
                  self.sb["RP"][:, 340 + 17 * kind:340 + 17 * kind + 17])
        s.barrier()

    def stage_mix_out(self):
        s = self.s
        E = self.ERA
        NP4 = self.NP4
        DOUTM = self.sb["DOUTM"]
        STR = self.sb["STR"]
        MIXC = self.at("MIXC", [128, 8, NT], BF16, E)
        SCB1 = self.sb["HS1"]
        rb = self.dint.get("rowbounce") or self.dram("rowbounce", [16, ROWW], F32)
        s.dma("gpsimd", rb[4:8, 0:ROWW], self.sb["ROWT"][4:8, 0:ROWW])
        s.dma_wait("gpsimd")
        s.dma_wait("sync", src="gpsimd")
        vcs = []
        for k in range(8):
            qq = "gpsimd" if k % 2 == 0 else "sync"
            dst = self.sb["BCA"] if k % 2 == 0 else self.sb["BCB"]
            if k >= 2:
                s.wait(qq, "mx_v", vcs[k - 2])
            if k < 4:
                srcap = self.dint["prow"].ap()[k:k + 1, 0:NT]
            else:
                srcap = rb[k:k + 1, HALO:HALO + NT]
            s.dma(qq, dst[:, HALO:HALO + NT], srcap.partition_broadcast(128))
            dv = s.dcnt[qq]
            s.dma_wait("vector", upto=dv, src=qq)
            def vc(p, k=k, dst=dst):
                return p.tensor_copy(MIXC[:, k, :], dst[:, HALO:HALO + NT])
            s.on("vector", vc)
            vcs.append(s.notify("vector", "mx_v"))
        s.barrier()
        for pi, (t0, t1) in enumerate(NP4):
            w = t1 - t0
            s.dma("gpsimd", STR[:, 0, :, 0:w],
                  self.dint["lsll"].ap()[0, :, :, t0:t1].rearrange("s p t -> p s t"))
            s.dma("gpsimd", STR[:, 1, :, 0:w],
                  self.dint["lsll"].ap()[1, :, :, t0:t1].rearrange("s p t -> p s t"))
            dv = s.dcnt["gpsimd"]
            s.dma_wait("vector", upto=dv, src="gpsimd")
            def v_mix1(p, t0=t0, w=w):
                r = None
                for c in range(NCH):
                    p.tensor_tensor(DOUTM[:, c, t0:t0 + w], DOUTM[:, c, t0:t0 + w],
                                    MIXC[:, 2, t0:t0 + w], AL.mult)
                for c in range(NCH):
                    p.tensor_tensor(SCB1[:, c, 0:w], STR[:, 0, c, 0:w],
                                    MIXC[:, 0, t0:t0 + w], AL.mult)
                p.drain()
                p.drain()
                for c in range(NCH):
                    p.tensor_tensor(DOUTM[:, c, t0:t0 + w], DOUTM[:, c, t0:t0 + w],
                                    SCB1[:, c, 0:w], AL.add)
                p.drain()
                for c in range(NCH):
                    p.tensor_tensor(SCB1[:, c, 0:w], STR[:, 0, c, 0:w],
                                    MIXC[:, 4 + c // 2, t0:t0 + w], AL.mult)
                p.drain()
                for c in range(NCH):
                    p.tensor_tensor(DOUTM[:, c, t0:t0 + w], DOUTM[:, c, t0:t0 + w],
                                    SCB1[:, c, 0:w], AL.add)
                p.drain()
                for c in range(NCH):
                    p.tensor_tensor(SCB1[:, c, 0:w], STR[:, 1, c, 0:w],
                                    MIXC[:, 1, t0:t0 + w], AL.mult)
                p.drain()
                for c in range(NCH):
                    r = p.tensor_tensor(DOUTM[:, c, t0:t0 + w],
                                        DOUTM[:, c, t0:t0 + w],
                                        SCB1[:, c, 0:w], AL.add)
                return r
            s.on("vector", v_mix1)
            vv = s.notify("vector", "mx_v")
            s.wait("gpsimd", "mx_v", vv)
            s.dma("gpsimd", STR[:, 0, :, 0:w],
                  self.dint["vdir"].ap()[:, :, t0:t1].rearrange("s p t -> p s t"))
            dv = s.dcnt["gpsimd"]
            s.dma_wait("vector", upto=dv, src="gpsimd")
            def v_mix2(p, t0=t0, w=w):
                r = None
                for c in range(NCH):
                    p.tensor_tensor(SCB1[:, c, 0:w], STR[:, 0, c, 0:w],
                                    MIXC[:, 3, t0:t0 + w], AL.mult)
                p.drain()
                for c in range(NCH):
                    r = p.tensor_tensor(DOUTM[:, c, t0:t0 + w],
                                        DOUTM[:, c, t0:t0 + w],
                                        SCB1[:, c, 0:w], AL.add)
                return r
            s.on("vector", v_mix2)
            vv = s.notify("vector", "mx_v")
            s.wait("gpsimd", "mx_v", vv)
        s.barrier()
        if self.dbg_on:
            self.dbg("omix", DOUTM[:, :, :], [128, NCH, NT], BF16)
        self.head_stats(DOUTM, 0, self.sb["OGR"], None, "hso")
        for kk in range(4):
            s.dma("gpsimd", self.sb["OGR"][4 + kk:5 + kk, 0:ROWW],
                  self.sb["STATR"][12 + kk:13 + kk, 0:ROWW])
        s.barrier()
        OG1 = self.at("OG1", [128, 4, NT], BF16, E + 98688)
        OGSB = self.at("OGSB", [8, ROWW], BF16, E + 156288)
        BOG1 = self.at("BOG1", [128, 4], F32, E + 119488)
        WOG2 = self.at("WOG2", [128, 4, 8], BF16, E + 119552)
        WB, WF = self.sb["WB"], self.sb["WF32"]
        WB2 = self.at("WB2", [128, NCH, 512], BF16, E + 119680)
        W9b = self.at("W9b", [8, 512], BF16, E + 127872)
        def v_ogsb(p):
            return p.tensor_copy(OGSB[0:8, 0:ROWW], self.sb["OGR"][0:8, 0:ROWW])
        s.on("vector", v_ogsb)
        for kc in range(NCH):
            s.dma("sync", WF[:, 0:512], self.di["wog1"][128 * kc:128 * (kc + 1), :])
            s.dma_wait("vector", src="sync")
            def vc(p, kc=kc):
                return p.tensor_copy(WB2[:, kc, :], WF[:, 0:512])
            s.on("vector", vc)
            vv = s.notify("vector", "pw_v")
            s.wait("sync", "pw_v", vv)
        s.dma("sync", WF[0:8, 0:512], self.di["wog1"][1024:1032, :])
        s.dma_wait("vector", src="sync")
        def vc9(p):
            return p.tensor_copy(W9b[0:8, :], WF[0:8, 0:512])
        s.on("vector", vc9)
        vv9 = s.notify("vector", "pw_v")
        s.wait("tensor", "pw_v")
        s.wait("sync", "pw_v", vv9)
        s.dma("sync", BOG1[:, :], self.di["bog1c"][:, :])
        s.dma("sync", WOG2[:, :, :], self.di["wog2b"].ap())
        s.barrier()
        PSW, PSR = self.ps["PSW"], self.ps["PSR"]
        for oc in range(4):
            def t_og(p, oc=oc):
                r = None
                for (t0, t1) in NP4:
                    for kc in range(NCH):
                        p.matmul(PSW[:, t0:t1], WB2[:, kc, 128 * oc:128 * oc + 128],
                                 self.sb["XN"][:, kc, HALO + t0:HALO + t1],
                                 start=(kc == 0), stop=False)
                    r = p.matmul(PSW[:, t0:t1], W9b[0:8, 128 * oc:128 * oc + 128],
                                 OGSB[0:8, HALO + t0:HALO + t1],
                                 start=False, stop=True)
                return r
            s.on("tensor", t_og)
            tv = s.notify("tensor", "pw_t")
            s.wait("scalar", "pw_t", tv)
            def s_og(p, oc=oc):
                r = None
                for (t0, t1) in NP4:
                    r = p.activation(OG1[:, oc, t0:t1], PSW[:, t0:t1], AF.Gelu,
                                     bias=BOG1[:, oc:oc + 1], scale=1.0)
                return r
            s.on("scalar", s_og)
            sv = s.notify("scalar", "pw_s")
            s.wait("tensor", "pw_s", sv)
        s.barrier()
        for (t0, t1) in NP4:
            w = t1 - t0
            def t_o2(p, t0=t0, w=w):
                r = None
                for kc in range(4):
                    r = p.matmul(PSR[0:8, 0:w], WOG2[:, kc, :],
                                 OG1[:, kc, t0:t0 + w],
                                 start=(kc == 0), stop=(kc == 3))
                return r
            s.on("tensor", t_o2)
            tv = s.notify("tensor", "og_t")
            s.wait("scalar", "og_t", tv)
            def s_o2(p, t0=t0, w=w):
                return p.activation(self.sb["ROWT"][0:1, HALO + t0:HALO + t0 + w],
                                    PSR[0:1, 0:w], AF.Sigmoid,
                                    bias=self.sb["VEC"][0:1, 4:5], scale=1.0)
            s.on("scalar", s_o2)
            sv = s.notify("scalar", "og_s")
            s.wait("tensor", "og_s", sv)
        s.barrier()
        self.bcast_row(0, self.sb["BCA"])
        s.barrier()
        def v_og(p):
            r = None
            for c in range(NCH):
                r = p.tensor_tensor(DOUTM[:, c, :], DOUTM[:, c, :],
                                    self.sb["BCA"][:, HALO:HALO + NT], AL.mult)
            return r
        s.on("vector", v_og)
        s.barrier()
        SCB1 = self.sb["HS1"]
        last_t = None
        for (t0, t1) in NP4:
            w = t1 - t0
            if last_t is not None:
                s.wait("vector", "rn_t", last_t)
            def v_sq(p, t0=t0, w=w):
                r = None
                for c in range(NCH):
                    r = p.tensor_tensor(SCB1[:, c, 0:w], DOUTM[:, c, t0:t0 + w],
                                        DOUTM[:, c, t0:t0 + w], AL.mult)
                return r
            s.on("vector", v_sq)
            vv = s.notify("vector", "rn_v")
            s.wait("tensor", "rn_v", vv)
            def t_rn(p, t0=t0, w=w):
                r = None
                for c in range(NCH):
                    h = c // 2
                    r = p.matmul(PSR[0:4, 0:w], self.sb["EJ"][:, 16 * h:16 * h + 4],
                                 SCB1[:, c, 0:w], start=(c == 0),
                                 stop=(c == NCH - 1))
                return r
            s.on("tensor", t_rn)
            tv = s.notify("tensor", "rn_t")
            last_t = tv
            s.wait("vector", "rn_t", tv)
            def v_cp(p, t0=t0, w=w):
                return p.tensor_copy(self.sb["ROWT"][0:4, HALO + t0:HALO + t0 + w],
                                     PSR[0:4, 0:w])
            s.on("vector", v_cp)
            vv2 = s.notify("vector", "rn_v")
            s.wait("tensor", "rn_v", vv2)
        def vz(p):
            p.memset(self.sb["ROWT"][0:4, 0:HALO], 0.0)
            return p.memset(self.sb["ROWT"][0:4, HALO + NT:ROWW], 0.0)
        s.on("vector", vz)
        s.barrier()
        self.rows_to_rp([0, 1, 2, 3])
        s.barrier()
        RP = self.sb["RP"]
        def v_rn(p):
            p.tensor_scalar_mul(RP[:, 0:68], RP[:, 0:68], 1.0 / 256.0)
            p.drain()
            p.tensor_scalar_add(RP[:, 0:68], RP[:, 0:68], RMS_EPS)
            p.drain()
            return p.reciprocal(RP[:, 0:68], RP[:, 0:68])
        s.on("vector", v_rn)
        vv = s.notify("vector", "rn_v")
        s.wait("scalar", "rn_v", vv)
        def s_rn(p):
            return p.activation(RP[:, 0:68], RP[:, 0:68], AF.Sqrt)
        s.on("scalar", s_rn)
        s.notify("scalar", "rn_s")
        s.barrier()
        self.rp_to_rows([0, 1, 2, 3], [0, 1, 2, 3])
        s.barrier()
        RSB = self.at("RSB", [128, 4, NT], BF16, E + 128896)
        rb2 = self.dint["rowbounce"]
        s.dma("gpsimd", rb2[0:4, 0:ROWW], self.sb["ROWT"][0:4, 0:ROWW])
        s.dma_wait("gpsimd")
        s.dma_wait("sync", src="gpsimd")
        vcs2 = []
        for h in range(4):
            qq = "gpsimd" if h % 2 == 0 else "sync"
            dst = self.sb["BCA"] if h % 2 == 0 else self.sb["BCB"]
            if h >= 2:
                s.wait(qq, "rn_v", vcs2[h - 2])
            s.dma(qq, dst[:, HALO:HALO + NT],
                  rb2[h:h + 1, HALO:HALO + NT].partition_broadcast(128))
            dv = s.dcnt[qq]
            s.dma_wait("vector", upto=dv, src=qq)
            def vc(p, h=h, dst=dst):
                return p.tensor_copy(RSB[:, h, :], dst[:, HALO:HALO + NT])
            s.on("vector", vc)
            vcs2.append(s.notify("vector", "rn_v"))
        s.barrier()
        NRMW = self.at("NRMW", [128, NCH], F32, E + 145280)
        s.dma("sync", NRMW[:, :], self.di["normwc"][:, :])
        s.barrier()
        def v_rnap(p):
            r = None
            for c in range(NCH):
                r = p.scalar_tensor_tensor(DOUTM[:, c, :], DOUTM[:, c, :],
                                           NRMW[:, c:c + 1], RSB[:, c // 2, :],
                                           AL.mult, AL.mult)
            return r
        s.on("vector", v_rnap)
        s.barrier()
        if self.dbg_on:
            self.dbg("onrm", DOUTM[:, :, :], [128, NCH, NT], BF16)
        X1a = self.at("X1a", [128, 4, NT], F32, E)
        X1b = self.at("X1b", [128, 4, NT], F32, E + 65792)
        def x1t(oc):
            return (X1a[:, oc, :] if oc < 4 else X1b[:, oc - 4, :])
        WB, WF = self.sb["WB"], self.sb["WF32"]
        for kc in range(NCH):
            s.dma("sync", WF[:, 0:1024], self.di["wo"][128 * kc:128 * (kc + 1), :])
            s.dma_wait("vector", src="sync")
            def vc(p, kc=kc):
                return p.tensor_copy(WB[:, kc, :], WF[:, 0:1024])
            s.on("vector", vc)
            vv = s.notify("vector", "pw_v")
            s.wait("sync", "pw_v", vv)
        s.wait("tensor", "pw_v")
        for oc in range(NCH):
            s.dma("sync", x1t(oc), self.di["x0"].ap().rearrange(
                "(c p) t -> p c t", p=128)[:, oc, HALO:HALO + NT])
        s.barrier()
        wcnt = {}
        for oc in range(NCH):
            for pidx, (t0, t1) in enumerate(NP4):
                if oc >= 1:
                    s.wait("tensor", "mx_v", wcnt[(oc - 1, pidx)])
                def t_wo(p, oc=oc, t0=t0, t1=t1):
                    r = None
                    for kc in range(NCH):
                        r = p.matmul(PSW[:, t0:t1],
                                     WB[:, kc, 128 * oc:128 * oc + 128],
                                     DOUTM[:, kc, t0:t1], start=(kc == 0),
                                     stop=(kc == NCH - 1))
                    return r
                s.on("tensor", t_wo)
                tv = s.notify("tensor", "pw_t")
                s.wait("vector", "pw_t", tv)
                def v_wo(p, oc=oc, t0=t0, t1=t1):
                    return p.tensor_tensor(x1t(oc)[:, t0:t1], x1t(oc)[:, t0:t1],
                                           PSW[:, t0:t1], AL.add)
                s.on("vector", v_wo)
                wcnt[(oc, pidx)] = s.notify("vector", "mx_v")
        s.barrier()
        if self.dbg_on:
            self.dbg("x1a", X1a[:, :, :], [128, 4, NT], F32)
            self.dbg("x1b", X1b[:, :, :], [128, 4, NT], F32)

    def bcast_row_dram(self, name, row, dst, width=NT):
        s = self.s
        def dst_ap():
            d = dst() if callable(dst) else dst
            return d[:, HALO:HALO + width]
        s.dma("gpsimd", dst_ap,
              lambda: self.dint[name].ap()[row:row + 1, 0:width]
              .partition_broadcast(128))

    def stage_ffn1(self):
        self.ffn_apply("f1")
        if self.dbg_on:
            X1a, X1b = self.sb["X1a"], self.sb["X1b"]
            self.dbg("x2a", X1a[:, :, :], [128, 4, NT], F32)
            self.dbg("x2b", X1b[:, :, :], [128, 4, NT], F32)

    def ffn_ternarize(self):
        s = self.s
        E = self.ERA
        WE = self.at("WE", [128, NCH, 1024], BF16, E + 98688)
        WF = self.sb["WF32"]
        TN1 = self.at("TN1", [128, 1024], F32, E + 119680)
        TN2 = self.at("TN2", [128, 1024], F32, E + 123776)
        TN3 = self.at("TN3", [128, 1024], F32, E + 127872)
        TB1 = self.at("TB1", [128, 1024], BF16, E + 131968)
        TB2 = self.at("TB2", [128, 1024], BF16, E + 134016)
        PRS = self.sb_off["PRSV"][0]
        SCL2 = self.at("SCL2", [128, 8], F32, E + 152448)
        self.sb["SCL2z"] = SCL2
        GAM = self.at("GAM", [128, NCH], F32, PRS)
        BET = self.at("BET", [128, NCH], F32, PRS + 32)
        AL1 = self.at("AL1", [1, 8], F32, PRS + 64)
        self.at("ROWB", [1, 128], F32, E + 152512)
        self.at("MK8", [128, 4], F32, E + 152480)
        s.dma("sync", GAM[:, :], self.di["gamc"][:, :])
        s.dma("sync", BET[:, :], self.di["betc"][:, :])
        s.barrier()
        PSW = self.ps["PSW"]
        for kc in range(NCH):
            s.dma("sync", WF[:, 0:1024],
                  self.di["ffnw"][128 * kc:128 * (kc + 1), :])
            s.dma_wait("scalar", src="sync")
            def s_a(p):
                return p.activation(TB1[:, :], WF[:, 0:1024], AF.Abs)
            s.on("scalar", s_a)
            sv = s.notify("scalar", "tz_s")
            s.wait("tensor", "tz_s", sv)
            def t_a(p, kc=kc):
                r = None
                for sg in range(2):
                    r = p.matmul(PSW[0:1, 512 * sg:512 * sg + 512],
                                 self.sb["EJ"][:, 0:1],
                                 TB1[:, 512 * sg:512 * sg + 512],
                                 start=(kc == 0), stop=(kc == NCH - 1))
                return r
            s.on("tensor", t_a)
            tv = s.notify("tensor", "tz_t")
            s.wait("sync", "tz_t", tv)
        s.barrier()
        def v_d(p):
            p.tensor_reduce(SCL2[0:1, 4:5], PSW[0:1, 0:1024],
                            mybir.AxisListType.X, AL.add)
            p.drain()
            return p.tensor_scalar_mul(SCL2[0:1, 4:5], SCL2[0:1, 4:5],
                                       0.7 / (1024.0 * 1024.0))
        s.on("vector", v_d)
        s.barrier()
        self.dram("scal2", [1, 8], F32)
        s.dma("gpsimd", self.dint["scal2"][0:1, 0:1], SCL2[0:1, 4:5])
        s.dma_wait("gpsimd")
        s.dma("gpsimd", SCL2[:, 5:6],
              self.dint["scal2"].ap()[0:1, 0:1].partition_broadcast(128))
        s.barrier()
        for kc in range(NCH):
            s.dma("sync", WF[:, 0:1024],
                  self.di["ffnw"][128 * kc:128 * (kc + 1), :])
            s.dma_wait("scalar", src="sync")
            def s_b(p):
                p.activation(TN1[:, :], WF[:, 0:1024], AF.Abs)
                return p.activation(TN2[:, :], WF[:, 0:1024], AF.Sign)
            s.on("scalar", s_b)
            sv = s.notify("scalar", "tz_s")
            s.wait("vector", "tz_s", sv)
            s.wait("sync", "tz_s", sv)
            def v_b(p):
                return p.tensor_scalar_sub(TN3[:, :], TN1[:, :], SCL2[:, 5:6])
            s.on("vector", v_b)
            vv = s.notify("vector", "tz_v")
            s.wait("scalar", "tz_v", vv)
            def s_c(p):
                return p.activation(TN3[:, :], TN3[:, :], AF.Sign)
            s.on("scalar", s_c)
            sv = s.notify("scalar", "tz_s")
            s.wait("vector", "tz_s", sv)
            def v_c(p, kc=kc):
                p.tensor_scalar_max(TN3[:, :], TN3[:, :], 0.0)
                p.drain()
                p.tensor_copy(TB1[:, :], TN3[:, :])
                p.drain()
                p.tensor_tensor(WE[:, kc, :], TN2[:, :], TN3[:, :], AL.mult)
                p.drain()
                return p.tensor_tensor(TB2[:, :], TN1[:, :], TN3[:, :],
                                       AL.mult)
            s.on("vector", v_c)
            vv = s.notify("vector", "tz_v")
            s.wait("tensor", "tz_v", vv)
            def t_b(p, kc=kc):
                r = None
                for sg in range(2):
                    sl_ = slice(512 * sg, 512 * sg + 512)
                    p.matmul(PSW[0:2, 1024 + 512 * sg:1536 + 512 * sg],
                             self.sb["EJ"][:, 0:2], TB2[:, sl_],
                             start=(kc == 0), stop=False)
                    r = p.matmul(PSW[0:2, 1024 + 512 * sg:1536 + 512 * sg],
                                 self.sb["EJ"][:, 16:18], TB1[:, sl_],
                                 start=False, stop=(kc == NCH - 1))
                return r
            s.on("tensor", t_b)
            tv = s.notify("tensor", "tz_t")
            s.wait("sync", "tz_t", tv)
        s.barrier()
        def v_al(p):
            p.tensor_reduce(SCL2[0:2, 6:7], PSW[0:2, 1024:2048],
                            mybir.AxisListType.X, AL.add)
            return p.drain()
        s.on("vector", v_al)
        s.barrier()
        s.dma("gpsimd", AL1[0:1, 2:4], SCL2[0:2, 6:7])
        s.barrier()
        def v_al2(p):
            p.tensor_scalar_max(AL1[0:1, 3:4], AL1[0:1, 3:4], 1.0)
            p.drain()
            p.reciprocal(AL1[0:1, 3:4], AL1[0:1, 3:4])
            p.drain()
            return p.tensor_tensor(AL1[0:1, 0:1], AL1[0:1, 2:3],
                                   AL1[0:1, 3:4], AL.mult)
        s.on("vector", v_al2)
        self.dram("wedr", [128, NCH, 1024], BF16)
        s.dma("gpsimd", self.dint["wedr"].ap(), WE[:, :, :])
        s.barrier()

    def ffn_apply(self, tag):
        s = self.s
        E = self.ERA
        NP4 = self.NP4
        WE = self.at("WEL" + tag, [128, NCH, 1024], BF16, E + 98688)
        s.dma("sync", WE[:, :, :], self.dint["wedr"].ap())
        SCL2 = self.at("SCL2" + tag, [128, 8], F32, E + 152448)
        GAM, BET, AL1 = self.sb["GAM"], self.sb["BET"], self.sb["AL1"]
        MK8, ROWB = self.sb["MK8"], self.sb["ROWB"]
        HS1, HS2 = self.sb["HS1"], self.sb["HS2"]
        INT = self.at("INT" + tag, [128, NCH, NT], BF16, E + 32768)
        S2 = self.at("S2" + tag, [128, NCH, 512], F32, E + 136064)
        X1a, X1b = self.sb["X1a"], self.sb["X1b"]
        def x1t(c):
            return (X1a[:, c, :] if c < 4 else X1b[:, c - 4, :])
        PSR, PSW = self.ps["PSR"], self.ps["PSW"]
        last_t = None
        for (t0, t1) in NP4:
            w = t1 - t0
            if last_t is not None:
                s.wait("vector", tag + "r_t", last_t)
                s.wait("scalar", tag + "r_t", last_t)
            def v_cast(p, t0=t0, w=w):
                r = None
                for c in range(NCH):
                    r = p.tensor_copy(HS1[:, c, 0:w], x1t(c)[:, t0:t0 + w])
                return r
            s.on("vector", v_cast)
            vv = s.notify("vector", tag + "r_v")
            def s_sq(p, t0=t0, w=w):
                r = None
                for c in range(NCH):
                    r = p.activation(HS2[:, c, 0:w], x1t(c)[:, t0:t0 + w],
                                     AF.Square)
                return r
            s.on("scalar", s_sq)
            sv = s.notify("scalar", tag + "r_s")
            s.wait("tensor", tag + "r_v", vv)
            s.wait("tensor", tag + "r_s", sv)
            def t_r(p, w=w):
                r = None
                for c in range(NCH):
                    p.matmul(PSR[0:2, 0:w], self.sb["EJ"][:, 0:2],
                             HS1[:, c, 0:w], start=(c == 0), stop=False)
                for c in range(NCH):
                    r = p.matmul(PSR[0:2, 0:w], self.sb["EJ"][:, 16:18],
                                 HS2[:, c, 0:w], start=False,
                                 stop=(c == NCH - 1))
                return r
            s.on("tensor", t_r)
            tv = s.notify("tensor", tag + "r_t")
            last_t = tv
            s.wait("vector", tag + "r_t", tv)
            def v_cp(p, t0=t0, w=w):
                return p.tensor_copy(self.sb["ROWT"][0:2, HALO + t0:HALO + t0 + w],
                                     PSR[0:2, 0:w])
            s.on("vector", v_cp)
            vv2 = s.notify("vector", tag + "r_v")
            s.wait("tensor", tag + "r_v", vv2)
        def vz(p):
            p.memset(self.sb["ROWT"][0:2, 0:HALO], 0.0)
            return p.memset(self.sb["ROWT"][0:2, HALO + NT:ROWW], 0.0)
        s.on("vector", vz)
        s.barrier()
        self.rows_to_rp([0, 1])
        s.barrier()
        RP = self.sb["RP"]
        mu, msq, var = RP[:, 0:17], RP[:, 17:34], RP[:, 34:51]
        t_, varb = RP[:, 51:68], RP[:, 68:85]
        def v_math(p):
            p.tensor_scalar_mul(mu, mu, 1.0 / D)
            p.tensor_scalar_mul(msq, msq, 1.0 / D)
            p.drain()
            p.tensor_tensor(var, mu, mu, AL.mult)
            p.drain()
            p.tensor_tensor(var, msq, var, AL.subtract)
            p.drain()
            p.tensor_scalar_add(t_, var, LN_EPS)
            p.drain()
            p.reciprocal(varb, t_)
            p.drain()
            p.tensor_tensor(varb, var, varb, AL.mult)
            p.drain()
            p.tensor_scalar_add(varb, varb, RMS_EPS)
            p.drain()
            p.tensor_tensor(t_, t_, varb, AL.mult)
            p.drain()
            return p.reciprocal(t_, t_)
        s.on("vector", v_math)
        vv = s.notify("vector", tag + "m_v")
        s.wait("scalar", tag + "m_v", vv)
        def s_math(p):
            return p.activation(t_, t_, AF.Sqrt)
        s.on("scalar", s_math)
        s.notify("scalar", tag + "m_s")
        s.barrier()
        self.rp_to_rows([0, 3], [0, 1])
        s.barrier()
        self.bcast_row(0, self.sb["BCA"])
        self.bcast_row(1, self.sb["BCB"])
        s.barrier()
        SC = self.sb["SC"]
        for pi, (t0, t1) in enumerate(NP4):
            w = t1 - t0
            def v_a(p, t0=t0, w=w):
                r = None
                for c in range(NCH):
                    p.tensor_tensor(SC[:, 0:w], x1t(c)[:, t0:t0 + w],
                                    self.sb["BCA"][:, HALO + t0:HALO + t0 + w],
                                    AL.subtract)
                    p.drain()
                    r = p.tensor_tensor(S2[:, c, 0:w], SC[:, 0:w],
                                        self.sb["BCB"][:, HALO + t0:HALO + t0 + w],
                                        AL.mult)
                    p.drain()
                return r
            s.on("vector", v_a)
            vv = s.notify("vector", tag + "a_v")
            s.wait("scalar", tag + "a_v", vv)
            def s_a(p, w=w):
                r = None
                for c in range(NCH):
                    r = p.activation(S2[:, c, 0:w], S2[:, c, 0:w], AF.Identity,
                                     bias=BET[:, c:c + 1], scale=GAM[:, c:c + 1])
                return r
            s.on("scalar", s_a)
            sv = s.notify("scalar", tag + "a_s")
            s.wait("vector", tag + "a_s", sv)
            def v_mx(p, pi=pi, w=w):
                return p.tensor_reduce(MK8[:, pi:pi + 1], S2[:, :, 0:w],
                                       mybir.AxisListType.XY, AL.max,
                                       apply_absolute_value=True)
            s.on("vector", v_mx)
            vv = s.notify("vector", tag + "a_v")
            s.wait("scalar", tag + "a_v", vv)
        s.barrier()
        def v_g(p):
            return p.tensor_reduce(SCL2[:, 2:3], MK8[:, 0:4],
                                   mybir.AxisListType.X, AL.max)
        s.on("vector", v_g)
        s.barrier()
        s.dma("gpsimd", ROWB[0:1, 0:128], SCL2[:, 2:3])
        s.barrier()
        def v_g2(p):
            return p.tensor_reduce(SCL2[0:1, 2:3], ROWB[0:1, 0:128],
                                   mybir.AxisListType.X, AL.max)
        s.on("vector", v_g2)
        s.barrier()
        qin = self.dram("qin" + tag, [1, 1], F32)
        qout = self.dram("qout" + tag, [1, 1], F32)
        s.dma("gpsimd", qin[0:1, 0:1], SCL2[0:1, 2:3])
        s.dma_wait("gpsimd")
        def g_ar(p):
            return p.collective_compute(
                "AllReduce", AL.max,
                replica_groups=[[0, 1, 2, 3, 4, 5, 6, 7]],
                ins=[self.dint["qin" + tag].ap().opt()],
                outs=[self.dint["qout" + tag].ap().opt()])
        s.on("gpsimd", g_ar, kind="data")
        s.notify("gpsimd", "ccd")
        s.wait("gpsimd", "ccd")
        s.dma("gpsimd", SCL2[0:1, 2:3], self.dint["qout" + tag].ap()[0:1, 0:1])
        s.barrier()
        def v_q(p):
            p.tensor_scalar_max(SCL2[0:1, 2:3], SCL2[0:1, 2:3], 1e-8)
            p.drain()
            p.tensor_scalar_mul(SCL2[0:1, 3:4], SCL2[0:1, 2:3], 1.0 / 127.0)
            p.drain()
            p.reciprocal(SCL2[0:1, 2:3], SCL2[0:1, 2:3])
            p.drain()
            p.tensor_scalar_mul(SCL2[0:1, 2:3], SCL2[0:1, 2:3], 127.0)
            p.drain()
            return p.tensor_tensor(SCL2[0:1, 3:4], SCL2[0:1, 3:4],
                                   AL1[0:1, 0:1], AL.mult)
        s.on("vector", v_q)
        s.barrier()
        s.dma("gpsimd", self.dint["scal2"][0:1, 2:4], SCL2[0:1, 2:4])
        s.dma_wait("gpsimd")
        s.dma("gpsimd", SCL2[:, 0:1],
              self.dint["scal2"].ap()[0:1, 2:3].partition_broadcast(128))
        s.dma("gpsimd", SCL2[:, 1:2],
              self.dint["scal2"].ap()[0:1, 3:4].partition_broadcast(128))
        s.barrier()
        def v_mg(p):
            return p.memset(SCL2[:, 7:8], MAGIC)
        s.on("vector", v_mg)
        s.barrier()
        for pi, (t0, t1) in enumerate(NP4):
            w = t1 - t0
            def v_a(p, t0=t0, w=w):
                r = None
                for c in range(NCH):
                    p.tensor_tensor(SC[:, 0:w], x1t(c)[:, t0:t0 + w],
                                    self.sb["BCA"][:, HALO + t0:HALO + t0 + w],
                                    AL.subtract)
                    p.drain()
                    r = p.tensor_tensor(S2[:, c, 0:w], SC[:, 0:w],
                                        self.sb["BCB"][:, HALO + t0:HALO + t0 + w],
                                        AL.mult)
                    p.drain()
                return r
            s.on("vector", v_a)
            vv = s.notify("vector", tag + "b_v")
            s.wait("scalar", tag + "b_v", vv)
            def s_a(p, w=w):
                r = None
                for c in range(NCH):
                    p.activation(S2[:, c, 0:w], S2[:, c, 0:w], AF.Identity,
                                 bias=BET[:, c:c + 1], scale=GAM[:, c:c + 1])
                    self.snop(p)
                    r = p.activation(S2[:, c, 0:w], S2[:, c, 0:w], AF.Identity,
                                     bias=SCL2[:, 7:8], scale=SCL2[:, 0:1])
                    self.snop(p)
                return r
            s.on("scalar", s_a)
            sv = s.notify("scalar", tag + "b_s")
            s.wait("vector", tag + "b_s", sv)
            def v_i(p, t0=t0, w=w):
                r = None
                for c in range(NCH):
                    r = p.tensor_scalar_sub(INT[:, c, t0:t0 + w],
                                            S2[:, c, 0:w], MAGIC)
                return r
            s.on("vector", v_i)
            vv = s.notify("vector", tag + "b_v")
            s.wait("scalar", tag + "b_v", vv)
        s.barrier()
        if self.dbg_on:
            self.dbg("ints" + tag, INT[:, :, :], [128, NCH, NT], BF16)
        vcnt = {}
        for oc in range(NCH):
            for pidx, (t0, t1) in enumerate(NP4):
                if oc >= 1:
                    s.wait("tensor", tag + "o_v", vcnt[(oc - 1, pidx)])
                def t_f(p, oc=oc, t0=t0, t1=t1):
                    r = None
                    for kc in range(NCH):
                        r = p.matmul(PSW[:, t0:t1],
                                     WE[:, kc, 128 * oc:128 * oc + 128],
                                     INT[:, kc, t0:t1], start=(kc == 0),
                                     stop=(kc == NCH - 1))
                    return r
                s.on("tensor", t_f)
                tv = s.notify("tensor", tag + "o_t")
                s.wait("vector", tag + "o_t", tv)
                def v_f(p, oc=oc, t0=t0, t1=t1):
                    return p.scalar_tensor_tensor(
                        x1t(oc)[:, t0:t1], PSW[:, t0:t1], SCL2[:, 1:2],
                        x1t(oc)[:, t0:t1], AL.mult, AL.add)
                s.on("vector", v_f)
                vcnt[(oc, pidx)] = s.notify("vector", tag + "o_v")
        s.barrier()

    def stage_attn_b(self):
        s = self.s
        E = self.ERA
        NP4 = self.NP4
        X1a, X1b = self.sb["X1a"], self.sb["X1b"]
        def x1t(c):
            return (X1a[:, c, :] if c < 4 else X1b[:, c - 4, :])
        PSR, PSW = self.ps["PSR"], self.ps["PSW"]
        HS1, HS2 = self.sb["HS1"], self.sb["HS2"]
        XN3 = self.at("XN3", [128, NCH, NT + 4], BF16, E + 32768)
        PB = self.at("PB", [64, NT + 4], F32, E + 119680)
        DD = self.at("DD", [64, NT], F32, E + 127904)
        FE = self.at("FE", [16, NT], BF16, E + 136096)
        Z1 = self.at("Z1", [64, NT + 8], F32, E + 140192)
        Zb = self.at("Zb", [64, NT + 8], F32, E + 148416)
        BW = self.at("BW", [128, NCH, 4, 64], BF16, E + 156640)
        CW3 = self.at("CW3", [128, NCH, 4], F32, E + 160736)
        DTWB = self.at("DTWB", [128, NCH, 16], BF16, E + 160864)
        TRWB = self.at("TRWB", [16, 64], BF16, E + 161120)
        DTB = self.at("DTB", [16, 1], F32, E + 161248)
        INB = self.at("INB", [64, 1], F32, E + 161280)
        OUTB = self.at("OUTB", [128, NCH], F32, E + 161312)
        HM = self.at("HM", [128, 1], F32, E + 161344)
        ZAG = self.at("ZAG", [64, 1], F32, E + 161376)
        WF = self.sb["WF32"]
        last_t = None
        for (t0, t1) in NP4:
            w = t1 - t0
            if last_t is not None:
                s.wait("vector", "b3r_t", last_t)
                s.wait("scalar", "b3r_t", last_t)
            def v_cast(p, t0=t0, w=w):
                r = None
                for c in range(NCH):
                    r = p.tensor_copy(HS1[:, c, 0:w], x1t(c)[:, t0:t0 + w])
                return r
            s.on("vector", v_cast)
            vv = s.notify("vector", "b3r_v")
            def s_sq(p, t0=t0, w=w):
                r = None
                for c in range(NCH):
                    r = p.activation(HS2[:, c, 0:w], x1t(c)[:, t0:t0 + w],
                                     AF.Square)
                return r
            s.on("scalar", s_sq)
            sv = s.notify("scalar", "b3r_s")
            s.wait("tensor", "b3r_v", vv)
            s.wait("tensor", "b3r_s", sv)
            def t_r(p, w=w):
                r = None
                for c in range(NCH):
                    p.matmul(PSR[0:2, 0:w], self.sb["EJ"][:, 0:2],
                             HS1[:, c, 0:w], start=(c == 0), stop=False)
                for c in range(NCH):
                    r = p.matmul(PSR[0:2, 0:w], self.sb["EJ"][:, 16:18],
                                 HS2[:, c, 0:w], start=False,
                                 stop=(c == NCH - 1))
                return r
            s.on("tensor", t_r)
            tv = s.notify("tensor", "b3r_t")
            last_t = tv
            s.wait("vector", "b3r_t", tv)
            def v_cp(p, t0=t0, w=w):
                return p.tensor_copy(self.sb["ROWT"][0:2, HALO + t0:HALO + t0 + w],
                                     PSR[0:2, 0:w])
            s.on("vector", v_cp)
            vv2 = s.notify("vector", "b3r_v")
            s.wait("tensor", "b3r_v", vv2)
        def vz(p):
            p.memset(self.sb["ROWT"][0:2, 0:HALO], 0.0)
            return p.memset(self.sb["ROWT"][0:2, HALO + NT:ROWW], 0.0)
        s.on("vector", vz)
        s.barrier()
        self.rows_to_rp([0, 1])
        s.barrier()
        RP = self.sb["RP"]
        mu, msq, var = RP[:, 0:17], RP[:, 17:34], RP[:, 34:51]
        def v_math(p):
            p.tensor_scalar_mul(mu, mu, 1.0 / D)
            p.tensor_scalar_mul(msq, msq, 1.0 / D)
            p.drain()
            p.tensor_tensor(var, mu, mu, AL.mult)
            p.drain()
            p.tensor_tensor(var, msq, var, AL.subtract)
            p.drain()
            p.tensor_scalar_add(var, var, LN_EPS)
            p.drain()
            return p.reciprocal(var, var)
        s.on("vector", v_math)
        vv = s.notify("vector", "b3m_v")
        s.wait("scalar", "b3m_v", vv)
        def s_math(p):
            return p.activation(var, var, AF.Sqrt)
        s.on("scalar", s_math)
        s.notify("scalar", "b3m_s")
        s.barrier()
        self.rp_to_rows([0, 2], [0, 1])
        s.barrier()
        self.bcast_row(0, self.sb["BCA"])
        self.bcast_row(1, self.sb["BCB"])
        s.barrier()
        SC = self.sb["SC"]
        for (t0, t1) in NP4:
            w = t1 - t0
            def v_x(p, t0=t0, w=w):
                r = None
                for c in range(NCH):
                    p.tensor_tensor(SC[:, 0:w], x1t(c)[:, t0:t0 + w],
                                    self.sb["BCA"][:, HALO + t0:HALO + t0 + w],
                                    AL.subtract)
                    p.drain()
                    r = p.tensor_tensor(XN3[:, c, 4 + t0:4 + t0 + w], SC[:, 0:w],
                                        self.sb["BCB"][:, HALO + t0:HALO + t0 + w],
                                        AL.mult)
                    p.drain()
                return r
            s.on("vector", v_x)
            s.notify("vector", "b3x_v")
        s.barrier()
        self.dram("bhx", [NCH, 128, 4], BF16)
        self.dram("bhy", [2, NCH, 128, 4], BF16)
        s.dma("gpsimd", self.dint["bhx"].ap().rearrange("s p t -> p s t"),
              XN3[:, :, NT:NT + 4])
        s.dma("sync", HM[:, 0:1], self.di["hmask"].ap()[0:1, 0:1]
              .partition_broadcast(128))
        s.barrier()
        def cc_h(p):
            return p.collective_compute(
                "AllGather", AL.bypass,
                replica_groups=[[0, 1], [2, 3], [4, 5], [6, 7]],
                ins=[self.dint["bhx"].ap().opt()],
                outs=[self.dint["bhy"].ap().opt()])
        s.on("gpsimd", cc_h, kind="data")
        s.notify("gpsimd", "ccd")
        s.wait("gpsimd", "ccd")
        s.dma("gpsimd", XN3[:, :, 0:4],
              self.dint["bhy"].ap()[0, :, :, :].rearrange("s p t -> p s t"))
        s.barrier()
        def v_hm(p):
            return p.tensor_scalar_mul(XN3[:, :, 0:4], XN3[:, :, 0:4],
                                       HM[:, 0:1])
        s.on("vector", v_hm)
        s.barrier()
        if self.dbg_on:
            self.dbg("xn3", XN3[:, :, :], [128, NCH, NT + 4], BF16)
        s.dma("sync", CW3[:, :, :], self.di["bconvw"].ap())
        s.dma("sync", DTB[:, :], self.di["dtb"][:, :])
        s.dma("sync", INB[:, :], self.di["inbc"][:, :])
        s.dma("sync", OUTB[:, :], self.di["outbc"][:, :])
        s.dma("sync", WF[0:16, 0:64], self.di["trwt"][:, :])
        s.dma_wait("vector", src="sync")
        def v_trw(p):
            return p.tensor_copy(TRWB[0:16, :], WF[0:16, 0:64])
        s.on("vector", v_trw)
        vv = s.notify("vector", "bw_v")
        s.wait("sync", "bw_v", vv)
        s.dma("sync", WF[:, 0:128].rearrange("p (c r) -> p c r", r=16),
              self.di["dtwt"].ap().rearrange("(c p) r -> p c r", p=128))
        s.dma_wait("vector", src="sync")
        def v_dtw(p):
            return p.tensor_copy(DTWB[:, :, :],
                                 WF[:, 0:128].rearrange("p (c r) -> p c r", r=16))
        s.on("vector", v_dtw)
        vv = s.notify("vector", "bw_v")
        s.wait("sync", "bw_v", vv)
        for kc in range(NCH):
            s.dma("sync", WF[:, 0:64], self.di["inwt"][128 * kc:128 * (kc + 1), :])
            s.dma_wait("vector", src="sync")
            def v_bw(p, kc=kc):
                for j in range(3):
                    p.tensor_scalar_mul(BW[:, kc, j, :], WF[:, 0:64],
                                        CW3[:, kc, j:j + 1])
                p.tensor_scalar_add(SC[:, 0:1], CW3[:, kc, 3:4], 1.0)
                p.drain()
                return p.tensor_scalar_mul(BW[:, kc, 3, :], WF[:, 0:64],
                                           SC[:, 0:1])
            s.on("vector", v_bw)
            vv = s.notify("vector", "bw_v")
            s.wait("sync", "bw_v", vv)
        s.wait("tensor", "bw_v")
        s.barrier()
        for (t0, t1) in NP4:
            w = t1 - t0
            def t_fe(p, t0=t0, w=w):
                r = None
                for kc in range(NCH):
                    r = p.matmul(PSR[0:16, 0:w], DTWB[:, kc, :],
                                 XN3[:, kc, 4 + t0:4 + t0 + w],
                                 start=(kc == 0), stop=(kc == NCH - 1))
                return r
            s.on("tensor", t_fe)
            tv = s.notify("tensor", "bd_t")
            s.wait("scalar", "bd_t", tv)
            def s_fe(p, t0=t0, w=w):
                return p.activation(FE[0:16, t0:t0 + w], PSR[0:16, 0:w],
                                    AF.Tanh, bias=DTB[0:16, 0:1], scale=1.0)
            s.on("scalar", s_fe)
            sv = s.notify("scalar", "bd_s")
            s.wait("tensor", "bd_s", sv)
        s.barrier()
        for (t0, t1) in NP4:
            w = t1 - t0
            def t_d(p, t0=t0, w=w):
                return p.matmul(PSR[0:64, 0:w], TRWB[0:16, :],
                                FE[0:16, t0:t0 + w], start=True, stop=True)
            s.on("tensor", t_d)
            tv = s.notify("tensor", "bd_t")
            s.wait("vector", "bd_t", tv)
            def v_d(p, t0=t0, w=w):
                return p.tensor_copy(DD[0:64, t0:t0 + w], PSR[0:64, 0:w])
            s.on("vector", v_d)
            vv = s.notify("vector", "bd_v")
            s.wait("tensor", "bd_v", vv)
        s.barrier()
        PWP = [(0, 1)] + [(1 + 512 * i, 1 + 512 * (i + 1)) for i in range(4)]
        for (u0, u1) in PWP:
            w = u1 - u0
            def t_p(p, u0=u0, w=w):
                r = None
                first = True
                for kc in range(NCH):
                    for j in range(4):
                        r = p.matmul(PSW[0:64, 0:w], BW[:, kc, j, :],
                                     XN3[:, kc, j + u0:j + u0 + w],
                                     start=first, stop=(kc == NCH - 1 and j == 3))
                        first = False
                return r
            s.on("tensor", t_p)
            tv = s.notify("tensor", "bp_t")
            s.wait("scalar", "bp_t", tv)
            def s_p(p, u0=u0, w=w):
                return p.activation(PB[0:64, u0:u0 + w], PSW[0:64, 0:w],
                                    AF.Identity, bias=INB[0:64, 0:1], scale=1.0)
            s.on("scalar", s_p)
            sv = s.notify("scalar", "bp_s")
            s.wait("tensor", "bp_s", sv)
        s.barrier()
        def v_e(p):
            p.tensor_scalar_mul(PB[0:64, 0:1], PB[0:64, 0:1], HM[0:64, 0:1])
            p.drain()
            return p.tensor_tensor(DD[0:64, 0:NT], DD[0:64, 0:NT],
                                   PB[0:64, 0:NT], AL.add)
        s.on("vector", v_e)
        s.barrier()
        if self.dbg_on:
            self.dbg("bpb", PB[0:64, 0:NT + 1], [64, NT + 1], F32)
            self.dbg("be", DD[0:64, 0:NT], [64, NT], F32)
        def v_z0(p):
            return p.memset(Z1[0:64, 0:1], 0.0)
        s.on("vector", v_z0)
        vv = s.notify("vector", "bs_v")
        s.wait("scalar", "bs_v", vv)
        def s_scan1(p):
            r = None
            for t in range(NT):
                r = p.activation(Z1[0:64, t + 1:t + 2], Z1[0:64, t:t + 1],
                                 AF.Silu, bias=DD[0:64, t:t + 1], scale=1.0)
                self.snop(p)
            return r
        s.on("scalar", s_scan1)
        s.notify("scalar", "bs_s")
        s.barrier()
        self.dram("zin", [1, 64], F32)
        self.dram("zout", [2, 64], F32)
        s.dma("gpsimd", self.dint["zin"][0:1, :], Z1[0:64, NT:NT + 1])
        s.barrier()
        def cc_z(p):
            return p.collective_compute(
                "AllGather", AL.bypass,
                replica_groups=[[0, 1], [2, 3], [4, 5], [6, 7]],
                ins=[self.dint["zin"].ap().opt()],
                outs=[self.dint["zout"].ap().opt()])
        s.on("gpsimd", cc_z, kind="data")
        s.notify("gpsimd", "ccd")
        s.wait("gpsimd", "ccd")
        s.dma("gpsimd", ZAG[0:64, 0:1], self.dint["zout"].ap()[0:1, :])
        s.barrier()
        def v_seed(p):
            p.tensor_scalar_mul(ZAG[0:64, 0:1], ZAG[0:64, 0:1], HM[0:64, 0:1])
            p.drain()
            return p.tensor_copy(Zb[0:64, 0:1], ZAG[0:64, 0:1])
        s.on("vector", v_seed)
        vv = s.notify("vector", "bs_v")
        s.wait("scalar", "bs_v", vv)
        def s_scan2(p):
            r = None
            for t in range(NT):
                r = p.activation(Zb[0:64, t + 1:t + 2], Zb[0:64, t:t + 1],
                                 AF.Silu, bias=DD[0:64, t:t + 1], scale=1.0)
                self.snop(p)
            return r
        s.on("scalar", s_scan2)
        s.notify("scalar", "bs_s")
        s.barrier()
        OWB = self.at("OWB", [64, 1024], BF16, E + 32768)
        SB3 = self.at("SB3", [64, NT], BF16, E + 34816)
        s.dma("sync", WF[0:64, 0:1024], self.di["outwt"][:, :])
        s.dma_wait("vector", src="sync")
        def v_ow(p):
            p.tensor_copy(OWB[0:64, :], WF[0:64, 0:1024])
            p.drain()
            return p.tensor_tensor(SB3[0:64, 0:NT], Zb[0:64, 1:NT + 1],
                                   PB[0:64, 1:NT + 1], AL.add)
        s.on("vector", v_ow)
        s.barrier()
        if self.dbg_on:
            self.dbg("bst", SB3[0:64, 0:NT], [64, NT], BF16)
        for oc in range(NCH):
            def t_o(p, oc=oc):
                r = None
                for (t0, t1) in NP4:
                    r = p.matmul(PSW[:, t0:t1], OWB[0:64, 128 * oc:128 * oc + 128],
                                 SB3[0:64, t0:t1], start=True, stop=True)
                return r
            s.on("tensor", t_o)
            tv = s.notify("tensor", "bo_t")
            s.wait("vector", "bo_t", tv)
            def v_o(p, oc=oc):
                r = None
                for (t0, t1) in NP4:
                    r = p.scalar_tensor_tensor(x1t(oc)[:, t0:t1], PSW[:, t0:t1],
                                               OUTB[:, oc:oc + 1],
                                               x1t(oc)[:, t0:t1],
                                               AL.add, AL.add)
                return r
            s.on("vector", v_o)
            vv = s.notify("vector", "bo_v")
            s.wait("tensor", "bo_v", vv)
        s.barrier()
        if self.dbg_on:
            self.dbg("x3a", X1a[:, :, :], [128, 4, NT], F32)
            self.dbg("x3b", X1b[:, :, :], [128, 4, NT], F32)

    def stage_ffn2(self):
        self.ffn_apply("f2")


    def stage_store(self):
        s = self.s
        X1a, X1b = self.sb["X1a"], self.sb["X1b"]
        for c in range(NCH):
            t = X1a[:, c, :] if c < 4 else X1b[:, c - 4, :]
            s.dma("gpsimd", self.do["y"].ap().rearrange(
                "(c p) t -> p c t", p=128)[:, c, :], t)
        s.barrier()


def _stage_inputs(inputs):
    x = np.asarray(inputs["L0_attn_a_hidden_states"])
    import ml_dtypes
    bf = ml_dtypes.bfloat16
    eye = np.eye(128, dtype=np.float32)
    ej = np.zeros((128, 256), np.float32)
    for j in range(16):
        ej[:, 16 * j + j] = 1.0
    tri = np.triu(np.ones((128, 128), np.float32), 1)
    triincl = np.triu(np.ones((128, 128), np.float32), 0)
    masks = np.concatenate([tri, triincl, np.eye(128, dtype=np.float32),
                            np.tril(np.ones((128, 128), np.float32), -1)], 1)

    maps = []
    for c in range(8):
        b, h = c // 2, c % 2
        t0 = NT * h
        xf = np.zeros((D, NTH), np.float32)
        xf[:, HALO:] = x[b, t0:t0 + NT, :].T
        if h == 1:
            xf[:, 0:HALO] = x[b, t0 - HALO:t0, :].T
        wsm = np.zeros((D, 16), np.float32)
        wsm[:, 0:4] = np.asarray(inputs["L0_attn_a_Wb"])
        wsm[:, 4:8] = np.asarray(inputs["L0_attn_a_Wres"])
        cvw = np.zeros((D, 16), np.float32)
        cvw[:, 0:4] = np.asarray(inputs["L0_attn_a_wq_conv"])
        cvw[:, 4:8] = np.asarray(inputs["L0_attn_a_wk_conv"])
        cvw[:, 8:12] = np.asarray(inputs["L0_attn_a_wv_conv"])
        wl = np.asarray(inputs["L0_attn_a_fir_long"]).reshape(D, 64)
        ws = np.asarray(inputs["L0_attn_a_fir_short"]).reshape(D, 5)
        ii, oo = np.indices((128, 128))
        def toep(w, K, shift):
            j = ii - oo + (K - 1) - shift
            ok = (j >= 0) & (j < K)
            out = np.zeros((w.shape[0], 128, 128), np.float32)
            jj = np.clip(j, 0, K - 1)
            out = np.where(ok[None, :, :], w[:, jj], 0.0)
            return out
        mych = slice(128 * c, 128 * (c + 1))
        lng = np.concatenate([toep(wl[mych], 64, 0),
                              toep(wl[mych], 64, 128)], axis=2)
        sht = np.concatenate([toep(ws[mych], 5, 0),
                              toep(ws[mych], 5, 128)], axis=2)
        firt = np.stack([lng, sht], axis=1)
        wg2 = np.zeros((D, 8), np.float32)
        wg2[:, 0:4] = np.asarray(inputs["L0_attn_a_Wg2"])
        wog2 = np.zeros((512, 8), np.float32)
        wog2[:, 0:1] = np.asarray(inputs["L0_attn_a_Wog2"])
        vec16 = np.zeros((1, 16), np.float32)
        vec16[0, 0:4] = np.asarray(inputs["L0_attn_a_bg2"])
        vec16[0, 4] = np.asarray(inputs["L0_attn_a_bog2"])[0]
        vec16[0, 5] = np.asarray(inputs["L0_attn_a_logit_temp"])[0]
        vec16[0, 6:10] = np.asarray(inputs["L0_attn_a_bres"])
        vec16[0, 10:14] = np.asarray(inputs["L0_attn_a_conv_res_logit"])
        nw = np.asarray(inputs["L0_attn_a_norm_w"])
        nwc = np.zeros((128, NCH), np.float32)
        for cc in range(NCH):
            nwc[:, cc] = nw[(cc % 2) * 128:(cc % 2) * 128 + 128]
        m = {
            "x0": np.ascontiguousarray(xf),
            "firt": np.ascontiguousarray(firt).astype(bf),
            "wq": np.ascontiguousarray(np.asarray(inputs["L0_attn_a_Wq"])),
            "wk": np.ascontiguousarray(np.asarray(inputs["L0_attn_a_Wk"])),
            "wv": np.ascontiguousarray(np.asarray(inputs["L0_attn_a_Wv"])),
            "wo": np.ascontiguousarray(np.asarray(inputs["L0_attn_a_Wo"])),
            "ffnw": np.ascontiguousarray(np.asarray(inputs["L0_ffn_weight"]).T),
            "gamc": np.ascontiguousarray(
                np.asarray(inputs["L0_ffn_gamma"]).reshape(NCH, 128).T),
            "betc": np.ascontiguousarray(
                np.asarray(inputs["L0_ffn_beta"]).reshape(NCH, 128).T),
            "inwt": np.ascontiguousarray(np.asarray(inputs["L0_attn_b_in_w"]).T),
            "bconvw": np.ascontiguousarray(
                np.asarray(inputs["L0_attn_b_conv_w"]).reshape(NCH, 128, 4)
                .transpose(1, 0, 2)),
            "dtwt": np.ascontiguousarray(np.asarray(inputs["L0_attn_b_dt_w"]).T),
            "dtb": np.ascontiguousarray(
                np.asarray(inputs["L0_attn_b_dt_b"]).reshape(16, 1)),
            "trwt": np.ascontiguousarray(np.asarray(inputs["L0_attn_b_tr_w"]).T),
            "outwt": np.ascontiguousarray(np.asarray(inputs["L0_attn_b_out_w"]).T),
            "outbc": np.ascontiguousarray(
                np.asarray(inputs["L0_attn_b_out_b"]).reshape(NCH, 128).T),
            "inbc": np.ascontiguousarray(
                np.asarray(inputs["L0_attn_b_in_b"]).reshape(64, 1)),
            "hmask": np.array([[float(h)]], np.float32),
            "wg1": np.ascontiguousarray(np.asarray(inputs["L0_attn_a_Wg1"])),
            "wog1": np.ascontiguousarray(np.asarray(inputs["L0_attn_a_Wog1"])),
            "wg2b": np.ascontiguousarray(
                wg2.reshape(NCH, 128, 8).transpose(1, 0, 2)).astype(bf),
            "wog2b": np.ascontiguousarray(
                wog2.reshape(4, 128, 8).transpose(1, 0, 2)).astype(bf),
            "bg1c": np.ascontiguousarray(
                np.asarray(inputs["L0_attn_a_bg1"]).reshape(NCH, 128).T),
            "bog1c": np.ascontiguousarray(
                np.asarray(inputs["L0_attn_a_bog1"]).reshape(4, 128).T),
            "normwc": nwc,
            "vec16": vec16,
            "bg2row": np.tile(np.asarray(inputs["L0_attn_a_bg2"]),
                              16).reshape(1, 64).astype(np.float32),
            "wsmallb": np.ascontiguousarray(
                wsm.reshape(NCH, 128, 16).transpose(1, 0, 2)).astype(bf),
            "convw": np.ascontiguousarray(
                cvw.reshape(NCH, 128, 16).transpose(1, 0, 2)),
            "eye128": eye,
            "ejpat": ej.astype(bf),
            "masks": masks.astype(bf),
        }
        maps.append(m)
    return maps


def kernel(**inputs):
    mk = MK(stage=_STAGE, dbg=_DBG).build()
    maps = _stage_inputs(inputs)
    res = run_bass_kernel_spmd(mk.nc, maps, core_ids=list(range(8)))
    if _STAGE < 11:
        return res
    outs = [r["y"] for r in res.results]
    y = np.empty((B, L, D), np.float32)
    for c in range(8):
        b, h = c // 2, c % 2
        y[b, NT * h:NT * (h + 1), :] = outs[c].T
    return y
